# version 23
# baseline (speedup 1.0000x reference)
import numpy as np
import ml_dtypes
from contextlib import ExitStack
import concourse.bass as bass
import concourse.mybir as mybir
from concourse.bass_utils import run_bass_kernel_spmd

import numpy as np
import concourse.bass as bass
import concourse.mybir as mybir

F32 = mybir.dt.float32
BF16 = mybir.dt.bfloat16
F16 = mybir.dt.float16
I32 = mybir.dt.int32
U8 = mybir.dt.uint8
AF = mybir.ActivationFunctionType
ALU = mybir.AluOpType
AX = mybir.AxisListType
DS = bass.ds
TS = bass.ts
IOffs = bass.IndirectOffsetOnAxis

ENGINES = ("tensor", "vector", "scalar", "gpsimd", "sync")


class Prog:
    def __init__(self, nc):
        self.nc = nc
        self.ops = []
        self.last_write = {}
        self.readers = {}
        self.final_ops = []

    def op(self, engine, fn, reads=(), writes=(), kind="compute", tag=""):
        deps = set()
        for r in reads:
            w = self.last_write.get(r)
            if w is not None:
                deps.add(w)
        for w in writes:
            lw = self.last_write.get(w)
            if lw is not None:
                deps.add(lw)
            for rd in self.readers.get(w, ()):
                deps.add(rd)
        opid = len(self.ops)
        deps.discard(opid)
        self.ops.append(dict(engine=engine, fn=fn, deps=deps, kind=kind, tag=tag))
        for r in reads:
            self.readers.setdefault(r, []).append(opid)
        for w in writes:
            self.last_write[w] = opid
            self.readers[w] = []
        return opid

    def dma(self, queue, out, in_, reads=(), writes=(), tag="", final=False, **kw):
        assert queue in ("sync", "scalar", "gpsimd")
        i = self.op(queue, lambda e: e.dma_start(out=out, in_=in_, **kw),
                    reads=reads, writes=writes, kind="dma", tag=tag)
        if final:
            self.final_ops.append(i)
        return i

    def finalize(self, trace=None, paranoid=False, paranoid_tags=()):
        nc = self.nc
        ops = self.ops
        if paranoid:
            for i, o in enumerate(ops):
                o["deps"] = set(range(i))
        elif paranoid_tags:
            for i, o in enumerate(ops):
                if o["tag"] in paranoid_tags:
                    o["deps"] = set(range(i))

        def semkey(o):
            return (o["engine"],
                    "dma" if o["kind"] == "dma" else
                    "coll" if o["kind"] == "coll" else "c")

        need_inc = [False] * len(ops)
        for o in ops:
            for d in o["deps"]:
                if ops[d]["engine"] == "tensor" and o["engine"] == "tensor":
                    continue
                need_inc[d] = True
        for i in self.final_ops:
            need_inc[i] = True
        for i, o in enumerate(ops):
            if o["kind"] in ("dma", "coll"):
                need_inc[i] = True

        semval = [None] * len(ops)
        cum = {}
        for i, o in enumerate(ops):
            if need_inc[i]:
                k = semkey(o)
                cum[k] = cum.get(k, 0) + (16 if o["kind"] == "dma" else 1)
                semval[i] = cum[k]

        sem_keys = sorted(set(semkey(ops[i]) for i in range(len(ops)) if need_inc[i]))
        sem_cms = [nc.semaphore(f"sem_{k[0]}_{k[1]}") for k in sem_keys]
        sems = {k: cm.__enter__() for k, cm in zip(sem_keys, sem_cms)}

        eng_ids = {e: [i for i, o in enumerate(ops) if o["engine"] == e]
                   for e in ENGINES}
        final_by_eng = {}
        for i in self.final_ops:
            k = semkey(ops[i])
            final_by_eng[k] = max(final_by_eng.get(k, 0), semval[i])

        def make_runner(e):
            def runner(eng_obj):
                frontier = {}
                for i in eng_ids[e]:
                    o = ops[i]
                    waits = {}
                    for d in sorted(o["deps"]):
                        p = ops[d]
                        if p["engine"] == "tensor" and e == "tensor":
                            continue
                        k = semkey(p)
                        v = semval[d]
                        assert v is not None, (o["tag"], p["tag"])
                        if waits.get(k, 0) < v:
                            waits[k] = v
                    for k, v in sorted(waits.items()):
                        if frontier.get(k, 0) >= v:
                            continue
                        eng_obj.wait_ge(sems[k], v)
                        frontier[k] = v
                        if trace is not None:
                            trace.append((e, "WAIT", k, v, o["tag"]))
                    try:
                        inst = o["fn"](eng_obj)
                    except Exception:
                        import traceback
                        traceback.print_exc()
                        raise
                    if need_inc[i]:
                        k = semkey(o)
                        inst.then_inc(sems[k], 16 if o["kind"] == "dma" else 1)
                        if trace is not None:
                            trace.append((e, "OP+INC", k, semval[i], o["tag"]))
                    elif trace is not None:
                        trace.append((e, "OP", None, None, o["tag"]))
                for k, v in sorted(final_by_eng.items()):
                    if frontier.get(k, 0) < v:
                        eng_obj.wait_ge(sems[k], v)
            return runner

        block_cm = nc.Block()
        block = block_cm.__enter__()
        for e in ENGINES:
            if eng_ids[e] or any(k[0] == e for k in final_by_eng):
                getattr(block, e)(make_runner(e))
        block_cm.__exit__(None, None, None)
        for cm in sem_cms:
            cm.__exit__(None, None, None)
        return dict(n_ops=len(ops),
                    n_incs=sum(need_inc),
                    per_engine={e: len(v) for e, v in eng_ids.items()})


"""Layer emitters for the mk_prb_d4 megakernel. Feature-major convention:
activations live in SBUF as [128 part, K*T free] where column block k holds
features [128k,128k+128) for all T tokens: sb[p, k*T+t] = act[t, 128k+p].
"""
import numpy as np

T = 2048
KD = 8
NCH = 4


class Ctx:
    def __init__(self, nc, prog, es):
        self.nc = nc
        self.p = prog
        self.es = es
        self.sb = {}
        self.ps = {}
        self.consts = {}
        self._cf32 = []
        self._ci32 = []
        self.dram = {}

    def sbuf(self, name, shape, dtype):
        h = self.es.enter_context(self.nc.sbuf_tensor(name, list(shape), dtype))
        self.sb[name] = h
        return h

    def psum(self, name, shape=(128, 512), dtype=F32):
        h = self.es.enter_context(self.nc.psum_tensor(name, list(shape), dtype))
        self.ps[name] = h
        return h

    def dram_t(self, name, shape, dtype, kind="Internal", **kw):
        h = self.nc.dram_tensor(name, list(shape), dtype, kind=kind, **kw).ap()
        self.dram[name] = h
        return h

    def const_f32(self, name, arr):
        arr = np.asarray(arr, np.float32)
        assert arr.shape[0] == 128
        self.consts[("f32", name)] = arr
        return name

    def const_i32(self, name, arr):
        arr = np.asarray(arr, np.int32)
        assert arr.shape[0] == 128
        self.consts[("i32", name)] = arr
        return name

    def const_bf16(self, name, arr):
        import ml_dtypes
        arr = np.asarray(arr, ml_dtypes.bfloat16)
        assert arr.shape[0] == 128
        self.consts[("bf16", name)] = arr
        return name

    def pack_consts(self):
        packs = {}
        for tag, dt in (("f32", F32), ("i32", I32), ("bf16", BF16)):
            items = [(k[1], v) for k, v in self.consts.items() if k[0] == tag]
            if not items:
                continue
            tot = sum(v.shape[1] for _, v in items)
            dname = "CONST_" + tag
            d = self.dram_t(dname, (128, tot), dt, kind="ExternalInput")
            s = self.sbuf(dname + "_sb", (128, tot), dt)
            self.p.dma("sync", s[:, :], d[:, :], writes=["consts"], tag="consts")
            off = 0
            self.caps = getattr(self, "caps", {})
            for name, v in items:
                self.caps[name] = s[:, off:off + v.shape[1]]
                off += v.shape[1]
            packs[dname] = np.concatenate([v for _, v in items], axis=1)
        self.const_inputs = packs
        return packs

    def cap(self, name):
        return self.caps[name]


def register_common_consts(ctx, skip=()):
    i = np.arange(128)
    reg = {
        "ident": lambda: ctx.const_f32("ident", np.eye(128)),
        "ones_col_bf": lambda: ctx.const_bf16("ones_col_bf", np.ones((128, 2))),
        "ident_bf": lambda: ctx.const_bf16("ident_bf", np.eye(128)),
        "ones_col": lambda: ctx.const_f32("ones_col", np.ones((128, 1))),
        "ones_row": lambda: ctx.const_f32("ones_row", np.ones((128, 128))),
        "mask_sl_neg": lambda: ctx.const_f32(
            "mask_sl_neg", np.where(i[:, None] > i[None, :], -1e30, 0.0)),
        "mask_triu_incl": lambda: ctx.const_f32(
            "mask_triu_incl", (i[:, None] <= i[None, :]).astype(np.float32)),
        "mask_triu_strict": lambda: ctx.const_f32(
            "mask_triu_strict", (i[:, None] < i[None, :]).astype(np.float32)),
        "mask_tril_strict": lambda: ctx.const_f32(
            "mask_tril_strict", (i[:, None] > i[None, :]).astype(np.float32)),
        "iota_row": lambda: ctx.const_f32(
            "iota_row", np.broadcast_to(i[None, :], (128, 128)).copy()),
        "eps_col": lambda: ctx.const_f32("eps_col", np.full((128, 1), 1e-5)),
        "zero_col": lambda: ctx.const_f32("zero_col", np.zeros((128, 1))),
    }
    for k, fn in reg.items():
        if k not in skip:
            fn()


def emit_ln(ctx, x_name, xn_name, sq_name="ln_sq", row_name="ln_rows",
            banks=("ps0", "ps1", "ps2", "ps3"), stats_name=None):
    p = ctx.p
    x = ctx.sb[x_name]
    xn = ctx.sb[xn_name]
    sq = ctx.sb[sq_name]
    rows = ctx.sb[row_name]
    ones_col = ctx.cap("ones_col")[:, 0:1]
    ones_row = ctx.cap("ones_row")[0:1, :]
    eps_col = ctx.cap("eps_col")
    TC = T // NCH
    pm, pq, pmb, prb = (ctx.ps[b] for b in banks)
    kpm, kpq, kmb, krb = banks
    for tc in range(NCH):
        for k in range(KD):
            xs = x[:, k * T + tc * TC: k * T + (tc + 1) * TC]
            p.op("tensor", lambda e, xs=xs, k=k: e.matmul(
                pm[0:1, :], ones_col, xs, start=(k == 0), stop=(k == KD - 1)),
                reads=[x_name, "consts"], writes=[kpm], tag="ln_sum")
            p.op("scalar", lambda e, xs=xs: e.activation(sq[:, :], xs, AF.Square),
                 reads=[x_name], writes=[sq_name], tag="ln_sq")
            p.op("tensor", lambda e, k=k: e.matmul(
                pq[0:1, :], ones_col, sq[:, :], start=(k == 0), stop=(k == KD - 1)),
                reads=[sq_name, "consts"], writes=[kpq], tag="ln_sum2")
        m = rows[0:1, 0:TC]
        s2 = rows[0:1, TC:2 * TC]
        r = rows[0:1, 2 * TC:3 * TC]
        mr = rows[0:1, 3 * TC:4 * TC]
        p.op("vector", lambda e, m=m: e.tensor_scalar(
            m, pm[0:1, :], 1.0 / 1024, None, ALU.mult),
            reads=[kpm], writes=[row_name], tag="ln_m")
        p.op("vector", lambda e, s2=s2: e.tensor_scalar(
            s2, pq[0:1, :], 1.0 / 1024, None, ALU.mult),
            reads=[kpq], writes=[row_name], tag="ln_s2")
        p.op("vector", lambda e, m=m, mr=mr: e.tensor_tensor(
            mr, m, m, ALU.mult), reads=[row_name], writes=[row_name], tag="ln_msq")
        p.op("vector", lambda e, s2=s2, mr=mr: e.tensor_tensor(
            s2, s2, mr, ALU.subtract), reads=[row_name], writes=[row_name], tag="ln_v")
        p.op("scalar", lambda e, s2=s2, r=r: e.activation(
            r, s2, AF.Sqrt, bias=eps_col[0:1, 0:1]),
            reads=[row_name, "consts"], writes=[row_name], tag="ln_sqrt")
        p.op("vector", lambda e, r=r: e.reciprocal(r, r),
             reads=[row_name], writes=[row_name], tag="ln_rstd")
        if stats_name is not None:
            st = ctx.sb[stats_name]
            p.op("vector", lambda e, m=m, tc=tc: e.tensor_copy(
                st[0:1, tc * TC:(tc + 1) * TC], m),
                reads=[row_name], writes=[stats_name], tag="ln_stm")
            p.op("vector", lambda e, r=r, tc=tc: e.tensor_copy(
                st[0:1, T + tc * TC:T + (tc + 1) * TC], r),
                reads=[row_name], writes=[stats_name], tag="ln_str")
        p.op("vector", lambda e, m=m, r=r, mr=mr: e.tensor_tensor(
            mr, m, r, ALU.mult), reads=[row_name], writes=[row_name], tag="ln_mr")
        p.op("vector", lambda e, mr=mr: e.tensor_scalar(
            mr, mr, -1.0, None, ALU.mult), reads=[row_name], writes=[row_name],
            tag="ln_mrneg")
        p.op("tensor", lambda e, mr=mr: e.matmul(pmb[:, :], ones_row, mr,
             start=True, stop=True), reads=[row_name, "consts"],
             writes=[kmb], tag="ln_bcast_m")
        p.op("tensor", lambda e, r=r: e.matmul(prb[:, :], ones_row, r,
             start=True, stop=True), reads=[row_name, "consts"],
             writes=[krb], tag="ln_bcast_r")
        for k in range(KD):
            xs = x[:, k * T + tc * TC: k * T + (tc + 1) * TC]
            xns = xn[:, k * T + tc * TC: k * T + (tc + 1) * TC]
            p.op("vector", lambda e, xs=xs: e.tensor_tensor(
                sq[:, :], xs, prb[:, :], ALU.mult),
                reads=[x_name, krb], writes=[sq_name], tag="ln_n1")
            p.op("vector", lambda e, xns=xns: e.tensor_tensor(
                xns, sq[:, :], pmb[:, :], ALU.add),
                reads=[sq_name, kmb], writes=[xn_name], tag="ln_n2")


def emit_proj(ctx, dst_name, src_name, w_dram, K_tiles, M_tiles,
              wslot="wbuf", banks=("ps0", "ps1", "ps2", "ps3"),
              w_row0=0, w_col0=0, src_t0=0, dst_t0=0,
              bias=None, act=None, scale=None, dst_dt_bf=True,
              add_to=None, add_bank="ps4", n_tok=T, wq="gpsimd",
              w_name=None, epilogue=None):
    p = ctx.p
    src = ctx.sb[src_name]
    dst = ctx.sb[dst_name] if dst_name else None
    wsl = ctx.sb[wslot]
    wcols = K_tiles * 128
    M_total = M_tiles * 128
    nch = n_tok // 512
    ps = [ctx.ps[b] for b in banks]
    for m in range(M_tiles):
        slot = m % 2
        wkey = f"{wslot}{slot}"
        wv = wsl[:, slot * wcols:(slot + 1) * wcols]
        src_ap = w_dram[w_row0:w_row0 + K_tiles * 128,
                        w_col0 + m * 128: w_col0 + (m + 1) * 128]
        p.dma("gpsimd", wv, src_ap.rearrange("(k p) c -> p k c", p=128),
              reads=[], writes=[wkey], tag=f"w_{dst_name}_{m}")
        for k in range(K_tiles):
            lhsT = wv[:, k * 128:(k + 1) * 128]
            for tc in range(nch):
                rhs = src[:, src_t0 + k * T + tc * 512: src_t0 + k * T + tc * 512 + 512]
                p.op("tensor", lambda e, lhsT=lhsT, rhs=rhs, k=k, tc=tc: e.matmul(
                    ps[tc][:, :], lhsT, rhs,
                    start=(k == 0), stop=(k == K_tiles - 1)),
                    reads=[wkey, src_name], writes=[banks[tc]],
                    tag=f"mm_{dst_name}")
        for tc in range(nch):
            pb = ps[tc]
            if epilogue is not None:
                epilogue(m, tc, pb, banks[tc])
                continue
            if add_to is not None:
                tgt = ctx.sb[add_to]
                out_ap = tgt[:, dst_t0 + m * T + tc * 512: dst_t0 + m * T + tc * 512 + 512]
                p.op("vector", lambda e, out_ap=out_ap, pb=pb: e.tensor_tensor(
                    out_ap, out_ap, pb[:, :], ALU.add),
                    reads=[banks[tc], add_to], writes=[add_to],
                    tag=f"acc_{add_to}")
                continue
            out_ap = dst[:, dst_t0 + m * T + tc * 512: dst_t0 + m * T + tc * 512 + 512]
            kw = {}
            if bias is not None:
                bname, bget = bias
                kw["bias"] = bget(m)
                rd = [banks[tc], bname]
            else:
                rd = [banks[tc]]
            if scale is not None:
                kw["scale"] = scale
            fn = act if act is not None else AF.Identity
            p.op("scalar", lambda e, out_ap=out_ap, pb=pb, fn=fn, kw=dict(kw): e.activation(
                out_ap, pb[:, :], fn, **kw),
                reads=rd, writes=[dst_name], tag=f"ev_{dst_name}")


def register_attn_consts(ctx):
    i = np.arange(128)[:, None]
    j = np.arange(512)[None, :]
    for r in range(4):
        ctx.const_f32(f"cmask{r}", np.where(128 * r + i > j, -1e30, 0.0))


def emit_mha(ctx, xn_name, out_name, wspecs, att_scale, biases=None,
             qb="mha_q", kb="mha_k", vb="mha_v", vtb="mha_vt",
             probs=("mha_pA", "mha_pB"), den="mha_den",
             banks=("ps0", "ps1", "ps2", "ps3", "ps4", "ps5", "ps6", "ps7")):
    p = ctx.p
    ident = ctx.cap("ident")
    ones_col = ctx.cap("ones_col_bf")[:, 0:1]
    ones_row = ctx.cap("ones_row")[0:1, :]
    b_tr, b_sc0, b_sc1, b_out, b_den = banks[4], banks[5], banks[6], banks[7], banks[0]
    ps_tr, ps_sc0, ps_sc1, ps_out, ps_den = (ctx.ps[b] for b in
                                             (b_tr, b_sc0, b_sc1, b_out, b_den))
    for hg in range(2):
        for name, buf, dt_scale, bidx in (("q", qb, att_scale, 0),
                                          ("k", kb, None, 1),
                                          ("v", vb, None, 2)):
            dram, col0 = wspecs[bidx]
            bias = None
            if biases is not None:
                bsb, boffs = biases
                bias = (bsb, lambda m, bo=boffs[bidx], hg=hg:
                        ctx.sb[bsb][:, bo + hg * 4 + m: bo + hg * 4 + m + 1])
            emit_proj(ctx, buf, xn_name, dram, 8, 4,
                      wslot="wbuf", banks=banks[0:4],
                      w_col0=col0 + hg * 512, bias=bias,
                      scale=dt_scale)
        for hl in range(8):
            base = (hl % 2) * 64
            vt = ctx.sb[vtb]
            for kt in range(16):
                src = ctx.sb[vb][base:base + 64,
                                 (hl // 2) * T + kt * 128:(hl // 2) * T + (kt + 1) * 128]
                idn = ident[base:base + 64, base:base + 64]
                p.op("tensor", lambda e, src=src, idn=idn: e.transpose(
                    ps_tr[:, 0:64], src, idn),
                    reads=[vb, "consts"], writes=[b_tr], tag="mha_tr")
                p.op("scalar", lambda e, kt=kt: e.activation(
                    vt[:, kt * 64:(kt + 1) * 64], ps_tr[:, 0:64], AF.Identity),
                    reads=[b_tr], writes=[vtb], tag="mha_vt_ev")
            for qc in range(4):
                for kt in range(4 * qc + 4):
                    psc = ps_sc0 if kt % 2 == 0 else ps_sc1
                    bsc = b_sc0 if kt % 2 == 0 else b_sc1
                    lhsT = ctx.sb[kb][base:base + 64,
                                      (hl // 2) * T + kt * 128:(hl // 2) * T + (kt + 1) * 128]
                    rhs = ctx.sb[qb][base:base + 64,
                                     (hl // 2) * T + qc * 512:(hl // 2) * T + (qc + 1) * 512]
                    p.op("tensor", lambda e, lhsT=lhsT, rhs=rhs, psc=psc: e.matmul(
                        psc[:, :], lhsT, rhs, start=True, stop=True),
                        reads=[kb, qb], writes=[bsc], tag="mha_qk")
                    r = kt - 4 * qc
                    if r >= 0:
                        cm = ctx.cap(f"cmask{r}")
                        p.op("vector", lambda e, psc=psc, cm=cm: e.tensor_tensor(
                            psc[:, :], psc[:, :], cm, ALU.add),
                            reads=[bsc, "consts"], writes=[bsc], tag="mha_mask")
                    pr = ctx.sb[probs[kt % 2]]
                    prk = probs[kt % 2]
                    p.op("scalar", lambda e, psc=psc, pr=pr: e.activation(
                        pr[:, :], psc[:, :], AF.Exp),
                        reads=[bsc], writes=[prk], tag="mha_exp")
                    p.op("tensor", lambda e, kt=kt, pr=pr: e.matmul(
                        ps_out[0:64, :], vt[:, kt * 64:(kt + 1) * 64], pr[:, :],
                        start=(kt == 0), stop=(kt == 4 * qc + 3)),
                        reads=[vtb, prk], writes=[b_out], tag="mha_pv")
                    p.op("tensor", lambda e, kt=kt, pr=pr, qc=qc: e.matmul(
                        ps_den[64:65, :], ones_col, pr[:, :],
                        start=(kt == 0), stop=(kt == 4 * qc + 3)),
                        reads=["consts", prk], writes=[b_den], tag="mha_sum")
                dn = ctx.sb[den]
                p.op("vector", lambda e: e.reciprocal(dn[0:1, 0:512], ps_den[64:65, :]),
                     reads=[b_den], writes=[den], tag="mha_recip")
                p.op("tensor", lambda e: e.matmul(ps_den[0:64, :], ones_row[0:1, 0:64],
                     dn[0:1, 0:512], start=True, stop=True),
                     reads=[den, "consts"], writes=[b_den], tag="mha_denb")
                p.op("scalar", lambda e: e.activation(dn[0:64, 512:1024],
                     ps_out[0:64, :], AF.Identity),
                     reads=[b_out], writes=[den], tag="mha_oev")
                h = hg * 8 + hl
                osl = ctx.sb[out_name][(h % 2) * 64:(h % 2) * 64 + 64,
                                       (h // 2) * T + qc * 512:(h // 2) * T + (qc + 1) * 512]
                p.op("vector", lambda e, osl=osl: e.tensor_tensor(
                    osl, dn[0:64, 512:1024], ps_den[0:64, :], ALU.mult),
                    reads=[den, b_den], writes=[out_name], tag="mha_onorm")


def emit_l1ffn(ctx, xn_name, dst_name, Wr, Wk, Wv, mu_name="l1_mu",
               xr="l1_xr", xk="l1_xk", dbuf="l1_d", rbuf="l1_r",
               banks=("ps0", "ps1", "ps2", "ps3")):
    p = ctx.p
    xn = ctx.sb[xn_name]
    mu = ctx.sb[mu_name]
    d = ctx.sb[dbuf]
    for k in range(KD):
        xs = xn[:, k * T:(k + 1) * T]
        p.op("vector", lambda e, xs=xs: e.tensor_tensor(
            d[:, 1:T], xs[:, 1:T], xs[:, 0:T - 1], ALU.subtract),
            reads=[xn_name], writes=[dbuf], tag="l1_d")
        p.op("vector", lambda e, xs=xs: e.tensor_copy(d[:, 0:1], xs[:, 0:1]),
             reads=[xn_name], writes=[dbuf], tag="l1_d0")
        for which, buf, mc in (("r", xr, k), ("k", xk, 8 + k)):
            xb = ctx.sb[buf]
            p.op("vector", lambda e, mc=mc: e.tensor_scalar(
                ctx.sb[rbuf][:, 0:T], d[:, 0:T], mu[:, mc:mc + 1], None, ALU.mult),
                reads=[dbuf, mu_name], writes=[rbuf], tag="l1_dm")
            p.op("vector", lambda e, xs=xs, xb=xb, k=k: e.tensor_tensor(
                xb[:, k * T + 1:(k + 1) * T], ctx.sb[rbuf][:, 1:T],
                xs[:, 0:T - 1], ALU.add),
                reads=[rbuf, xn_name], writes=[buf], tag="l1_xr")
            p.op("vector", lambda e, xb=xb, k=k: e.tensor_copy(
                xb[:, k * T:k * T + 1], ctx.sb[rbuf][:, 0:1]),
                reads=[rbuf], writes=[buf], tag="l1_xr0")
    emit_proj(ctx, rbuf + "big", xr, Wr, 8, 8, wslot="wbuf", banks=banks,
              act=AF.Sigmoid)
    emit_proj(ctx, xk + "o", xk, Wk, 8, 8, wslot="wbuf", banks=banks,
              act=AF.Relu)
    ko = ctx.sb[xk + "o"]
    for k in range(KD):
        p.op("vector", lambda e, k=k: e.tensor_tensor(
            ko[:, k * T:(k + 1) * T], ko[:, k * T:(k + 1) * T],
            ko[:, k * T:(k + 1) * T], ALU.mult),
            reads=[xk + "o"], writes=[xk + "o"], tag="l1_sq")
    rb = ctx.sb[rbuf + "big"]
    dst = ctx.sb[dst_name]

    def epi(m, tc, pb, bank):
        osl = dst[:, m * T + tc * 512:m * T + (tc + 1) * 512]
        rsl = rb[:, m * T + tc * 512:m * T + (tc + 1) * 512]
        p.op("vector", lambda e, osl=osl, rsl=rsl, pb=pb: e.tensor_tensor(
            osl, rsl, pb[:, :], ALU.mult),
            reads=[bank, rbuf + "big"], writes=[dst_name], tag="l1_out")

    emit_proj(ctx, None, xk + "o", Wv, 8, 8, wslot="wbuf", banks=banks,
              epilogue=epi)


def register_moe_consts(ctx):
    i = np.arange(128)
    ctx.const_f32("iota_d", np.broadcast_to(i[None, :] * 1e-11, (128, 128)).copy())
    ctx.const_f32("gmask16", (i[:, None] // 8 == np.arange(16)[None, :])
                  .astype(np.float32))


def register_quant_consts(ctx):
    ctx.const_f32("qc512", np.full((128, 1), 1024.0))
    ctx.const_f32("qcm", np.full((128, 1), -3590.0))
    ctx.const_f32("qc6", np.full((128, 1), 6.0))


def emit_quant(ctx, src, dst, W, scr="q", srckey="", dstkey="", sgn_src=None):
    p = ctx.p
    qa = ctx.sb[scr + "a"]
    qs = ctx.sb[scr + "s"]
    qf = ctx.sb[scr + "f"]
    qbm = ctx.sb[scr + "bm"]
    ka, ks, kf, kbm = scr + "a", scr + "s", scr + "f", scr + "bm"
    NB = W // 16
    c512 = ctx.cap("qc512")[:, 0:1]
    cm = ctx.cap("qcm")[:, 0:1]
    sgn_in = sgn_src if sgn_src is not None else src
    p.op("scalar", lambda e: e.activation(qa[:, 0:W], src, AF.Abs),
         reads=[srckey], writes=[ka], tag="q_abs")
    p.op("vector", lambda e: e.tensor_reduce(
        qbm[:, 0:NB], qa[:, 0:W].rearrange("p (b s) -> p b s", s=16), AX.X, ALU.max),
        reads=[ka], writes=[kbm], tag="q_bm")
    p.op("vector", lambda e: e.tensor_scalar(
        qbm[:, 0:NB], qbm[:, 0:NB], 6e-12, None, ALU.max),
        reads=[kbm], writes=[kbm], tag="q_bmc")
    p.op("vector", lambda e: e.reciprocal(qbm[:, NB:2 * NB], qbm[:, 0:NB]),
         reads=[kbm], writes=[kbm], tag="q_rcp")
    p.op("vector", lambda e: e.tensor_scalar(
        qbm[:, 2 * NB:3 * NB], qbm[:, 0:NB], 1.0 / 6.0, None, ALU.mult),
        reads=[kbm], writes=[kbm], tag="q_bm6")
    rbc = qbm[:, NB:2 * NB].unsqueeze(-1).broadcast_to([128, NB, 16])
    bmbc = qbm[:, 2 * NB:3 * NB].unsqueeze(-1).broadcast_to([128, NB, 16])
    p.op("vector", lambda e: e.tensor_tensor(
        qa[:, 0:W].rearrange("p (b s) -> p b s", s=16),
        qa[:, 0:W].rearrange("p (b s) -> p b s", s=16), rbc, ALU.mult),
        reads=[ka, kbm], writes=[ka], tag="q_a")
    for i, sc in enumerate((12.0, 6.0, 3.0)):
        p.op("scalar", lambda e, i=i, sc=sc: e.activation(
            qf[:, i * W:(i + 1) * W], qa[:, 0:W], AF.Identity, bias=c512, scale=sc),
            reads=[ka, "consts"], writes=[kf], tag="q_r")
    p.op("vector", lambda e: e.tensor_scalar(
        qf[:, 3 * W:4 * W], qf[:, 0:W], 0.5, 514.0, ALU.mult, ALU.min),
        reads=[kf], writes=[kf], tag="q_A")
    p.op("vector", lambda e: e.tensor_scalar(
        qf[:, 4 * W:5 * W], qf[:, W:2 * W], 1026.0, 1028.0, ALU.max, ALU.min),
        reads=[kf], writes=[kf], tag="q_B")
    p.op("vector", lambda e: e.tensor_scalar(
        qf[:, 5 * W:6 * W], qf[:, 2 * W:3 * W], 1026.0, 2.0, ALU.max, ALU.mult),
        reads=[kf], writes=[kf], tag="q_C")
    p.op("scalar", lambda e: e.activation(qa[:, 0:W], sgn_in, AF.Sign),
         reads=[srckey], writes=[ka], tag="q_sgn")
    p.op("vector", lambda e: e.tensor_tensor(
        qs[:, 0:W], qf[:, 3 * W:4 * W], qf[:, 4 * W:5 * W], ALU.add),
        reads=[kf], writes=[ks], tag="q_s1")
    p.op("vector", lambda e: e.tensor_tensor(
        qs[:, 0:W], qs[:, 0:W], qf[:, 5 * W:6 * W], ALU.add),
        reads=[kf, ks], writes=[ks], tag="q_s2")
    p.op("scalar", lambda e: e.activation(
        qs[:, 0:W], qs[:, 0:W], AF.Identity, bias=cm, scale=1.0),
        reads=[ks, "consts"], writes=[ks], tag="q_d1")
    p.op("vector", lambda e: e.tensor_tensor(
        qs[:, 0:W], qs[:, 0:W], qa[:, 0:W], ALU.mult),
        reads=[ka, ks], writes=[ks], tag="q_d2")
    p.op("vector", lambda e: e.tensor_tensor(
        dst.rearrange("p (b s) -> p b s", s=16),
        qs[:, 0:W].rearrange("p (b s) -> p b s", s=16), bmbc, ALU.mult),
        reads=[ks, kbm], writes=[dstkey], tag="q_d3")


def emit_l3ffn(ctx, xn_name, WG, WU, WD, WGQ, WUQ, WDQ, epilogue,
               skip_wquant=False):
    p = ctx.p
    identb = ctx.cap("ident_bf")
    psbf = ctx.ps["psbf"]
    IC = [(i * 512, min(512, 2816 - i * 512)) for i in range(6)]
    if not skip_wquant:
        for W_nat, WQT, nrows, ncols in ((WG, WGQ, 2816, 1024),
                                         (WU, WUQ, 2816, 1024),
                                         (WD, WDQ, 1024, 2816)):
            for rt in range(nrows // 128):
                for half in range(ncols // 1408 if ncols > 1408 else 1):
                    h0 = half * 1408
                    hw = min(1408, ncols - h0)
                    p.dma("sync", ctx.sb["l3w"][:, 0:hw],
                          W_nat[rt * 128:(rt + 1) * 128, h0:h0 + hw],
                          writes=["l3w"], tag="l3wload")
                    emit_quant(ctx, ctx.sb["l3w"][:, 0:hw],
                               ctx.sb["l3wq"][:, h0:h0 + hw], hw,
                               srckey="l3w", dstkey="l3wq")
                for kc in range(ncols // 128):
                    p.op("tensor", lambda e, kc=kc: e.transpose(
                        psbf[:, 0:128], ctx.sb["l3wq"][:, kc * 128:(kc + 1) * 128],
                        identb), reads=["l3wq", "consts"], writes=["psbf"],
                        tag="l3wtr")
                    p.op("scalar", lambda e, kc=kc: e.activation(
                        ctx.sb["l3st"][:, kc * 128:(kc + 1) * 128], psbf[:, 0:128],
                        AF.Identity), reads=["psbf"], writes=["l3st"], tag="l3wtrev")
                p.dma("sync",
                      WQT[:, rt * 128:(rt + 1) * 128].rearrange(
                          "(kc p) c -> p kc c", p=128),
                      ctx.sb["l3st"][:, 0:ncols].rearrange(
                          "p (kc c) -> p kc c", c=128),
                      reads=["l3st"], writes=["WQT"], tag="l3wspill")
    xn = ctx.sb[xn_name]
    for tt in range(16):
        for k in range(KD):
            p.op("tensor", lambda e, k=k, tt=tt: e.transpose(
                psbf[:, k * 128:(k + 1) * 128],
                xn[:, k * T + tt * 128:k * T + (tt + 1) * 128], identb),
                reads=[xn_name, "consts"], writes=["psbf"], tag="l3xtr")
        p.op("scalar", lambda e: e.activation(
            ctx.sb["l3xt"][:, 0:1024], psbf[:, 0:1024], AF.Identity),
            reads=["psbf"], writes=["l3xt"], tag="l3xtrev")
        emit_quant(ctx, ctx.sb["l3xt"][:, 0:1024], ctx.sb["l3xq"][:, 0:1024],
                   1024, srckey="l3xt", dstkey="l3xq")
        for k in range(KD):
            p.op("tensor", lambda e, k=k: e.transpose(
                psbf[:, k * 128:(k + 1) * 128],
                ctx.sb["l3xq"][:, k * 128:(k + 1) * 128], identb),
                reads=["l3xq", "consts"], writes=["psbf"], tag="l3xqtr")
        for k in range(KD):
            p.op("scalar", lambda e, k=k, tt=tt: e.activation(
                ctx.sb["xq"][:, k * T + tt * 128:k * T + (tt + 1) * 128],
                psbf[:, k * 128:(k + 1) * 128], AF.Identity),
                reads=["psbf"], writes=["xq"], tag="l3xqev")
    xq = ctx.sb["xq"]
    hsc = ctx.sb["l3hsc"]
    hbuf = ctx.sb["l3hbuf"]
    hq = ctx.sb["l3hq"]
    wb = ctx.sb["wbuf"]
    wbd = ctx.sb["wbufd"]
    ps = [ctx.ps[f"ps{i}"] for i in range(6)]
    for tc in range(4):
        for ici, (icol0, iw) in enumerate(IC):
            nit = iw // 128
            for wi, (WQT, wkey) in enumerate(((WGQ, "wbufA"), (WUQ, "wbufB"))):
                wsl = wb[:, wi * 4096:wi * 4096 + 8 * 512]
                p.dma("gpsimd", wsl[:, 0:8 * iw].rearrange("p (k c) -> p k c", c=iw),
                      WQT[:, icol0:icol0 + iw].rearrange("(k p) c -> p k c", p=128),
                      reads=["WQT"], writes=[wkey], tag="l3wg_ld")
                for k in range(KD):
                    lhs_all = [xq[:, k * T + tc * 512 + tt * 128:
                                  k * T + tc * 512 + (tt + 1) * 128]
                               for tt in range(4)]
                    for tt in range(4):
                        p.op("tensor", lambda e, k=k, tt=tt, wsl=wsl, iw=iw,
                             lh=lhs_all[tt]: e.matmul(
                                 ps[tt][:, 0:iw], lh, wsl[:, k * iw:(k + 1) * iw],
                                 start=(k == 0), stop=(k == KD - 1)),
                            reads=[wkey, "xq"], writes=[f"ps{tt}"], tag="l3gu")
                if wi == 0:
                    for tt in range(4):
                        p.op("scalar", lambda e, tt=tt, iw=iw: e.activation(
                            hsc[:, tt * 512:tt * 512 + iw], ps[tt][:, 0:iw],
                            AF.Silu), reads=[f"ps{tt}"], writes=["l3hsc"],
                            tag="l3silu")
            for tt in range(4):
                p.op("vector", lambda e, tt=tt, iw=iw: e.tensor_tensor(
                    hbuf[:, 0:iw], hsc[:, tt * 512:tt * 512 + iw],
                    ps[tt][:, 0:iw], ALU.mult),
                    reads=["l3hsc", f"ps{tt}"], writes=["l3hbuf"], tag="l3h")
                emit_quant(ctx, hbuf[:, 0:iw], ctx.sb["l3hql"][:, 0:iw], iw,
                           srckey="l3hbuf", dstkey="l3hql")
                for j in range(nit):
                    p.op("tensor", lambda e, j=j: e.transpose(
                        psbf[:, j * 128:(j + 1) * 128],
                        ctx.sb["l3hql"][:, j * 128:(j + 1) * 128], identb),
                        reads=["l3hql", "consts"], writes=["psbf"], tag="l3htr")
                for j in range(nit):
                    it = ici * 4 + j
                    p.op("scalar", lambda e, j=j, it=it, tt=tt: e.activation(
                        hq[:, it * 512 + tt * 128:it * 512 + (tt + 1) * 128],
                        psbf[:, j * 128:(j + 1) * 128], AF.Identity),
                        reads=["psbf"], writes=["l3hq"], tag="l3hqev")
        for dt in range(KD):
            slot = dt % 2
            wsl = wbd[:, slot * 22 * 128:(slot + 1) * 22 * 128]
            p.dma("gpsimd", wsl.rearrange("p (it c) -> p it c", c=128),
                  WDQ[:, dt * 128:(dt + 1) * 128].rearrange("(it p) c -> p it c", p=128),
                  reads=["WQT"], writes=[f"wbufd{slot}"], tag="l3wd_ld")
            pb = ps[4 + slot]
            for it in range(22):
                p.op("tensor", lambda e, it=it, wsl=wsl, pb=pb: e.matmul(
                    pb[:, :], wsl[:, it * 128:(it + 1) * 128],
                    hq[:, it * 512:(it + 1) * 512],
                    start=(it == 0), stop=(it == 21)),
                    reads=[f"wbufd{slot}", "l3hq"], writes=[f"ps{4 + slot}"],
                    tag="l3dn")
            epilogue(dt, tc, pb, f"ps{4 + slot}")


def emit_mamba(ctx, xn_name, INPJ, OUTPJ, ZSP, cols_name, epilogue):
    p = ctx.p
    nq = T // 512
    ident = ctx.cap("ident")
    identb = ctx.cap("ident_bf")
    psb7 = ctx.ps["ps7"]
    ones_row = ctx.cap("ones_row")
    ones_col = ctx.cap("ones_col")
    cols = ctx.sb[cols_name]
    ck = cols_name
    CW0, CB0, DTB, ALOG, DEXP, GNW, NRMW, ACOL = 0, 96, 120, 121, 122, 138, 154, 162
    eps_col = ctx.cap("eps_col")
    p.op("scalar", lambda e: e.activation(cols[:, ACOL:ACOL + 1],
         cols[:, ALOG:ALOG + 1], AF.Exp), reads=[ck], writes=[ck], tag="m_A")
    p.op("vector", lambda e: e.tensor_scalar(cols[:, ACOL:ACOL + 1],
         cols[:, ACOL:ACOL + 1], -1.0, None, ALU.mult), reads=[ck], writes=[ck])
    dg = ctx.sb["m_diag"]
    xn = ctx.sb[xn_name]
    sq = ctx.sb["ln_sq"]
    rows = ctx.sb["m_rows"]
    pm = ctx.ps["ps6"]
    for tc in range(NCH):
        for k in range(KD):
            p.op("scalar", lambda e, k=k, tc=tc: e.activation(
                sq[:, :], xn[:, k * T + tc * 512:k * T + (tc + 1) * 512],
                AF.Square), reads=[xn_name], writes=["ln_sq"], tag="m_sq")
            p.op("tensor", lambda e, k=k: e.matmul(
                pm[0:1, :], ones_col[:, 0:1], sq[:, :],
                start=(k == 0), stop=(k == KD - 1)),
                reads=["ln_sq", "consts"], writes=["ps6"], tag="m_sum2")
        r = rows[0:1, tc * 512:(tc + 1) * 512]
        p.op("vector", lambda e, r=r: e.tensor_scalar(
            r, pm[0:1, :], 1.0 / 1024, None, ALU.mult),
            reads=["ps6"], writes=["m_rows"], tag="m_ms")
        p.op("scalar", lambda e, r=r: e.activation(
            r, r, AF.Sqrt, bias=eps_col[0:1, 0:1]),
            reads=["m_rows", "consts"], writes=["m_rows"], tag="m_msq")
        p.op("vector", lambda e, r=r: e.reciprocal(r, r),
             reads=["m_rows"], writes=["m_rows"], tag="m_rr")
    S = ctx.sb["m_S"]
    Sb = ctx.sb["m_Sb"]
    p.op("vector", lambda e: e.memset(S[:, :], 0.0), writes=["m_S"])
    p.op("vector", lambda e: e.memset(Sb[:, :], 0.0), writes=["m_Sb"])
    xcp = ctx.sb["m_xcp"]
    for ct in range(24):
        p.op("vector", lambda e, ct=ct: e.memset(xcp[:, ct * 515:ct * 515 + 3], 0.0),
             writes=["m_xcp"], tag="m_pad0")
    wsl = ctx.sb["wbuf"]
    xmq = ctx.sb["m_xmq"]
    zq = ctx.sb["m_zq"]
    xbc = ctx.sb["m_xbc"]
    dts = ctx.sb["m_dt"]
    thp = ctx.sb["m_thp"]
    rowR = ctx.sb["m_rowR"]
    CUMDR = ctx.dram_t("M_CUMB", (32, 128), F32)
    yq = ctx.sb["m_yq"]
    wexp = ctx.sb["m_wexp"]
    wbf = ctx.sb["m_wbf"]
    xtok = ctx.sb["m_xtok"]
    btok = ctx.sb["m_btok"]
    chat = ctx.sb["m_chat"]
    gsc = ctx.sb["m_gsc"]
    ps0, ps1, ps2, ps3, ps4, ps5 = (ctx.ps[f"ps{i}"] for i in range(6))
    for q in range(nq):
        t0 = q * 512
        mbq = ctx.ps["ps6"]
        p.op("tensor", lambda e, t0=t0: e.matmul(mbq[:, :], ones_row[0:1, :],
             rows[0:1, t0:t0 + 512], start=True, stop=True),
             reads=["m_rows", "consts"], writes=["ps6"], tag="m_rbc")
        for k in range(KD):
            p.op("vector", lambda e, k=k, t0=t0: e.tensor_tensor(
                sq[:, :], xn[:, k * T + t0:k * T + t0 + 512], mbq[:, :], ALU.mult),
                reads=[xn_name, "ps6"], writes=["ln_sq"], tag="m_xm1")
            p.op("vector", lambda e, k=k: e.tensor_scalar(
                xmq[:, k * 512:(k + 1) * 512], sq[:, :],
                cols[:, NRMW + k:NRMW + k + 1], None, ALU.mult),
                reads=["ln_sq", ck], writes=["m_xmq"], tag="m_xm2")
        for m in range(41):
            mw = 128 if m < 40 else 32
            slot = m % 2
            wk = f"wb{slot}"
            wv = wsl[:, slot * 1024:slot * 1024 + 1024]
            p.dma("gpsimd", wv[:, 0:8 * mw].rearrange("p (k c) -> p k c", c=mw),
                  INPJ[:, m * 128:m * 128 + mw].rearrange("(k p) c -> p k c", p=128),
                  writes=[wk], tag="m_wld")
            pb = ps0 if slot == 0 else ps1
            pk = f"ps{slot}"
            for k in range(KD):
                p.op("tensor", lambda e, k=k, wv=wv, mw=mw, pb=pb: e.matmul(
                    pb[0:mw, :], wv[:, k * mw:(k + 1) * mw],
                    xmq[:, k * 512:(k + 1) * 512],
                    start=(k == 0), stop=(k == KD - 1)),
                    reads=[wk, "m_xmq"], writes=[pk], tag="m_inpj")
            if m < 16:
                p.op("scalar", lambda e, m=m, pb=pb: e.activation(
                    zq[:, m * 512:(m + 1) * 512], pb[:, :], AF.Identity),
                    reads=[pk], writes=["m_zq"], tag="m_zev")
            elif m < 40:
                ct = m - 16
                p.op("scalar", lambda e, ct=ct, pb=pb: e.activation(
                    xcp[:, ct * 515 + 3:ct * 515 + 515], pb[:, :], AF.Identity),
                    reads=[pk], writes=["m_xcp"], tag="m_xcev")
            else:
                p.op("scalar", lambda e, pb=pb: e.activation(
                    dts[0:32, 0:512], pb[0:32, :], AF.Identity),
                    reads=[pk], writes=["m_dt"], tag="m_dtev")
        p.dma("sync", ZSP[:, t0:t0 + 512].rearrange("(m p) c -> p m c", p=128),
              zq[:, :].rearrange("p (m c) -> p m c", c=512), reads=["m_zq"],
              writes=["ZSP"], tag="m_zsp")
        for ct in range(24):
            for k in range(4):
                p.op("vector", lambda e, ct=ct, k=k: e.tensor_scalar(
                    dg[:, k * 128:(k + 1) * 128],
                    ident, cols[:, CW0 + ct * 4 + k:CW0 + ct * 4 + k + 1], None,
                    ALU.mult), reads=[ck, "consts"], writes=["m_diag"], tag="m_dg")
            for k in range(4):
                p.op("tensor", lambda e, ct=ct, k=k: e.matmul(
                    ps2[:, :], dg[:, k * 128:(k + 1) * 128],
                    xcp[:, ct * 515 + k:ct * 515 + k + 512],
                    start=(k == 0), stop=(k == 3)),
                    reads=["m_diag", "m_xcp"], writes=["ps2"], tag="m_conv")
            p.op("scalar", lambda e, ct=ct: e.activation(
                xbc[:, ct * 512:(ct + 1) * 512], ps2[:, :], AF.Silu,
                bias=cols[:, CB0 + ct:CB0 + ct + 1]),
                reads=["ps2", ck], writes=["m_xbc"], tag="m_convev")
            p.op("vector", lambda e, ct=ct: e.tensor_copy(
                xcp[:, ct * 515:ct * 515 + 3],
                xcp[:, ct * 515 + 512:ct * 515 + 515]),
                reads=["m_xcp"], writes=["m_xcp"], tag="m_tail")
        D0, D1, D2, D3, D4, D5 = (dts[0:32, i * 512:(i + 1) * 512] for i in range(6))
        p.op("scalar", lambda e: e.activation(D1, D0, AF.Identity,
             bias=cols[0:32, DTB:DTB + 1]), reads=["m_dt", ck], writes=["m_dt"])
        p.op("scalar", lambda e: e.activation(D2, D1, AF.Abs),
             reads=["m_dt"], writes=["m_dt"])
        p.op("scalar", lambda e: e.activation(D2, D2, AF.Exp, scale=-1.0),
             reads=["m_dt"], writes=["m_dt"])
        p.op("scalar", lambda e: e.activation(D2, D2, AF.Ln,
             bias=ones_col[0:32, 0:1]), reads=["m_dt", "consts"],
             writes=["m_dt"])
        p.op("scalar", lambda e: e.activation(D3, D1, AF.Relu),
             reads=["m_dt"], writes=["m_dt"])
        p.op("vector", lambda e: e.tensor_add(D3, D3, D2),
             reads=["m_dt"], writes=["m_dt"])
        p.op("vector", lambda e: e.tensor_scalar(D4, D3,
             cols[0:32, ACOL:ACOL + 1], None, ALU.mult),
             reads=["m_dt", ck], writes=["m_dt"])
        for c in range(4):
            zb = ctx.cap("zero_col")[0:32, 0:1].broadcast_to([32, 128])
            p.op("vector", lambda e, c=c, zb=zb: e.tensor_tensor_scan(
                D5[:, c * 128:(c + 1) * 128], D4[:, c * 128:(c + 1) * 128],
                zb, 0.0, ALU.add, ALU.add),
                reads=["m_dt", "consts"], writes=["m_dt"], tag="m_cum")
        p.op("scalar", lambda e: e.activation(dts[32:64, 0:512],
             D5, AF.Exp), reads=["m_dt"], writes=["m_dt"], tag="m_acum")
        for c in range(4):
            tc0 = c * 128
            p.dma("sync", CUMDR[:, :], D5[:, tc0:tc0 + 128],
                  reads=["m_dt"], writes=["M_CUMB"], tag="m_cb1")
            p.dma("sync", rowR[0:1, 0:4096].rearrange("q (h t) -> q h t", t=128),
                  CUMDR[:, :].unsqueeze(0), reads=["M_CUMB"], writes=["m_rowRa"],
                  tag="m_cb2")
            p.dma("sync", CUMDR[:, :], dts[32:64, tc0:tc0 + 128],
                  reads=["m_dt"], writes=["M_CUMB"], tag="m_cb3")
            p.dma("sync", rowR[64:65, 0:4096].rearrange("q (h t) -> q h t", t=128),
                  CUMDR[:, :].unsqueeze(0), reads=["M_CUMB"], writes=["m_rowRb"],
                  tag="m_cb4")
            p.op("tensor", lambda e, tc0=tc0: e.transpose(
                ps3[0:128, 0:32], D5[:, tc0:tc0 + 128], ident[0:32, 0:32]),
                reads=["m_dt", "consts"], writes=["ps3"], tag="m_cT")
            p.op("vector", lambda e: e.tensor_scalar(
                thp[:, 0:32], ps3[:, 0:32], -1.0, None, ALU.mult),
                reads=["ps3"], writes=["m_thp"], tag="m_cTn")
            p.op("tensor", lambda e, tc0=tc0: e.transpose(
                ps3[0:128, 32:64], D3[:, tc0:tc0 + 128], ident[0:32, 0:32]),
                reads=["m_dt", "consts"], writes=["ps3"], tag="m_dT")
            p.op("vector", lambda e: e.tensor_copy(thp[:, 32:64], ps3[:, 32:64]),
                 reads=["ps3"], writes=["m_thp"], tag="m_dTev")
            for g in range(8):
                bt = 16 + g // 2
                bb = (g % 2) * 64
                p.op("tensor", lambda e, bt=bt, bb=bb, tc0=tc0: e.transpose(
                    psb7[0:128, 0:64],
                    xbc[bb:bb + 64, bt * 512 + tc0:bt * 512 + tc0 + 128],
                    identb[bb:bb + 64, bb:bb + 64]),
                    reads=["m_xbc", "consts"], writes=["ps7"], tag="m_btr")
                p.op("scalar", lambda e, g=g: e.activation(
                    btok[:, g * 64:(g + 1) * 64], psb7[:, 0:64], AF.Identity),
                    reads=["ps7"], writes=["m_btok"], tag="m_btev")
            for h in range(32):
                g = h // 4
                ht = h // 2
                hb = (h % 2) * 64
                bt = 16 + g // 2
                gb = (g % 2) * 64
                ctt = 20 + g // 2
                p.op("tensor", lambda e, bt=bt, gb=gb, ctt=ctt, tc0=tc0: e.matmul(
                    ps4[0:128, 0:128],
                    xbc[gb:gb + 64, bt * 512 + tc0:bt * 512 + tc0 + 128],
                    xbc[gb:gb + 64, ctt * 512 + tc0:ctt * 512 + tc0 + 128],
                    start=True, stop=True),
                    reads=["m_xbc"], writes=["ps4"], tag="m_G")
                p.op("tensor", lambda e, h=h: e.matmul(
                    ps5[:, 0:128], ones_row[0:1, :],
                    rowR[0:1, h * 128:(h + 1) * 128],
                    start=True, stop=True),
                    reads=["m_rowRa", "consts"], writes=["ps5"], tag="m_R")
                p.op("vector", lambda e: e.tensor_tensor(
                    wexp[:, 0:128], ps5[:, 0:128], ctx.cap("mask_sl_neg"),
                    ALU.add), reads=["ps5", "consts"], writes=["m_wexp"],
                    tag="m_mask")
                p.op("scalar", lambda e, h=h: e.activation(
                    wexp[:, 128:256], wexp[:, 0:128], AF.Exp,
                    bias=thp[:, h:h + 1]),
                    reads=["m_wexp", "m_thp"], writes=["m_wexp"], tag="m_exp")
                p.op("vector", lambda e: e.tensor_tensor(
                    wbf[:, 0:128], wexp[:, 128:256], ps4[:, 0:128], ALU.mult),
                    reads=["m_wexp", "ps4"], writes=["m_wbf"], tag="m_W")
                p.op("tensor", lambda e, ht=ht, hb=hb, tc0=tc0: e.transpose(
                    psb7[0:128, 64:128],
                    xbc[hb:hb + 64, ht * 512 + tc0:ht * 512 + tc0 + 128],
                    identb[hb:hb + 64, hb:hb + 64]),
                    reads=["m_xbc", "consts"], writes=["ps7"], tag="m_xtr")
                p.op("vector", lambda e, h=h: e.tensor_scalar(
                    xtok[:, 0:64], psb7[:, 64:128], thp[:, 32 + h:33 + h], None,
                    ALU.mult),
                    reads=["ps7", "m_thp"], writes=["m_xtok"], tag="m_xtev")
                p.op("tensor", lambda e, h=h: e.matmul(
                    ps5[0:64, 128:256], ones_row[64:65, 0:64],
                    rowR[64:65, h * 128:(h + 1) * 128],
                    start=True, stop=True),
                    reads=["m_rowRb", "consts"], writes=["ps5"], tag="m_acbc")
                p.op("vector", lambda e, gb=gb, ctt=ctt, tc0=tc0: e.tensor_tensor(
                    chat[0:64, 64:192],
                    xbc[gb:gb + 64, ctt * 512 + tc0:ctt * 512 + tc0 + 128],
                    ps5[0:64, 128:256], ALU.mult),
                    reads=["m_xbc", "ps5"], writes=["m_chat"], tag="m_chat")
                p.op("tensor", lambda e: e.matmul(
                    ps2[0:64, 0:128], xtok[:, 0:64], wbf[:, 0:128],
                    start=True, stop=False),
                    reads=["m_xtok", "m_wbf"], writes=["ps2"], tag="m_Y1")
                p.op("tensor", lambda e, h=h: e.matmul(
                    ps2[0:64, 0:128], Sb[0:64, h * 64:(h + 1) * 64],
                    chat[0:64, 64:192], start=False, stop=True),
                    reads=["m_Sb", "m_chat"], writes=["ps2"], tag="m_Y2")
                dbc = cols[hb:hb + 64, DEXP + ht:DEXP + ht + 1].broadcast_to(
                    [64, 128])
                p.op("vector", lambda e, ht=ht, hb=hb, dbc=dbc, tc0=tc0: e.tensor_tensor(
                    gsc[0:64, 0:128],
                    xbc[hb:hb + 64, ht * 512 + tc0:ht * 512 + tc0 + 128],
                    dbc, ALU.mult),
                    reads=["m_xbc", ck], writes=["m_gsc"], tag="m_dxs")
                p.op("vector", lambda e, ht=ht, hb=hb, tc0=tc0: e.tensor_tensor(
                    yq[hb:hb + 64, ht * 512 + tc0:ht * 512 + tc0 + 128],
                    gsc[0:64, 0:128], ps2[0:64, 0:128], ALU.add),
                    reads=["m_gsc", "ps2"], writes=["m_yq"], tag="m_yev")
                if hasattr(ctx, "mamba_head_hook"):
                    ctx.mamba_head_hook(q, c, h)
                p.op("vector", lambda e, g=g: e.tensor_scalar(
                    chat[:, 0:64], btok[:, g * 64:(g + 1) * 64],
                    wexp[:, 255:256], None, ALU.mult),
                    reads=["m_btok", "m_wexp"], writes=["m_chat"], tag="m_bhat")
                p.op("tensor", lambda e: e.matmul(
                    ps3[0:64, 0:64], chat[:, 0:64],
                    xtok[:, 0:64], start=True, stop=True),
                    reads=["m_chat", "m_xtok"], writes=["ps3"], tag="m_Smm")
                p.op("vector", lambda e, h=h: e.tensor_scalar(
                    S[0:64, h * 64:(h + 1) * 64], S[0:64, h * 64:(h + 1) * 64],
                    ps5[0:64, 255:256], None, ALU.mult),
                    reads=["m_S", "ps5"], writes=["m_S"], tag="m_Ssc")
                p.op("vector", lambda e, h=h: e.tensor_tensor(
                    S[0:64, h * 64:(h + 1) * 64], S[0:64, h * 64:(h + 1) * 64],
                    ps3[0:64, 0:64], ALU.add),
                    reads=["m_S", "ps3"], writes=["m_S"], tag="m_Sadd")
                p.op("vector", lambda e, h=h: e.tensor_copy(
                    Sb[0:64, h * 64:(h + 1) * 64], S[0:64, h * 64:(h + 1) * 64]),
                    reads=["m_S"], writes=["m_Sb"], tag="m_Sbf")
        if hasattr(ctx, "mamba_pre_hook"):
            ctx.mamba_pre_hook(q)
        p.dma("sync", zq[:, :].rearrange("p (m c) -> p m c", c=512),
              ZSP[:, t0:t0 + 512].rearrange("(m p) c -> p m c", p=128),
              reads=["ZSP"], writes=["m_zq"], tag="m_zrl")
        for mt in range(16):
            p.op("scalar", lambda e, mt=mt: e.activation(
                gsc[:, 0:512], zq[:, mt * 512:(mt + 1) * 512], AF.Silu),
                reads=["m_zq"], writes=["m_gsc"], tag="m_silz")
            p.op("vector", lambda e, mt=mt: e.tensor_tensor(
                yq[:, mt * 512:(mt + 1) * 512], yq[:, mt * 512:(mt + 1) * 512],
                gsc[:, 0:512], ALU.mult),
                reads=["m_gsc", "m_yq"], writes=["m_yq"], tag="m_gmul")
        for gi in range(8):
            for j in range(2):
                mt = gi * 2 + j
                p.op("scalar", lambda e, mt=mt: e.activation(
                    sq[:, :], yq[:, mt * 512:(mt + 1) * 512], AF.Square),
                    reads=["m_yq"], writes=["ln_sq"], tag="m_gq")
                p.op("tensor", lambda e, j=j: e.matmul(
                    ps2[0:1, :], ones_col[:, 0:1], sq[:, :],
                    start=(j == 0), stop=(j == 1)),
                    reads=["ln_sq", "consts"], writes=["ps2"], tag="m_gsum")
            p.op("vector", lambda e: e.tensor_scalar(
                gsc[0:1, 0:512], ps2[0:1, :], 1.0 / 256, None, ALU.mult),
                reads=["ps2"], writes=["m_gsc"], tag="m_gms")
            p.op("scalar", lambda e: e.activation(
                gsc[0:1, 0:512], gsc[0:1, 0:512], AF.Sqrt,
                bias=eps_col[0:1, 0:1]), reads=["m_gsc", "consts"],
                writes=["m_gsc"], tag="m_gsq")
            p.op("vector", lambda e: e.reciprocal(gsc[0:1, 0:512], gsc[0:1, 0:512]),
                 reads=["m_gsc"], writes=["m_gsc"], tag="m_grr")
            p.op("tensor", lambda e: e.matmul(ps2[:, :], ones_row[0:1, :],
                 gsc[0:1, 0:512], start=True, stop=True),
                 reads=["m_gsc", "consts"], writes=["ps2"], tag="m_gbc")
            for j in range(2):
                mt = gi * 2 + j
                p.op("vector", lambda e, mt=mt: e.tensor_tensor(
                    sq[:, :], yq[:, mt * 512:(mt + 1) * 512], ps2[:, :], ALU.mult),
                    reads=["m_yq", "ps2"], writes=["ln_sq"], tag="m_gn1")
                p.op("vector", lambda e, mt=mt: e.tensor_scalar(
                    yq[:, mt * 512:(mt + 1) * 512], sq[:, :],
                    cols[:, GNW + mt:GNW + mt + 1], None, ALU.mult),
                    reads=["ln_sq", ck], writes=["m_yq"], tag="m_gn2")
        for m in range(KD):
            slot = m % 2
            wkeys = ["wb0", "wb1"] if slot == 0 else ["wb2", "wb3"]
            wv = wsl[:, slot * 2048:slot * 2048 + 2048]
            p.dma("gpsimd", wv.rearrange("p (k c) -> p k c", c=128),
                  OUTPJ[:, m * 128:(m + 1) * 128].rearrange("(k p) c -> p k c", p=128),
                  writes=wkeys, tag="m_wold")
            pb = ps0 if slot == 0 else ps1
            for k in range(16):
                p.op("tensor", lambda e, k=k, wv=wv, pb=pb: e.matmul(
                    pb[:, :], wv[:, k * 128:(k + 1) * 128],
                    yq[:, k * 512:(k + 1) * 512],
                    start=(k == 0), stop=(k == 15)),
                    reads=wkeys + ["m_yq"], writes=[f"ps{slot}"],
                    tag="m_opj")
            epilogue(m, q, pb, f"ps{slot}")


def emit_deltanet(ctx, xn_name, WQ, WK, WV, WB, LNG, dst_name,
                  lngb_done=[False]):
    p = ctx.p
    identb = ctx.cap("ident_bf")
    ones_row = ctx.cap("ones_row")
    mlow = ctx.cap("mask_triu_strict")
    minc = ctx.cap("mask_triu_incl")
    eps_col = ctx.cap("eps_col")
    xn = ctx.sb[xn_name]
    psb = ctx.ps["ps7"]
    ps0, ps1, ps2, ps3, ps4, ps5 = (ctx.ps[f"ps{i}"] for i in range(6))
    wq, wk, wv = ctx.sb["d_wq"], ctx.sb["d_wk"], ctx.sb["d_wv"]
    S = ctx.sb["d_S"]
    kt, vt, qt = ctx.sb["d_kt"], ctx.sb["d_vt"], ctx.sb["d_qt"]
    ktT, qtT = ctx.sb["d_ktT"], ctx.sb["d_qtT"]
    Rb = ctx.sb["d_R"]
    Pb = ctx.sb["d_P"]
    Mb = ctx.sb["d_M"]
    osb = ctx.sb["d_osb"]
    dcol = ctx.sb["d_cols"]
    lngb = ctx.sb["d_lngb"]
    brow = ctx.sb["d_brow"]
    dst = ctx.sb[dst_name]
    for k in range(KD):
        p.dma("gpsimd", wq[:, k * 1024:(k + 1) * 1024],
              WQ[k * 128:(k + 1) * 128, :], writes=["d_wq"], tag="d_ldq")
        p.dma("gpsimd", wk[:, k * 1024:(k + 1) * 1024],
              WK[k * 128:(k + 1) * 128, :], writes=["d_wk"], tag="d_ldk")
        p.dma("gpsimd", wv[:, k * 1024:(k + 1) * 1024],
              WV[k * 128:(k + 1) * 128, :], writes=["d_wv"], tag="d_ldv")
    p.dma("gpsimd", ctx.sb["d_wb"][:, 0:8], WB[:, :], writes=["d_wb"],
          tag="d_ldb")
    p.dma("sync", ctx.sb["d_lngrow"][0:1, 0:2048], LNG[0:1, :],
          writes=["d_lngrow"], tag="d_ldg")
    for j in range(4):
        p.op("tensor", lambda e, j=j: e.matmul(
            ctx.ps["ps6"][:, :], ones_row[0:1, :],
            ctx.sb["d_lngrow"][0:1, j * 512:(j + 1) * 512],
            start=True, stop=True),
            reads=["d_lngrow", "consts"], writes=["ps6"], tag="d_lngbc")
        p.op("vector", lambda e, j=j: e.tensor_copy(
            lngb[:, j * 512:(j + 1) * 512], ctx.ps["ps6"][:, :]),
            reads=["ps6"], writes=["d_lngb"], tag="d_lngev")
    p.op("vector", lambda e: e.memset(S[:, :], 0.0), writes=["d_S"])
    nchunks = T // 128
    for ci in range(nchunks):
        t0 = ci * 128
        for which, wmat, wkey, dstb, dkey in (
                ("k", wk, "d_wk", kt, "d_kt"), ("v", wv, "d_wv", vt, "d_vt"),
                ("q", wq, "d_wq", qt, "d_qt")):
            for half in range(2):
                pb = ps0 if half == 0 else ps1
                pk = f"ps{0 if half == 0 else 1}"
                for k in range(KD):
                    p.op("tensor", lambda e, k=k, half=half, wmat=wmat, pb=pb,
                         t0=t0: e.matmul(
                             pb[:, :],
                             xn[:, k * T + t0:k * T + t0 + 128],
                             wmat[:, k * 1024 + half * 512:k * 1024 + (half + 1) * 512],
                             start=(k == 0), stop=(k == KD - 1)),
                        reads=[wkey, xn_name], writes=[pk], tag=f"d_p{which}")
            if which == "k":
                for half in range(2):
                    pb = ps0 if half == 0 else ps1
                    p.op("scalar", lambda e, half=half, pb=pb: e.activation(
                        osb[:, half * 512:(half + 1) * 512], pb[:, :],
                        AF.Identity), reads=[f"ps{half}"], writes=["d_osb"],
                        tag="d_kev")
                p.op("scalar", lambda e: e.activation(
                    ctx.sb["ln_sq"][:, 0:512], osb[:, 0:512], AF.Square),
                    reads=["d_osb"], writes=["ln_sq"], tag="d_k2a")
                p.op("vector", lambda e: e.tensor_reduce(
                    dcol[:, 0:1], ctx.sb["ln_sq"][:, 0:512], AX.X, ALU.add),
                    reads=["ln_sq"], writes=["d_cols"], tag="d_k2r")
                p.op("scalar", lambda e: e.activation(
                    ctx.sb["ln_sq"][:, 0:512], osb[:, 512:1024], AF.Square),
                    reads=["d_osb"], writes=["ln_sq"], tag="d_k2b")
                p.op("vector", lambda e: e.tensor_reduce(
                    dcol[:, 1:2], ctx.sb["ln_sq"][:, 0:512], AX.X, ALU.add),
                    reads=["ln_sq"], writes=["d_cols"], tag="d_k2r2")
                p.op("vector", lambda e: e.tensor_add(
                    dcol[:, 0:1], dcol[:, 0:1], dcol[:, 1:2]),
                    reads=["d_cols"], writes=["d_cols"], tag="d_k2s")
                p.op("scalar", lambda e: e.activation(
                    dcol[:, 1:2], dcol[:, 0:1], AF.Sqrt),
                    reads=["d_cols"], writes=["d_cols"], tag="d_ksq")
                p.op("vector", lambda e: e.tensor_scalar(
                    dcol[:, 1:2], dcol[:, 1:2], 1e-12, None, ALU.max),
                    reads=["d_cols"], writes=["d_cols"], tag="d_kmax")
                p.op("vector", lambda e: e.reciprocal(dcol[:, 2:3], dcol[:, 1:2]),
                     reads=["d_cols"], writes=["d_cols"], tag="d_krec")
                p.op("vector", lambda e: e.tensor_scalar(
                    kt[:, 0:1024], osb[:, 0:1024], dcol[:, 2:3], None, ALU.mult),
                    reads=["d_osb", "d_cols"], writes=["d_kt"], tag="d_kn")
            elif which == "v":
                for half in range(2):
                    pb = ps0 if half == 0 else ps1
                    p.op("scalar", lambda e, half=half, pb=pb: e.activation(
                        vt[:, half * 512:(half + 1) * 512], pb[:, :], AF.Silu),
                        reads=[f"ps{half}"], writes=["d_vt"], tag="d_vev")
            else:
                for half in range(2):
                    pb = ps0 if half == 0 else ps1
                    p.op("scalar", lambda e, half=half, pb=pb: e.activation(
                        qt[:, half * 512:(half + 1) * 512], pb[:, :], AF.Identity,
                        scale=0.03125), reads=[f"ps{half}"], writes=["d_qt"],
                        tag="d_qev")
        for k in range(KD):
            p.op("tensor", lambda e, k=k, t0=t0: e.matmul(
                ps2[0:1, 0:128], ctx.sb["d_wb"][:, k:k + 1],
                xn[:, k * T + t0:k * T + t0 + 128],
                start=(k == 0), stop=(k == KD - 1)),
                reads=["d_wb", xn_name], writes=["ps2"], tag="d_brow")
        p.op("scalar", lambda e: e.activation(brow[0:1, 0:128], ps2[0:1, 0:128],
             AF.Sigmoid), reads=["ps2"], writes=["d_brow"], tag="d_bsig")
        p.op("tensor", lambda e: e.matmul(
            ctx.ps["ps6"][:, 0:1], brow[0:1, 0:128], ones_row[0:1, 0:1],
            start=True, stop=True),
            reads=["d_brow", "consts"], writes=["ps6"], tag="d_bcol")
        p.op("vector", lambda e: e.tensor_copy(dcol[:, 4:5], ctx.ps["ps6"][:, 0:1]),
             reads=["ps6"], writes=["d_cols"], tag="d_bcol2")
        p.op("vector", lambda e: e.tensor_scalar(
            dcol[:, 5:6], dcol[:, 4:5], -1.0, None, ALU.mult),
            reads=["d_cols"], writes=["d_cols"], tag="d_bneg")
        for k in range(KD):
            p.op("tensor", lambda e, k=k: e.transpose(
                psb[:, 0:128], kt[:, k * 128:(k + 1) * 128], identb),
                reads=["d_kt", "consts"], writes=["ps7"], tag="d_ktr")
            p.op("scalar", lambda e, k=k: e.activation(
                ktT[:, k * 128:(k + 1) * 128], psb[:, 0:128], AF.Identity),
                reads=["ps7"], writes=["d_ktT"], tag="d_ktrev")
            p.op("tensor", lambda e, k=k: e.transpose(
                psb[:, 0:128], qt[:, k * 128:(k + 1) * 128], identb),
                reads=["d_qt", "consts"], writes=["ps7"], tag="d_qtr")
            p.op("scalar", lambda e, k=k: e.activation(
                qtT[:, k * 128:(k + 1) * 128], psb[:, 0:128], AF.Identity),
                reads=["ps7"], writes=["d_qtT"], tag="d_qtrev")
        for k in range(KD):
            p.op("tensor", lambda e, k=k: e.matmul(
                ps3[:, 0:128], ktT[:, k * 128:(k + 1) * 128],
                ktT[:, k * 128:(k + 1) * 128],
                start=(k == 0), stop=(k == KD - 1)),
                reads=["d_ktT"], writes=["ps3"], tag="d_kkt")
        p.op("vector", lambda e: e.tensor_tensor(
            osb[:, 0:128], ps3[:, 0:128], mlow, ALU.mult),
            reads=["ps3", "consts"], writes=["d_osb"], tag="d_p0a")
        p.op("vector", lambda e: e.tensor_scalar(
            Pb[:, 0:128], osb[:, 0:128], dcol[:, 5:6], None, ALU.mult),
            reads=["d_osb", "d_cols"], writes=["d_P"], tag="d_p0b")
        for dv in range(2):
            pb = ps4 if dv == 0 else ps5
            for k in range(KD):
                p.op("tensor", lambda e, k=k, dv=dv, pb=pb: e.matmul(
                    pb[:, :], ktT[:, k * 128:(k + 1) * 128],
                    S[:, k * 1024 + dv * 512:k * 1024 + (dv + 1) * 512],
                    start=(k == 0), stop=(k == KD - 1)),
                    reads=["d_ktT", "d_S"], writes=[f"ps{4 + dv}"], tag="d_ks0")
        for dv in range(2):
            pb = ps4 if dv == 0 else ps5
            p.op("vector", lambda e, dv=dv, pb=pb: e.tensor_tensor(
                Rb[:, dv * 512:(dv + 1) * 512], vt[:, dv * 512:(dv + 1) * 512],
                pb[:, :], ALU.subtract),
                reads=["d_vt", f"ps{4 + dv}"], writes=["d_R"], tag="d_R")
        for lvl in range(7):
            for dv in range(2):
                pb = ps4 if dv == 0 else ps5
                p.op("tensor", lambda e, dv=dv, pb=pb, lvl=lvl: e.matmul(
                    pb[:, :], Pb[:, 0:128], Rb[:, dv * 512:(dv + 1) * 512],
                    start=True, stop=True),
                    reads=["d_P", "d_R"], writes=[f"ps{4 + dv}"], tag="d_ap")
            for dv in range(2):
                pb = ps4 if dv == 0 else ps5
                p.op("vector", lambda e, dv=dv, pb=pb: e.tensor_tensor(
                    Rb[:, dv * 512:(dv + 1) * 512], Rb[:, dv * 512:(dv + 1) * 512],
                    pb[:, :], ALU.add),
                    reads=[f"ps{4 + dv}", "d_R"], writes=["d_R"], tag="d_apadd")
            if lvl < 6:
                p.op("tensor", lambda e: e.transpose(psb[:, 0:128], Pb[:, 0:128],
                     identb), reads=["d_P", "consts"], writes=["ps7"],
                     tag="d_ptr")
                p.op("scalar", lambda e: e.activation(
                    Pb[:, 128:256], psb[:, 0:128], AF.Identity),
                    reads=["ps7"], writes=["d_P"], tag="d_ptrev")
                p.op("tensor", lambda e: e.matmul(
                    ps3[:, 0:128], Pb[:, 128:256], Pb[:, 0:128],
                    start=True, stop=True),
                    reads=["d_P"], writes=["ps3"], tag="d_psq")
                p.op("scalar", lambda e: e.activation(
                    Pb[:, 0:128], ps3[:, 0:128], AF.Identity),
                    reads=["ps3"], writes=["d_P"], tag="d_psqev")
        for dv in range(2):
            p.op("vector", lambda e, dv=dv: e.tensor_scalar(
                Rb[:, dv * 512:(dv + 1) * 512], Rb[:, dv * 512:(dv + 1) * 512],
                dcol[:, 4:5], None, ALU.mult),
                reads=["d_R", "d_cols"], writes=["d_R"], tag="d_vadj")
        for k in range(KD):
            p.op("tensor", lambda e, k=k: e.matmul(
                ps3[:, 0:128], ktT[:, k * 128:(k + 1) * 128],
                qtT[:, k * 128:(k + 1) * 128],
                start=(k == 0), stop=(k == KD - 1)),
                reads=["d_ktT", "d_qtT"], writes=["ps3"], tag="d_kqt")
        p.op("vector", lambda e: e.tensor_tensor(
            Mb[:, 0:128], ps3[:, 0:128], minc, ALU.mult),
            reads=["ps3", "consts"], writes=["d_M"], tag="d_minc")
        for dv in range(2):
            pb = ps4 if dv == 0 else ps5
            p.op("tensor", lambda e, dv=dv, pb=pb: e.matmul(
                pb[:, :], Mb[:, 0:128], Rb[:, dv * 512:(dv + 1) * 512],
                start=True, stop=False),
                reads=["d_M", "d_R"], writes=[f"ps{4 + dv}"], tag="d_O1")
            for k in range(KD):
                p.op("tensor", lambda e, k=k, dv=dv, pb=pb: e.matmul(
                    pb[:, :], qtT[:, k * 128:(k + 1) * 128],
                    S[:, k * 1024 + dv * 512:k * 1024 + (dv + 1) * 512],
                    start=False, stop=(k == KD - 1)),
                    reads=["d_qtT", "d_S"], writes=[f"ps{4 + dv}"], tag="d_O2")
        for kk in range(KD):
            p.op("tensor", lambda e, kk=kk: e.matmul(
                ps3[:, 0:512], kt[:, kk * 128:(kk + 1) * 128], Rb[:, 0:512],
                start=True, stop=True),
                reads=["d_kt", "d_R"], writes=["ps3"], tag="d_su1")
            p.op("vector", lambda e, kk=kk: e.tensor_tensor(
                S[:, kk * 1024:kk * 1024 + 512], S[:, kk * 1024:kk * 1024 + 512],
                ps3[:, 0:512], ALU.add),
                reads=["ps3", "d_S"], writes=["d_S"], tag="d_su1a")
            p.op("tensor", lambda e, kk=kk: e.matmul(
                ps3[:, 0:512], kt[:, kk * 128:(kk + 1) * 128], Rb[:, 512:1024],
                start=True, stop=True),
                reads=["d_kt", "d_R"], writes=["ps3"], tag="d_su2")
            p.op("vector", lambda e, kk=kk: e.tensor_tensor(
                S[:, kk * 1024 + 512:(kk + 1) * 1024],
                S[:, kk * 1024 + 512:(kk + 1) * 1024],
                ps3[:, 0:512], ALU.add),
                reads=["ps3", "d_S"], writes=["d_S"], tag="d_su2a")
        for dv in range(2):
            pb = ps4 if dv == 0 else ps5
            p.op("scalar", lambda e, dv=dv, pb=pb: e.activation(
                osb[:, dv * 512:(dv + 1) * 512], pb[:, :], AF.Identity),
                reads=[f"ps{4 + dv}"], writes=["d_osb"], tag="d_oev")
        p.op("vector", lambda e: e.tensor_reduce(
            dcol[:, 0:1], osb[:, 0:1024], AX.X, ALU.add),
            reads=["d_osb"], writes=["d_cols"], tag="d_om")
        p.op("scalar", lambda e: e.activation(
            ctx.sb["ln_sq"][:, 0:512], osb[:, 0:512], AF.Square),
            reads=["d_osb"], writes=["ln_sq"], tag="d_osq1")
        p.op("vector", lambda e: e.tensor_reduce(
            dcol[:, 1:2], ctx.sb["ln_sq"][:, 0:512], AX.X, ALU.add),
            reads=["ln_sq"], writes=["d_cols"], tag="d_osr1")
        p.op("scalar", lambda e: e.activation(
            ctx.sb["ln_sq"][:, 0:512], osb[:, 512:1024], AF.Square),
            reads=["d_osb"], writes=["ln_sq"], tag="d_osq2")
        p.op("vector", lambda e: e.tensor_reduce(
            dcol[:, 2:3], ctx.sb["ln_sq"][:, 0:512], AX.X, ALU.add),
            reads=["ln_sq"], writes=["d_cols"], tag="d_osr2")
        p.op("vector", lambda e: e.tensor_add(dcol[:, 1:2], dcol[:, 1:2],
             dcol[:, 2:3]), reads=["d_cols"], writes=["d_cols"], tag="d_os")
        p.op("vector", lambda e: e.tensor_scalar(
            dcol[:, 0:1], dcol[:, 0:1], 1.0 / 1024, None, ALU.mult),
            reads=["d_cols"], writes=["d_cols"], tag="d_om2")
        p.op("vector", lambda e: e.tensor_scalar(
            dcol[:, 1:2], dcol[:, 1:2], 1.0 / 1024, None, ALU.mult),
            reads=["d_cols"], writes=["d_cols"], tag="d_ov")
        p.op("vector", lambda e: e.tensor_tensor(
            dcol[:, 3:4], dcol[:, 0:1], dcol[:, 0:1], ALU.mult),
            reads=["d_cols"], writes=["d_cols"], tag="d_om3")
        p.op("vector", lambda e: e.tensor_tensor(
            dcol[:, 1:2], dcol[:, 1:2], dcol[:, 3:4], ALU.subtract),
            reads=["d_cols"], writes=["d_cols"], tag="d_ov2")
        p.op("scalar", lambda e: e.activation(
            dcol[:, 1:2], dcol[:, 1:2], AF.Sqrt, bias=eps_col[:, 0:1]),
            reads=["d_cols", "consts"], writes=["d_cols"], tag="d_ost")
        p.op("vector", lambda e: e.reciprocal(dcol[:, 1:2], dcol[:, 1:2]),
             reads=["d_cols"], writes=["d_cols"], tag="d_orc")
        p.op("vector", lambda e: e.tensor_tensor(
            dcol[:, 3:4], dcol[:, 0:1], dcol[:, 1:2], ALU.mult),
            reads=["d_cols"], writes=["d_cols"], tag="d_onm")
        p.op("vector", lambda e: e.tensor_scalar(
            dcol[:, 3:4], dcol[:, 3:4], -1.0, None, ALU.mult),
            reads=["d_cols"], writes=["d_cols"], tag="d_onm2")
        p.op("vector", lambda e: e.tensor_scalar(
            osb[:, 0:1024], osb[:, 0:1024], dcol[:, 3:4], dcol[:, 1:2],
            ALU.bypass, ALU.mult) if False else e.tensor_scalar(
            osb[:, 0:1024], osb[:, 0:1024], dcol[:, 1:2], dcol[:, 3:4],
            ALU.mult, ALU.add),
            reads=["d_cols", "d_osb"], writes=["d_osb"], tag="d_onorm")
        p.op("vector", lambda e: e.tensor_tensor(
            osb[:, 0:1024], osb[:, 0:1024], lngb[:, 0:1024], ALU.mult),
            reads=["d_osb", "d_lngb"], writes=["d_osb"], tag="d_og")
        p.op("vector", lambda e: e.tensor_tensor(
            osb[:, 0:1024], osb[:, 0:1024], lngb[:, 1024:2048], ALU.add),
            reads=["d_osb", "d_lngb"], writes=["d_osb"], tag="d_ob")
        p.op("vector", lambda e: e.tensor_copy(vt[:, 0:1024], osb[:, 0:1024]),
             reads=["d_osb"], writes=["d_vt"], tag="d_ocast")
        for k in range(KD):
            p.op("tensor", lambda e, k=k: e.transpose(
                psb[:, 0:128], vt[:, k * 128:(k + 1) * 128], identb),
                reads=["d_vt", "consts"], writes=["ps7"], tag="d_otr")
            p.op("vector", lambda e, k=k, t0=t0: e.tensor_tensor(
                dst[:, k * T + t0:k * T + t0 + 128],
                dst[:, k * T + t0:k * T + t0 + 128],
                psb[:, 0:128], ALU.add),
                reads=["ps7", dst_name], writes=[dst_name], tag="d_oadd")


def emit_moe(ctx, xn_name, x_name, mrow_ap, rrow_ap, spec):
    p = ctx.p
    E, K, CAP = spec["E"], spec["K"], spec["CAP"]
    Eloc = E // 8
    kind = spec["kind"]
    NT = 16
    A = T * K
    ident = ctx.cap("ident")
    identb = ctx.cap("ident_bf")
    ones_row = ctx.cap("ones_row")
    ones_col = ctx.cap("ones_col")
    psb = ctx.ps["ps7"]
    ps0, ps1, ps2, ps3, ps4, ps5, ps6 = (ctx.ps[f"ps{i}"] for i in range(7))
    xn = ctx.sb[xn_name]
    RW = spec["RW"]
    sc = ctx.sb["e_sc"]
    lg = ctx.sb["e_lg"]
    easg = ctx.sb["e_easg"]
    wasg = ctx.sb["e_wasg"]
    rwsb = ctx.sb["e_rw"]
    iota_e = ctx.cap("iota_row")
    XP = spec["XP"]
    xrt = ctx.sb["e_xrt"]
    for k in range(KD):
        p.dma("sync", rwsb[:, k * E:(k + 1) * E], RW[k * 128:(k + 1) * 128, :],
              writes=["e_rw"], tag="e_ldrw")
    for k in range(KD):
        p.op("tensor", lambda e, k=k: e.matmul(
            ps6[0:1, 0:E], ones_col[:, 0:1], rwsb[:, k * E:(k + 1) * E],
            start=(k == 0), stop=(k == KD - 1)),
            reads=["e_rw", "consts"], writes=["ps6"], tag="e_csr")
    p.op("vector", lambda e: e.tensor_copy(sc[0:1, 0:E], ps6[0:1, 0:E]),
         reads=["ps6"], writes=["e_sc"], tag="e_csr2")
    for tl in range(NT):
        for k in range(KD):
            sl = k % 2
            p.dma("sync", xrt[:, sl * 128:(sl + 1) * 128],
                  XP[k * 128:(k + 1) * 128, tl * 128:(tl + 1) * 128],
                  writes=[f"e_xrt{sl}"], tag="e_xrl")
            p.op("tensor", lambda e, tl=tl, k=k, sl=sl: e.matmul(
                ps0[:, 0:E], xrt[:, sl * 128:(sl + 1) * 128],
                rwsb[:, k * E:(k + 1) * E],
                start=(k == 0), stop=(k == KD - 1)),
            reads=["e_rw", f"e_xrt{sl}"], writes=["ps0"], tag="e_lg1")
        p.op("vector", lambda e, tl=tl: e.tensor_tensor(
            sc[0:1, 16 * E:16 * E + 128], mrow_ap[0:1, tl * 128:(tl + 1) * 128],
            rrow_ap[0:1, tl * 128:(tl + 1) * 128], ALU.mult),
            reads=["e_mr"], writes=["e_sc"], tag="e_mr")
        p.op("tensor", lambda e, tl=tl: e.matmul(
            ps1[:, 0:E], sc[0:1, 16 * E:16 * E + 128], sc[0:1, 0:E],
            start=True, stop=True),
            reads=["e_sc"], writes=["ps1"], tag="e_out")
        p.op("tensor", lambda e, tl=tl: e.transpose(
            ps2[:, 0:1], rrow_ap[0:1, tl * 128:(tl + 1) * 128], ident[0:1, 0:1]),
            reads=["e_mr", "consts"], writes=["ps2"], tag="e_rT")
        p.op("vector", lambda e: e.tensor_copy(sc[:, 2 * E:2 * E + 1], ps2[:, 0:1]),
             reads=["ps2"], writes=["e_sc"], tag="e_rT2")
        p.op("vector", lambda e, tl=tl: e.tensor_scalar(
            lg[:, tl * E:(tl + 1) * E], ps0[:, 0:E], sc[:, 2 * E:2 * E + 1],
            None, ALU.mult), reads=["ps0", "e_sc"], writes=["e_lg"], tag="e_lgs")
        p.op("vector", lambda e, tl=tl: e.tensor_tensor(
            lg[:, tl * E:(tl + 1) * E], lg[:, tl * E:(tl + 1) * E], ps1[:, 0:E],
            ALU.subtract), reads=["ps1", "e_lg"], writes=["e_lg"], tag="e_lgf")
    if kind == "l0":
        p.op("scalar", lambda e: e.activation(lg[:, 0:NT * E], lg[:, 0:NT * E],
             AF.Exp), reads=["e_lg"], writes=["e_lg"], tag="e_sm1")
        p.op("vector", lambda e: e.tensor_reduce(
            sc[:, 0:NT], lg[:, 0:NT * E].rearrange("p (t e) -> p t e", e=E),
            AX.X, ALU.add), reads=["e_lg"], writes=["e_sc"], tag="e_sm2")
        p.op("vector", lambda e: e.reciprocal(sc[:, NT:2 * NT], sc[:, 0:NT]),
             reads=["e_sc"], writes=["e_sc"], tag="e_sm3")
        p.op("vector", lambda e: e.tensor_tensor(
            lg[:, 0:NT * E].rearrange("p (t e) -> p t e", e=E),
            lg[:, 0:NT * E].rearrange("p (t e) -> p t e", e=E),
            sc[:, NT:2 * NT].unsqueeze(-1).broadcast_to([128, NT, E]), ALU.mult),
            reads=["e_sc", "e_lg"], writes=["e_lg"], tag="e_sm4")
        sel_off = 0
    else:
        p.op("scalar", lambda e: e.activation(lg[:, 0:NT * E], lg[:, 0:NT * E],
             AF.Sigmoid), reads=["e_lg"], writes=["e_lg"], tag="e_sg")
        p.dma("sync", sc[0:1, 3 * E:4 * E], spec["BIASR"][0:1, :],
              writes=["e_sc"], tag="e_ldbias")
        p.op("tensor", lambda e: e.matmul(ps2[:, 0:E], ones_row[0:1, :],
             sc[0:1, 3 * E:4 * E], start=True, stop=True),
             reads=["e_sc", "consts"], writes=["ps2"], tag="e_bb")
        p.op("vector", lambda e: e.tensor_copy(ctx.sb["e_sel"][:, NT * E:NT * E + E],
             ps2[:, 0:E]), reads=["ps2"], writes=["e_sel"], tag="e_bb2")
        for tl in range(NT):
            p.op("vector", lambda e, tl=tl: e.tensor_tensor(
                ctx.sb["e_sel"][:, tl * E:(tl + 1) * E], lg[:, tl * E:(tl + 1) * E],
                ctx.sb["e_sel"][:, NT * E:NT * E + E], ALU.add),
                reads=["e_lg", "e_sel"], writes=["e_sel"], tag="e_selb")
        sel_off = None
    selbuf = lg if kind == "l0" else ctx.sb["e_sel"]
    selkey = "e_lg" if kind == "l0" else "e_sel"
    iota_d = ctx.cap("iota_d")
    p.op("vector", lambda e: e.tensor_tensor(
        selbuf[:, 0:NT * E].rearrange("p (t e) -> p t e", e=E),
        selbuf[:, 0:NT * E].rearrange("p (t e) -> p t e", e=E),
        iota_d[:, 0:E].unsqueeze(1).broadcast_to([128, NT, E]), ALU.subtract),
        reads=[selkey, "consts"], writes=[selkey], tag="e_tb")
    for j in range(K):
        p.op("vector", lambda e: e.tensor_reduce(
            sc[:, 0:NT], selbuf[:, 0:NT * E].rearrange("p (t e) -> p t e", e=E),
            AX.X, ALU.max), reads=[selkey], writes=["e_sc"], tag="e_tmax")
        p.op("vector", lambda e: e.tensor_tensor(
            ctx.sb["e_eq"][:, 0:NT * E].rearrange("p (t e) -> p t e", e=E),
            selbuf[:, 0:NT * E].rearrange("p (t e) -> p t e", e=E),
            sc[:, 0:NT].unsqueeze(-1).broadcast_to([128, NT, E]), ALU.is_equal),
            reads=[selkey, "e_sc"], writes=["e_eq"], tag="e_eq")
        p.op("vector", lambda e: e.tensor_tensor(
            ctx.sb["e_eq2"][:, 0:NT * E].rearrange("p (t e) -> p t e", e=E),
            ctx.sb["e_eq"][:, 0:NT * E].rearrange("p (t e) -> p t e", e=E),
            iota_e[:, 0:E].unsqueeze(1).broadcast_to([128, NT, E]),
            ALU.mult), reads=["e_eq", "consts"], writes=["e_eq2"], tag="e_ix1")
        p.op("vector", lambda e, j=j: e.tensor_reduce(
            easg[:, j * NT:(j + 1) * NT],
            ctx.sb["e_eq2"][:, 0:NT * E].rearrange("p (t e) -> p t e", e=E),
            AX.X, ALU.add), reads=["e_eq2"], writes=["e_easg"], tag="e_ix2")
        p.op("vector", lambda e: e.tensor_tensor(
            ctx.sb["e_eq2"][:, 0:NT * E], ctx.sb["e_eq"][:, 0:NT * E],
            lg[:, 0:NT * E], ALU.mult),
            reads=["e_eq", "e_lg"], writes=["e_eq2"], tag="e_w1")
        p.op("vector", lambda e, j=j: e.tensor_reduce(
            wasg[:, j * NT:(j + 1) * NT],
            ctx.sb["e_eq2"][:, 0:NT * E].rearrange("p (t e) -> p t e", e=E),
            AX.X, ALU.add), reads=["e_eq2"], writes=["e_wasg"], tag="e_w2")
        if j < K - 1:
            p.op("vector", lambda e: e.tensor_scalar(
                ctx.sb["e_eq"][:, 0:NT * E], ctx.sb["e_eq"][:, 0:NT * E],
                1e30, None, ALU.mult), reads=["e_eq"], writes=["e_eq"],
                tag="e_big")
            p.op("vector", lambda e: e.tensor_tensor(
                selbuf[:, 0:NT * E], selbuf[:, 0:NT * E],
                ctx.sb["e_eq"][:, 0:NT * E], ALU.subtract),
                reads=["e_eq", selkey], writes=[selkey], tag="e_sub")
    if kind == "l0":
        p.op("vector", lambda e: e.tensor_tensor(
            sc[:, 0:NT], wasg[:, 0:NT], wasg[:, NT:2 * NT], ALU.add),
            reads=["e_wasg"], writes=["e_sc"], tag="e_gn1")
        p.op("vector", lambda e: e.reciprocal(sc[:, NT:2 * NT], sc[:, 0:NT]),
             reads=["e_sc"], writes=["e_sc"], tag="e_gn2")
        for j in range(2):
            p.op("vector", lambda e, j=j: e.tensor_tensor(
                wasg[:, j * NT:(j + 1) * NT], wasg[:, j * NT:(j + 1) * NT],
                sc[:, NT:2 * NT], ALU.mult),
                reads=["e_sc", "e_wasg"], writes=["e_wasg"], tag="e_gn3")
    else:
        p.op("vector", lambda e: e.tensor_tensor(
            sc[:, 0:NT], wasg[:, 0:NT], wasg[:, NT:2 * NT], ALU.add),
            reads=["e_wasg"], writes=["e_sc"], tag="e_wn1")
        for j in range(2, K):
            p.op("vector", lambda e, j=j: e.tensor_tensor(
                sc[:, 0:NT], sc[:, 0:NT], wasg[:, j * NT:(j + 1) * NT],
                ALU.add), reads=["e_wasg", "e_sc"], writes=["e_sc"],
                tag="e_wn1a")
        p.op("vector", lambda e: e.tensor_scalar(
            sc[:, 0:NT], sc[:, 0:NT], 1e-20, None, ALU.add),
            reads=["e_sc"], writes=["e_sc"], tag="e_wn1b")
        p.op("vector", lambda e: e.reciprocal(sc[:, NT:2 * NT], sc[:, 0:NT]),
             reads=["e_sc"], writes=["e_sc"], tag="e_wn2")
        p.op("vector", lambda e: e.tensor_scalar(
            sc[:, NT:2 * NT], sc[:, NT:2 * NT], 2.5, None, ALU.mult),
            reads=["e_sc"], writes=["e_sc"], tag="e_wn3")
        for j in range(K):
            p.op("vector", lambda e, j=j: e.tensor_tensor(
                wasg[:, j * NT:(j + 1) * NT], wasg[:, j * NT:(j + 1) * NT],
                sc[:, NT:2 * NT], ALU.mult),
                reads=["e_sc", "e_wasg"], writes=["e_wasg"], tag="e_wn4")
    return dict(easg=easg, wasg=wasg)


def emit_moe_exec(ctx, xn_name, x_name, spec):
    p = ctx.p
    E, K, CAP = spec["E"], spec["K"], spec["CAP"]
    Eloc = E // 8
    kind = spec["kind"]
    NT = 16
    A = T * K
    NAT = A // 128
    ident = ctx.cap("ident")
    identb = ctx.cap("ident_bf")
    ones_row = ctx.cap("ones_row")
    ones_col = ctx.cap("ones_col")
    iota_e = ctx.cap("iota_row")
    triu_s = ctx.cap("mask_triu_strict")
    psb = ctx.ps["ps7"]
    ps0, ps1, ps2, ps3, ps4, ps5, ps6 = (ctx.ps[f"ps{i}"] for i in range(7))
    xn = ctx.sb[xn_name]
    easg, wasg = ctx.sb["e_easg"], ctx.sb["e_wasg"]
    sc = ctx.sb["e_sc"]
    ebf = ctx.sb["e_bf"]
    easgA = ctx.sb["e_easgA"]
    oh4 = ctx.sb["e_oh4"]
    slots = ctx.sb["e_slots"]
    wfin = ctx.sb["e_wfin"]
    rows2 = ctx.sb["e_rows"]
    pc = ctx.sb["e_pc"]
    ncg = 2 if kind == "l0" else 1
    CNT_W = E * ncg
    def _bar(tag, reads, writes):
        p.op("gpsimd", lambda e: e.collective_compute(
            "AllReduce", ALU.add, replica_groups=[list(range(8))],
            ins=[spec["BARI"][:, :].opt()], outs=[spec["BARO"][:, :].opt()]),
            reads=["BARI"] + reads, writes=writes, kind="coll", tag=tag)
    if kind == "l2":
        for j in range(K):
            p.op("vector", lambda e, j=j: e.tensor_copy(
                ebf[:, 0:NT * K].rearrange("p (t j) -> p t j", j=K)[:, :, j],
                easg[:, j * NT:(j + 1) * NT]),
                reads=["e_easg"], writes=["e_bf"], tag="e_bf1")
            p.op("vector", lambda e, j=j: e.tensor_copy(
                ebf[:, NT * K:2 * NT * K].rearrange("p (t j) -> p t j", j=K)[:, :, j],
                wasg[:, j * NT:(j + 1) * NT]),
                reads=["e_wasg"], writes=["e_bf"], tag="e_bf2")
        p.dma("sync", spec["EADR"][0:NT, :, :].rearrange("t p j -> p t j"),
              ebf[:, 0:NT * K].rearrange("p (t j) -> p t j", j=K),
              reads=["e_bf"], writes=["EADR"], tag="e_bo1")
        p.dma("sync", spec["EADR"][NT:2 * NT, :, :].rearrange("t p j -> p t j"),
              ebf[:, NT * K:2 * NT * K].rearrange("p (t j) -> p t j", j=K),
              reads=["e_bf"], writes=["EADR"], tag="e_bo2")
        EADRf = spec["EADR"].rearrange("t p j -> (t p j)")
        p.dma("scalar", easgA[:, 0:NAT],
              EADRf[0:A].rearrange("(a q) -> a q", q=128), transpose=True,
              reads=["EADR"], writes=["e_easgA"], tag="e_bt1")
        p.dma("scalar", easgA[:, NAT:2 * NAT],
              EADRf[A:2 * A].rearrange("(a q) -> a q", q=128), transpose=True,
              reads=["EADR"], writes=["e_easgA"], tag="e_bt2")
    else:
        p.op("vector", lambda e: e.tensor_copy(easgA[:, 0:NAT], easg[:, 0:NAT]),
             reads=["e_easg"], writes=["e_easgA"], tag="e_cA")
        p.op("vector", lambda e: e.tensor_copy(easgA[:, NAT:2 * NAT],
             wasg[:, 0:NAT]), reads=["e_wasg"], writes=["e_easgA"], tag="e_cA2")
    for tl in range(NT):
        for k in range(KD):
            p.op("tensor", lambda e, tl=tl, k=k: e.transpose(
                psb[:, k * 128:(k + 1) * 128],
                xn[:, k * T + tl * 128:k * T + (tl + 1) * 128], identb),
                reads=[xn_name, "consts"], writes=["ps7"], tag="e_xtr")
        p.op("scalar", lambda e: e.activation(
            ctx.sb["e_xtk"][:, 0:1024], psb[:, 0:1024], AF.Identity),
            reads=["ps7"], writes=["e_xtk"], tag="e_xtev")
        p.dma("sync", spec["XAGI"][tl * 128:(tl + 1) * 128, :],
              ctx.sb["e_xtk"][:, 0:1024], reads=["e_xtk"], writes=["XAGI"],
              tag="e_xout")
    _bar("e_barx0", ["XAGI"], ["XAGI"])
    p.op("gpsimd", lambda e: e.collective_compute(
        "AllGather", ALU.bypass, replica_groups=[list(range(8))],
        ins=[spec["XAGI"][:, :].opt()], outs=[spec["XALL"][:, :].opt()]),
        reads=["XAGI"], writes=["XALL"], kind="coll", tag="e_agx")
    TPG = NAT // ncg
    for g in range(ncg):
        for bi in range(TPG // 4):
            t0 = g * TPG + bi * 4
            p.op("vector", lambda e, t0=t0: e.tensor_tensor(
                oh4[:, 0:4 * E].rearrange("p (b e) -> p b e", e=E),
                easgA[:, t0:t0 + 4].unsqueeze(-1).broadcast_to([128, 4, E]),
                iota_e[:, 0:E].unsqueeze(1).broadcast_to([128, 4, E]),
                ALU.is_equal),
                reads=["e_easgA", "consts"], writes=["e_oh4"], tag="e_oh")
            p.op("tensor", lambda e, bi=bi: e.matmul(
                ps6[0:1, 0:4 * E], ones_col[:, 0:1], oh4[:, 0:4 * E],
                start=(bi == 0), stop=(bi == TPG // 4 - 1)),
                reads=["e_oh4", "consts"], writes=["ps6"], tag="e_cnt")
        p.op("vector", lambda e, g=g: e.tensor_reduce(
            sc[0:1, 8 * E + g * E:8 * E + (g + 1) * E],
            ps6[0:1, 0:4 * E].rearrange("p (b e) -> p e b", e=E), AX.X, ALU.add),
            reads=["ps6"], writes=["e_sc"], tag="e_cnt2")
    p.dma("sync", spec["CNTI"][0:1, :], sc[0:1, 8 * E:8 * E + CNT_W],
          reads=["e_sc"], writes=["CNTI"], tag="e_cnti")
    _bar("e_barc0", ["CNTI"], ["CNTI"])
    p.op("gpsimd", lambda e: e.collective_compute(
        "AllGather", ALU.bypass, replica_groups=[list(range(8))],
        ins=[spec["CNTI"][:, :].opt()], outs=[spec["CNTO"][:, :].opt()]),
        reads=["CNTI"], writes=["CNTO"], kind="coll", tag="e_agc")
    _bar("e_barc1", ["CNTO"], ["CNTO"])
    cnts = ctx.sb["e_cnts"]
    p.dma("sync", cnts[0:8, 0:CNT_W], spec["CNTO"][:, :], reads=["CNTO"],
          writes=["e_cnts"], tag="e_ldcnt")
    rmask = ctx.sb["e_rmask"]
    p.op("tensor", lambda e: e.matmul(ps6[0:1, 0:CNT_W], rmask[0:8, 0:1],
         cnts[0:8, 0:CNT_W], start=True, stop=True),
         reads=["e_rmask", "e_cnts"], writes=["ps6"], tag="e_strt")
    p.op("vector", lambda e: e.tensor_copy(sc[0:1, 10 * E:10 * E + CNT_W],
         ps6[0:1, 0:CNT_W]), reads=["ps6"], writes=["e_sc"], tag="e_strt2")
    if kind == "l0":
        p.op("tensor", lambda e: e.matmul(ps6[0:1, 0:E], ones_col[0:8, 0:1],
             cnts[0:8, 0:E], start=True, stop=True),
             reads=["e_cnts", "consts"], writes=["ps6"], tag="e_tj0")
        p.op("vector", lambda e: e.tensor_copy(sc[0:1, 12 * E:13 * E],
             ps6[0:1, 0:E]), reads=["ps6"], writes=["e_sc"], tag="e_tj0c")
        p.op("vector", lambda e: e.tensor_tensor(
            sc[0:1, 11 * E:12 * E], sc[0:1, 11 * E:12 * E],
            sc[0:1, 12 * E:13 * E], ALU.add), reads=["e_sc"], writes=["e_sc"],
            tag="e_tj0a")
    for g in range(ncg):
        for b in range(4):
            p.op("vector", lambda e, g=g, b=b: e.tensor_copy(
                rows2[0:1, g * 4 * E + b * E:g * 4 * E + (b + 1) * E],
                sc[0:1, 10 * E + g * E:10 * E + (g + 1) * E]),
                reads=["e_sc"], writes=["e_rows"], tag="e_gsc")
    nmapz = (E * CAP + 128) // 128
    p.op("vector", lambda e: e.memset(ctx.sb["e_xtk"][:, 0:nmapz], 0.0),
         writes=["e_xtk"], tag="e_ctz0")
    p.dma("gpsimd", spec["MAPI"][0:E * CAP + 128, :].rearrange("(p a) c -> p (a c)", p=128),
          ctx.sb["e_xtk"][:, 0:nmapz], reads=["e_xtk"],
          writes=["MAPI"], tag="e_ctz")
    for g in range(ncg):
        p.op("vector", lambda e, g=g: e.memset(rows2[0:1, 8 * E:12 * E], 0.0),
             writes=["e_rows"], tag="e_car0")
        p.op("tensor", lambda e, g=g: e.matmul(
            ps5[:, 0:4 * E], ones_row[0:1, :],
            rows2[0:1, g * 4 * E:(g + 1) * 4 * E], start=True, stop=True),
            reads=["e_rows", "consts"], writes=["ps5"], tag="e_sbc")
        for bi in range(TPG // 4):
            t0 = g * TPG + bi * 4
            p.op("vector", lambda e, t0=t0: e.tensor_tensor(
                oh4[:, 0:4 * E].rearrange("p (b e) -> p b e", e=E),
                easgA[:, t0:t0 + 4].unsqueeze(-1).broadcast_to([128, 4, E]),
                iota_e[:, 0:E].unsqueeze(1).broadcast_to([128, 4, E]),
                ALU.is_equal),
                reads=["e_easgA", "consts"], writes=["e_oh4"], tag="e_oh2")
            p.op("tensor", lambda e: e.matmul(ps6[0:1, 0:4 * E], ones_col[:, 0:1],
                 oh4[:, 0:4 * E], start=True, stop=True),
                 reads=["e_oh4", "consts"], writes=["ps6"], tag="e_tot")
            p.op("vector", lambda e: e.tensor_copy(rows2[0:1, 8 * E:9 * E],
                 rows2[0:1, 12 * E:13 * E]) if False else e.tensor_copy(
                 rows2[0:1, 12 * E:13 * E], rows2[0:1, 8 * E:9 * E]),
                 reads=["e_rows"], writes=["e_rows"], tag="e_cc0")
            for b in range(3):
                p.op("vector", lambda e, b=b: e.tensor_tensor(
                    rows2[0:1, (13 + b) * E:(14 + b) * E],
                    rows2[0:1, (12 + b) * E:(13 + b) * E],
                    ps6[0:1, b * E:(b + 1) * E], ALU.add),
                    reads=["ps6", "e_rows"], writes=["e_rows"], tag="e_cc")
            p.op("vector", lambda e: e.tensor_tensor(
                rows2[0:1, 8 * E:9 * E], rows2[0:1, 15 * E:16 * E],
                ps6[0:1, 3 * E:4 * E], ALU.add),
                reads=["ps6", "e_rows"], writes=["e_rows"], tag="e_cup")
            p.op("tensor", lambda e: e.matmul(ps4[:, 0:4 * E], triu_s,
                 oh4[:, 0:4 * E], start=True, stop=False),
                 reads=["e_oh4", "consts"], writes=["ps4"], tag="e_pos1")
            p.op("tensor", lambda e: e.matmul(ps4[:, 0:4 * E], ones_row[0:1, :],
                 rows2[0:1, 12 * E:16 * E], start=False, stop=True),
                 reads=["e_rows", "consts"], writes=["ps4"], tag="e_pos2")
            p.op("vector", lambda e: e.tensor_tensor(
                ctx.sb["e_pp"][:, 0:4 * E], ps4[:, 0:4 * E], oh4[:, 0:4 * E],
                ALU.mult), reads=["ps4", "e_oh4"], writes=["e_pp"], tag="e_ps1")
            p.op("vector", lambda e: e.tensor_reduce(
                pc[:, 0:4],
                ctx.sb["e_pp"][:, 0:4 * E].rearrange("p (b e) -> p b e", e=E),
                AX.X, ALU.add), reads=["e_pp"], writes=["e_pc"], tag="e_ps2")
            p.op("vector", lambda e: e.tensor_tensor(
                ctx.sb["e_pp"][:, 0:4 * E], ps5[:, 0:4 * E], oh4[:, 0:4 * E],
                ALU.mult), reads=["ps5", "e_oh4"], writes=["e_pp"], tag="e_ss1")
            p.op("vector", lambda e: e.tensor_reduce(
                pc[:, 4:8],
                ctx.sb["e_pp"][:, 0:4 * E].rearrange("p (b e) -> p b e", e=E),
                AX.X, ALU.add), reads=["e_pp"], writes=["e_pc"], tag="e_ss2")
            p.op("vector", lambda e: e.tensor_tensor(
                pc[:, 8:12], pc[:, 0:4], pc[:, 4:8], ALU.add),
                reads=["e_pc"], writes=["e_pc"], tag="e_pg")
            p.op("vector", lambda e: e.tensor_scalar(
                pc[:, 12:16], pc[:, 8:12], float(CAP), None, ALU.is_lt),
                reads=["e_pc"], writes=["e_pc"], tag="e_keep")
            p.op("vector", lambda e, t0=t0: e.tensor_scalar(
                pc[:, 16:20], easgA[:, t0:t0 + 4], float(CAP), None, ALU.mult),
                reads=["e_easgA", "e_pc"], writes=["e_pc"], tag="e_sl1")
            p.op("vector", lambda e: e.tensor_tensor(
                pc[:, 16:20], pc[:, 16:20], pc[:, 8:12], ALU.add),
                reads=["e_pc"], writes=["e_pc"], tag="e_sl2")
            p.op("vector", lambda e: e.tensor_scalar(
                pc[:, 20:24], pc[:, 12:16], 1e9, -1e9, ALU.mult, ALU.add),
                reads=["e_pc"], writes=["e_pc"], tag="e_sl3")
            p.op("vector", lambda e: e.tensor_tensor(
                pc[:, 16:20], pc[:, 16:20], pc[:, 20:24], ALU.subtract),
                reads=["e_pc"], writes=["e_pc"], tag="e_sl4")
            p.op("vector", lambda e: e.tensor_scalar(
                pc[:, 16:20], pc[:, 16:20], float(E * CAP), None, ALU.min),
                reads=["e_pc"], writes=["e_pc"], tag="e_sl4b")
            p.op("vector", lambda e, t0=t0: e.tensor_copy(
                slots[:, t0:t0 + 4], pc[:, 16:20]),
                reads=["e_pc"], writes=["e_slots"], tag="e_sl5")
            p.op("vector", lambda e, t0=t0: e.tensor_scalar(
                ctx.sb["e_slotg"][:, t0:t0 + 4], pc[:, 16:20],
                float(E * CAP - 1), None, ALU.min),
                reads=["e_pc"], writes=["e_slotg"], tag="e_sl6")
            p.op("vector", lambda e, t0=t0: e.tensor_tensor(
                wfin[:, t0:t0 + 4], easgA[:, NAT + t0:NAT + t0 + 4],
                pc[:, 12:16], ALU.mult),
                reads=["e_easgA", "e_pc"], writes=["e_wfin"], tag="e_wf")
            if kind == "l2":
                p.op("vector", lambda e, bi2=bi: e.tensor_scalar(
                    pc[:, 28:32], ctx.sb["e_tokpat"][:, 0:4], float(bi2 * 64),
                    None, ALU.add),
                    reads=["e_tokpat", "e_pc"], writes=["e_pc"], tag="e_tid")
            else:
                ti = (t0 % TPG)
                p.op("vector", lambda e, ti=ti: e.tensor_scalar(
                    pc[:, 28:32], ctx.sb["e_tokpat"][:, 4:8],
                    float(ti * 128), None, ALU.add),
                    reads=["e_tokpat", "e_pc"], writes=["e_pc"], tag="e_tid0")
            for b in range(4):
                p.op("gpsimd", lambda e, b=b, t0=t0: e.indirect_dma_start(
                    out=spec["MAPI"][:, :],
                    out_offset=IOffs(slots[:, t0 + b:t0 + b + 1], 0),
                    in_=pc[:, 28 + b:29 + b], in_offset=None),
                    reads=["e_pc", "e_slots"], writes=["MAPI"], kind="dma",
                    tag="e_ctsc")
    _bar("e_barm0", ["MAPI"], ["MAPI"])
    p.op("gpsimd", lambda e: e.collective_compute(
        "ReduceScatter", ALU.add, replica_groups=[list(range(8))],
        ins=[spec["MAPI"][0:E * CAP, :].opt()], outs=[spec["MAPD"][:, :].opt()]),
        reads=["MAPI"], writes=["MAPD"], kind="coll", tag="e_rsm")
    p.op("gpsimd", lambda e: e.collective_compute(
        "AllReduce", ALU.add, replica_groups=[list(range(8))],
        ins=[spec["BARI"][:, :].opt()], outs=[spec["BARO"][:, :].opt()]),
        reads=["BARI"], writes=["XALL", "MAPD"], kind="coll", tag="e_bar1")
    if spec.get("delay1", 0):
        for dd in range(spec["delay1"]):
            p.op("vector", lambda e: e.memset(ctx.sb["e_hsb"][:, 0:8192], 0.0),
                 reads=["XALL", "MAPD"], writes=["XALL", "MAPD", "e_hsb"],
                 tag="e_dly1")
    NCAP = CAP // 128
    mapc = ctx.sb["e_mapc"]
    for le in range(Eloc):
        p.dma("sync", ctx.sb["e_mapf"][0:NCAP, 0:128],
              spec["MAPD"][le * CAP:(le + 1) * CAP, 0:1]
              .rearrange("(a q) c -> a (q c)", q=128),
              reads=["MAPD"], writes=["e_mapf"], tag="e_mld")
        p.op("tensor", lambda e: e.transpose(ps4[:, 0:NCAP],
             ctx.sb["e_mapf"][0:NCAP, 0:128], ident[0:NCAP, 0:NCAP]),
             reads=["e_mapf", "consts"], writes=["ps4"], tag="e_mtr")
        p.op("vector", lambda e: e.tensor_copy(mapc[:, 0:NCAP], ps4[:, 0:NCAP]),
             reads=["ps4"], writes=["e_mapc"], tag="e_mcast")
        spec["ffn"](ctx, le, None)
    _bar("e_bare0", ["EOI"], ["EOI"])
    p.op("gpsimd", lambda e: e.collective_compute(
        "AllGather", ALU.bypass, replica_groups=[list(range(8))],
        ins=[spec["EOI"][:, :].opt()], outs=[spec["EOALL"][:, :].opt()]),
        reads=["EOI"], writes=["EOALL"], kind="coll", tag="e_ageo")
    if "mid" in spec:
        spec["mid"]()
    p.op("gpsimd", lambda e: e.collective_compute(
        "AllReduce", ALU.add, replica_groups=[list(range(8))],
        ins=[spec["BARI"][:, :].opt()], outs=[spec["BARO"][:, :].opt()]),
        reads=["BARI", "EOALL"], writes=["EOALL"], kind="coll", tag="e_bar2")
    if spec.get("delay", 0):
        for dd in range(spec["delay"]):
            p.op("vector", lambda e: e.memset(ctx.sb["e_hsb"][:, 0:8192], 0.0),
                 reads=["EOALL"], writes=["EOALL", "e_hsb"], tag="e_dly")
    gkbufs = (ctx.sb["e_dtok"], ctx.sb["e_dspf"])
    gw = ctx.sb.get("e_gw")
    stg = ctx.sb.get("e_stg")
    for b4 in range(NAT // 4):
        for bb in range(4):
            i = b4 * 4 + bb
            gk = gkbufs[i % 2]
            gkkey = "e_dtok" if i % 2 == 0 else "e_dspf"
            p.op("gpsimd", lambda e, i=i, gk=gk: e.indirect_dma_start(
                out=gk[:, 0:1024], out_offset=None,
                in_=spec["EOALL"][:, :],
                in_offset=IOffs(ctx.sb["e_slotg"][:, i:i + 1], 0)),
                reads=["EOALL", "e_slotg"], writes=[gkkey], kind="dma",
                tag="e_cgat")
            if i == 0 and "snap" in spec:
                p.op("vector", lambda e: e.tensor_copy(
                    ctx.sb["e_gk0"][:, 0:1024], gk[:, 0:1024]),
                    reads=["e_dtok"], writes=["e_gk0"], tag="e_snap")
            if kind == "l2":
                p.op("vector", lambda e, i=i: e.tensor_scalar(
                    gw[:, 0:16], ctx.cap("gmask16")[:, 0:16], wfin[:, i:i + 1],
                    None, ALU.mult),
                    reads=["consts", "e_wfin"], writes=["e_gw"], tag="e_glh")
                for nn in range(2):
                    p.op("tensor", lambda e, nn=nn, gk=gk: e.matmul(
                        ps4[0:16, 0:512] if nn == 0 else ps5[0:16, 0:512],
                        gw[:, 0:16], gk[:, nn * 512:(nn + 1) * 512],
                        start=True, stop=True),
                        reads=["e_gw", gkkey], writes=[f"ps{4 + nn}"],
                        tag="e_gmm")
                for nn in range(2):
                    p.op("scalar", lambda e, nn=nn: e.activation(
                        stg[0:16, nn * 512:(nn + 1) * 512],
                        (ps4 if nn == 0 else ps5)[0:16, 0:512], AF.Identity),
                        reads=[f"ps{4 + nn}"], writes=["e_stg"], tag="e_gev")
                p.dma("sync", spec["COMB"][i * 16:(i + 1) * 16, :],
                      stg[0:16, 0:1024], reads=["e_stg"], writes=["COMB"],
                      tag="e_gout")
            else:
                jj = i // NT
                tl = i % NT
                p.op("vector", lambda e, i=i, gk=gk: e.tensor_scalar(
                    gk[:, 0:1024], gk[:, 0:1024],
                    wfin[:, i:i + 1], None, ALU.mult),
                    reads=[gkkey, "e_wfin"], writes=[gkkey], tag="e_gmw")
                if jj == 0:
                    p.dma("gpsimd", spec["COMB"][tl * 128:(tl + 1) * 128, :],
                          gk[:, 0:1024],
                          reads=[gkkey], writes=["COMB"], tag="e_gc1")
                else:
                    p.dma("gpsimd", spec["COMB"][tl * 128:(tl + 1) * 128, :],
                          gk[:, 0:1024], accum_op=ALU.add,
                          reads=[gkkey], writes=["COMB"], tag="e_gc2")
    xdst = ctx.sb[x_name]
    for k in range(KD):
        p.dma("scalar", ctx.sb["e_combT"][:, 0:T],
              spec["COMB"][:, k * 128:(k + 1) * 128], transpose=True,
              reads=["COMB"], writes=["e_combT"], tag="e_ctr")
        p.op("vector", lambda e, k=k: e.tensor_tensor(
            xdst[:, k * T:(k + 1) * T], xdst[:, k * T:(k + 1) * T],
            ctx.sb["e_combT"][:, 0:T], ALU.add),
            reads=["e_combT", x_name], writes=[x_name], tag="e_cadd")


def make_moe_dispatch_ffn(ctx, spec):
    p = ctx.p
    E, K, CAP = spec["E"], spec["K"], spec["CAP"]
    kind = spec["kind"]
    identb = ctx.cap("ident_bf")
    ident = ctx.cap("ident")
    psb = ctx.ps["ps7"]
    ps0, ps1, ps2, ps3, ps4, ps5 = (ctx.ps[f"ps{i}"] for i in range(6))
    wsl = ctx.sb["wbuf"]
    dtik = ctx.sb["e_dtok"]
    dspf = ctx.sb["e_dspf"]
    hsb = ctx.sb["e_hsb"]
    eosb = ctx.sb["e_eosb"]
    mapc = ctx.sb["e_mapc"]

    def ffn(ctx, le, _):
        NCH_S = CAP // 512
        for scb in range(NCH_S):
            for it in range(4):
                i = scb * 4 + it
                p.op("gpsimd", lambda e, i=i: e.indirect_dma_start(
                    out=dtik[:, 0:1024], out_offset=None,
                    in_=spec["XALL"][:, :],
                    in_offset=IOffs(mapc[:, i:i + 1], 0)),
                    reads=["XALL", "e_mapc"], writes=["e_dtok"], kind="dma",
                    tag="e_dgat")
                for k in range(KD):
                    p.op("tensor", lambda e, k=k: e.transpose(
                        psb[:, k * 128:(k + 1) * 128],
                        dtik[:, k * 128:(k + 1) * 128], identb),
                        reads=["e_dtok", "consts"], writes=["ps7"], tag="e_dtr")
                p.op("scalar", lambda e, it=it: e.activation(
                    dspf[:, 0:8 * 512].rearrange("p (k s) -> p k s", s=512)
                    [:, :, it * 128:(it + 1) * 128],
                    psb[:, 0:1024].rearrange("p (k c) -> p k c", c=128),
                    AF.Identity),
                    reads=["ps7"], writes=["e_dspf"], tag="e_dev")
            if "DDSP" in spec and le == spec["E"] // 8 - 1 and scb == 0:
                p.dma("sync", spec["DDSP"][:, :], dspf[:, 0:8 * 512],
                      reads=["e_dspf"], writes=["D"], tag="e_snapd")
            if kind == "l0":
                H2 = 32
                for m in range(H2):
                    slot = m % 2
                    wv = wsl[:, slot * 1024:slot * 1024 + 1024]
                    p.dma("sync", wv.rearrange("p (k c) -> p k c", c=128),
                          spec["WFC1"][:, m * 128:(m + 1) * 128]
                          .rearrange("(k p) c -> p k c", p=128),
                          reads=["WFC1b"], writes=[f"wb{slot}"], tag="e_w1ld")
                    pb = ps0 if slot == 0 else ps1
                    for k in range(KD):
                        p.op("tensor", lambda e, k=k, wv=wv, pb=pb: e.matmul(
                            pb[:, :], wv[:, k * 128:(k + 1) * 128],
                            dspf[:, k * 512:(k + 1) * 512],
                            start=(k == 0), stop=(k == KD - 1)),
                            reads=[f"wb{slot}", "e_dspf"], writes=[f"ps{slot}"],
                            tag="e_fc1")
                    p.op("scalar", lambda e, m=m, pb=pb: e.activation(
                        hsb[:, m * 512:(m + 1) * 512], pb[:, :], AF.Relu,
                        bias=ctx.sb["e_b1"][:, m:m + 1]),
                        reads=[f"ps{slot}", "e_b1"], writes=["e_hsb"],
                        tag="e_fc1ev")
                for d in range(KD):
                    wv = wsl[:, 2048:2048 + 2048]
                    pb = ps2 if d % 2 == 0 else ps3
                    for dh in range(2):
                        p.dma("sync", wv.rearrange("p (k c) -> p k c", c=128),
                              spec["WFC2"][dh * 2048:(dh + 1) * 2048,
                                           d * 128:(d + 1) * 128]
                              .rearrange("(k p) c -> p k c", p=128),
                              reads=["WFC2b"], writes=["wc0"], tag="e_w2ld")
                        for hh in range(16):
                            p.op("tensor", lambda e, hh=hh, dh=dh, wv=wv,
                                 pb=pb: e.matmul(
                                pb[:, :], wv[:, hh * 128:(hh + 1) * 128],
                                hsb[:, (dh * 16 + hh) * 512:
                                     (dh * 16 + hh + 1) * 512],
                                start=(dh == 0 and hh == 0),
                                stop=(dh == 1 and hh == 15)),
                                reads=["wc0", "e_hsb"],
                                writes=[f"ps{2 + d % 2}"], tag="e_fc2")
                    p.op("scalar", lambda e, d=d, pb=pb: e.activation(
                        eosb[:, d * 512:(d + 1) * 512], pb[:, :], AF.Identity,
                        bias=ctx.sb["e_b2"][:, d:d + 1]),
                        reads=[f"ps{2 + d % 2}", "e_b2"], writes=["e_eosb"],
                        tag="e_fc2ev")
            else:
                for m in range(KD):
                    slot = m % 2
                    wv = wsl[:, slot * 1024:slot * 1024 + 1024]
                    p.dma("sync", wv.rearrange("p (k c) -> p k c", c=128),
                          spec["WGU"][:, le * 1024 + m * 128:le * 1024 + (m + 1) * 128]
                          .rearrange("(k p) c -> p k c", p=128),
                          reads=["WGUb"], writes=[f"wb{slot}"], tag="e_wguld")
                    pb = ps0 if slot == 0 else ps1
                    for k in range(KD):
                        p.op("tensor", lambda e, k=k, wv=wv, pb=pb: e.matmul(
                            pb[:, :], wv[:, k * 128:(k + 1) * 128],
                            dspf[:, k * 512:(k + 1) * 512],
                            start=(k == 0), stop=(k == KD - 1)),
                            reads=[f"wb{slot}", "e_dspf"], writes=[f"ps{slot}"],
                            tag="e_gu")
                    if m < 4:
                        p.op("vector", lambda e, m=m, pb=pb: e.tensor_scalar(
                            ctx.sb["e_gbuf"][:, m * 512:(m + 1) * 512],
                            pb[:, :], 7.0, None, ALU.min),
                            reads=[f"ps{slot}"], writes=["e_gbuf"], tag="e_gclip")
                    else:
                        p.op("vector", lambda e, m=m, pb=pb: e.tensor_scalar(
                            hsb[:, (m - 4) * 512:(m - 3) * 512],
                            pb[:, :], 7.0, -7.0, ALU.min, ALU.max),
                            reads=[f"ps{slot}"], writes=["e_hsb"], tag="e_uclip")
                for m in range(4):
                    p.op("scalar", lambda e, m=m: e.activation(
                        ctx.sb["e_gbuf"][:, m * 512:(m + 1) * 512],
                        ctx.sb["e_gbuf"][:, m * 512:(m + 1) * 512], AF.Silu),
                        reads=["e_gbuf"], writes=["e_gbuf"], tag="e_gsilu")
                    p.op("vector", lambda e, m=m: e.tensor_tensor(
                        hsb[:, m * 512:(m + 1) * 512],
                        hsb[:, m * 512:(m + 1) * 512],
                        ctx.sb["e_gbuf"][:, m * 512:(m + 1) * 512], ALU.mult),
                        reads=["e_gbuf", "e_hsb"], writes=["e_hsb"], tag="e_h")
                for d in range(KD):
                    slot = d % 2
                    wv = wsl[:, 2048 + slot * 512:2048 + slot * 512 + 512]
                    p.dma("sync", wv.rearrange("p (k c) -> p k c", c=128),
                          spec["WDN"][:, le * 1024 + d * 128:le * 1024 + (d + 1) * 128]
                          .rearrange("(k p) c -> p k c", p=128),
                          reads=["WDNb"], writes=[f"wc{slot}"], tag="e_wdld")
                    pb = ps2 if slot == 0 else ps3
                    for hh in range(4):
                        p.op("tensor", lambda e, hh=hh, wv=wv, pb=pb: e.matmul(
                            pb[:, :], wv[:, hh * 128:(hh + 1) * 128],
                            hsb[:, hh * 512:(hh + 1) * 512],
                            start=(hh == 0), stop=(hh == 3)),
                            reads=[f"wc{slot}", "e_hsb"], writes=[f"ps{2 + slot}"],
                            tag="e_dn")
                    p.op("scalar", lambda e, d=d, pb=pb: e.activation(
                        eosb[:, d * 512:(d + 1) * 512], pb[:, :], AF.Identity),
                        reads=[f"ps{2 + slot}"], writes=["e_eosb"], tag="e_dnev")
            for it in range(4):
                for k in range(KD):
                    p.op("tensor", lambda e, it=it, k=k: e.transpose(
                        psb[:, k * 128:(k + 1) * 128],
                        eosb[:, 0:8 * 512].rearrange("p (k s) -> p k s", s=512)
                        [:, k, it * 128:(it + 1) * 128], identb),
                        reads=["e_eosb", "consts"], writes=["ps7"], tag="e_etr")
                p.op("scalar", lambda e: e.activation(
                    dtik[:, 0:1024], psb[:, 0:1024], AF.Identity),
                    reads=["ps7"], writes=["e_dtok"], tag="e_eev")
                p.dma("sync",
                      spec["EOI"][le * CAP + scb * 512 + it * 128:
                                  le * CAP + scb * 512 + (it + 1) * 128, :],
                      dtik[:, 0:1024], reads=["e_dtok"], writes=["EOI"],
                      tag="e_eout")
            if "DEOI2" in spec and le == spec["E"] // 8 - 1 and scb == 0:
                p.dma("sync", spec["DEOI2"][:, :],
                      spec["EOI"][le * CAP:le * CAP + 512, :],
                      reads=["EOI"], writes=["D"], tag="e_snap2")
    return ffn


def emit_l2shared(ctx, xn_name, x_name, SG, SU, SDT, SGB=None, SUB=None,
                  SDB=None):
    p = ctx.p
    xn = ctx.sb[xn_name]
    xd = ctx.sb[x_name]
    wsl = ctx.sb["wbuf"]
    p.dma("gpsimd", SGB[:, :], SG[:, :], writes=["SGBb"], tag="sh_cg")
    p.dma("gpsimd", SUB[:, :], SU[:, :], writes=["SUBb"], tag="sh_cu")
    p.dma("gpsimd", SDB[:, :], SDT[:, :], writes=["SDBb"], tag="sh_cd")
    SG, SU, SDT = SGB, SUB, SDB
    hsb = ctx.sb["e_hsb"]
    gb = ctx.sb["e_gbuf"]
    ps0, ps1, ps2, ps3 = (ctx.ps[f"ps{i}"] for i in range(4))
    for tc in range(NCH):
        t0 = tc * 512
        for m in range(16):
            slot = m % 2
            wv = wsl[:, slot * 1024:slot * 1024 + 1024]
            p.dma("sync", wv.rearrange("p (k c) -> p k c", c=128),
                  SG[:, m * 128:(m + 1) * 128].rearrange("(k p) c -> p k c", p=128),
                  reads=["SGBb"], writes=[f"wb{slot}"], tag="sh_g")
            pb = ps0 if slot == 0 else ps1
            for k in range(KD):
                p.op("tensor", lambda e, k=k, wv=wv, pb=pb, t0=t0: e.matmul(
                    pb[:, :], wv[:, k * 128:(k + 1) * 128],
                    xn[:, k * T + t0:k * T + t0 + 512],
                    start=(k == 0), stop=(k == KD - 1)),
                    reads=[f"wb{slot}", xn_name], writes=[f"ps{slot}"],
                    tag="sh_gmm")
            p.op("scalar", lambda e, m=m, pb=pb: e.activation(
                hsb[:, m * 512:(m + 1) * 512], pb[:, :], AF.Silu),
                reads=[f"ps{slot}"], writes=["e_hsb"], tag="sh_gsil")
        for m in range(16):
            slot = m % 2
            wv = wsl[:, 2048 + slot * 1024:2048 + slot * 1024 + 1024]
            p.dma("sync", wv.rearrange("p (k c) -> p k c", c=128),
                  SU[:, m * 128:(m + 1) * 128].rearrange("(k p) c -> p k c", p=128),
                  reads=["SUBb"], writes=[f"wc{slot}"], tag="sh_u")
            pb = ps2 if slot == 0 else ps3
            for k in range(KD):
                p.op("tensor", lambda e, k=k, wv=wv, pb=pb, t0=t0: e.matmul(
                    pb[:, :], wv[:, k * 128:(k + 1) * 128],
                    xn[:, k * T + t0:k * T + t0 + 512],
                    start=(k == 0), stop=(k == KD - 1)),
                    reads=[f"wc{slot}", xn_name], writes=[f"ps{2 + slot}"],
                    tag="sh_umm")
            p.op("vector", lambda e, m=m, pb=pb: e.tensor_tensor(
                hsb[:, m * 512:(m + 1) * 512], hsb[:, m * 512:(m + 1) * 512],
                pb[:, :], ALU.mult),
                reads=[f"ps{2 + slot}", "e_hsb"], writes=["e_hsb"], tag="sh_h")
        for d in range(KD):
            slot = d % 2
            wv = wsl[:, 4096 + slot * 2048:4096 + slot * 2048 + 2048]
            p.dma("sync", wv.rearrange("p (k c) -> p k c", c=128),
                  SDT[:, d * 128:(d + 1) * 128].rearrange("(k p) c -> p k c", p=128),
                  reads=["SDBb"], writes=[f"wd{slot}"], tag="sh_d")
            pb = ps0 if slot == 0 else ps1
            for hh in range(16):
                p.op("tensor", lambda e, hh=hh, wv=wv, pb=pb: e.matmul(
                    pb[:, :], wv[:, hh * 128:(hh + 1) * 128],
                    hsb[:, hh * 512:(hh + 1) * 512],
                    start=(hh == 0), stop=(hh == 15)),
                    reads=[f"wd{slot}", "e_hsb"], writes=[f"ps{slot}"],
                    tag="sh_dmm")
            p.op("vector", lambda e, d=d, pb=pb, t0=t0: e.tensor_tensor(
                xd[:, d * T + t0:d * T + t0 + 512],
                xd[:, d * T + t0:d * T + t0 + 512], pb[:, :], ALU.add),
                reads=[f"ps{slot}", x_name], writes=[x_name], tag="sh_add")



MOE_PTAGS = set("e_mld,e_mtr,e_mcast,e_dgat,e_dtr,e_dev,e_wguld,e_gu,"
                "e_gclip,e_uclip,e_gsilu,e_h,e_wdld,e_dn,e_dnev,e_etr,e_eev,"
                "e_eout,e_bare0,e_ageo,e_bar2,e_cgat,e_glh,e_gmm,e_gev,"
                "e_gout,e_ctr,e_cadd,sh_g,sh_gmm,sh_gsil,sh_u,sh_umm,sh_h,"
                "sh_d,sh_dmm,sh_add,st,e_w1ld,e_fc1,e_fc1ev,e_w2ld,e_fc2,"
                "e_fc2ev,e_gmw,e_gc1,e_gc2".split(","))


def emit_ln_stream(ctx, XT, xn_name, xbuf_name, stats_name=None,
                   sq_name="ln_sq", row_name="ln_rows", row_off=0,
                   banks=("ps0", "ps1", "ps2", "ps3")):
    p = ctx.p
    xb = ctx.sb[xbuf_name]
    xn = ctx.sb[xn_name]
    sq = ctx.sb[sq_name]
    rows = ctx.sb[row_name][:, row_off:row_off + 2048]
    ones_col = ctx.cap("ones_col")[:, 0:1]
    ones_row = ctx.cap("ones_row")[0:1, :]
    eps_col = ctx.cap("eps_col")
    TC = 512
    pm, pq, pmb, prb = (ctx.ps[b] for b in banks)
    kpm, kpq, kmb, krb = banks
    for tc in range(T // TC):
        for k in range(KD):
            p.dma("sync", xb[:, k * TC:(k + 1) * TC],
                  XT[k * 128:(k + 1) * 128, tc * TC:(tc + 1) * TC],
                  writes=[xbuf_name], tag="lnx")
        for k in range(KD):
            xs = xb[:, k * TC:(k + 1) * TC]
            p.op("tensor", lambda e, xs=xs, k=k: e.matmul(
                pm[0:1, :], ones_col, xs, start=(k == 0), stop=(k == KD - 1)),
                reads=[xbuf_name, "consts"], writes=[kpm], tag="ln_sum")
            p.op("scalar", lambda e, xs=xs: e.activation(sq[:, :], xs, AF.Square),
                 reads=[xbuf_name], writes=[sq_name], tag="ln_sq")
            p.op("tensor", lambda e, k=k: e.matmul(
                pq[0:1, :], ones_col, sq[:, :], start=(k == 0), stop=(k == KD - 1)),
                reads=[sq_name, "consts"], writes=[kpq], tag="ln_sum2")
        m = rows[0:1, 0:TC]
        s2 = rows[0:1, TC:2 * TC]
        r = rows[0:1, 2 * TC:3 * TC]
        mr = rows[0:1, 3 * TC:4 * TC]
        p.op("vector", lambda e, m=m: e.tensor_scalar(
            m, pm[0:1, :], 1.0 / 1024, None, ALU.mult),
            reads=[kpm], writes=[row_name], tag="ln_m")
        p.op("vector", lambda e, s2=s2: e.tensor_scalar(
            s2, pq[0:1, :], 1.0 / 1024, None, ALU.mult),
            reads=[kpq], writes=[row_name], tag="ln_s2")
        p.op("vector", lambda e, m=m, mr=mr: e.tensor_tensor(
            mr, m, m, ALU.mult), reads=[row_name], writes=[row_name], tag="ln_msq")
        p.op("vector", lambda e, s2=s2, mr=mr: e.tensor_tensor(
            s2, s2, mr, ALU.subtract), reads=[row_name], writes=[row_name],
            tag="ln_v")
        p.op("scalar", lambda e, s2=s2, r=r: e.activation(
            r, s2, AF.Sqrt, bias=eps_col[0:1, 0:1]),
            reads=[row_name, "consts"], writes=[row_name], tag="ln_sqrt")
        p.op("vector", lambda e, r=r: e.reciprocal(r, r),
             reads=[row_name], writes=[row_name], tag="ln_rstd")
        if stats_name is not None:
            st = ctx.sb[stats_name]
            p.op("vector", lambda e, m=m, tc=tc: e.tensor_copy(
                st[0:1, tc * TC:(tc + 1) * TC], m),
                reads=[row_name], writes=[stats_name], tag="ln_stm")
            p.op("vector", lambda e, r=r, tc=tc: e.tensor_copy(
                st[0:1, T + tc * TC:T + (tc + 1) * TC], r),
                reads=[row_name], writes=[stats_name], tag="ln_str")
        p.op("vector", lambda e, m=m, r=r, mr=mr: e.tensor_tensor(
            mr, m, r, ALU.mult), reads=[row_name], writes=[row_name], tag="ln_mr")
        p.op("vector", lambda e, mr=mr: e.tensor_scalar(
            mr, mr, -1.0, None, ALU.mult), reads=[row_name], writes=[row_name],
            tag="ln_mrneg")
        p.op("tensor", lambda e, mr=mr: e.matmul(pmb[:, 0:TC], ones_row, mr,
             start=True, stop=True), reads=[row_name, "consts"],
             writes=[kmb], tag="ln_bcast_m")
        p.op("tensor", lambda e, r=r: e.matmul(prb[:, 0:TC], ones_row, r,
             start=True, stop=True), reads=[row_name, "consts"],
             writes=[krb], tag="ln_bcast_r")
        for k in range(KD):
            xs = xb[:, k * TC:(k + 1) * TC]
            xns = xn[:, k * T + tc * TC:k * T + (tc + 1) * TC]
            p.op("vector", lambda e, xs=xs: e.tensor_tensor(
                sq[:, :], xs, prb[:, 0:TC], ALU.mult),
                reads=[xbuf_name, krb], writes=[sq_name], tag="ln_n1")
            p.op("vector", lambda e, xns=xns: e.tensor_tensor(
                xns, sq[:, :], pmb[:, 0:TC], ALU.add),
                reads=[sq_name, kmb], writes=[xn_name], tag="ln_n2")


def emit_resid_out(ctx, XT, XO, terms, xbuf_name="ln_xb"):
    p = ctx.p
    xb = ctx.sb[xbuf_name]
    TC = 512
    for tc in range(T // TC):
        for k in range(KD):
            p.dma("sync", xb[:, k * TC:(k + 1) * TC],
                  XT[k * 128:(k + 1) * 128, tc * TC:(tc + 1) * TC],
                  writes=[xbuf_name], tag="rox")
        for k in range(KD):
            xs = xb[:, k * TC:(k + 1) * TC]
            for tname in terms:
                tb = ctx.sb[tname]
                p.op("vector", lambda e, xs=xs, tb=tb, k=k, tc=tc:
                     e.tensor_tensor(xs, xs,
                                     tb[:, k * T + tc * TC:k * T + (tc + 1) * TC],
                                     ALU.add),
                     reads=[tname, xbuf_name], writes=[xbuf_name], tag="roadd")
        for k in range(KD):
            p.dma("sync", XO[k * 128:(k + 1) * 128, tc * TC:(tc + 1) * TC],
                  xb[:, k * TC:(k + 1) * TC], reads=[xbuf_name],
                  writes=["XOUT"], tag="st", final=True)


def new_ctx():
    nc = bass.Bass("TRN2", target_bir_lowering=False)
    p = Prog(nc)
    es = ExitStack()
    ctx = Ctx(nc, p, es)
    return nc, p, es, ctx


def build_l0a():
    nc, p, es, ctx = new_ctx()
    XT = ctx.dram_t("XT", (1024, T), F32, kind="ExternalInput")
    INPJ = ctx.dram_t("INPJ", (1024, 5152), F32, kind="ExternalInput")
    OUTPJ = ctx.dram_t("OUTPJ", (2048, 1024), F32, kind="ExternalInput")
    COLS = ctx.dram_t("COLS", (128, 163), F32, kind="ExternalInput")
    ZSP = ctx.dram_t("ZSP", (2048, T), BF16)
    XOI = ctx.dram_t("XOI", (1024, T), F32)
    XO = ctx.dram_t("XO", (1024, T), F32, kind="ExternalOutput")
    ctx.sbuf("xn", (128, KD * T), BF16)
    ctx.sbuf("oevb", (128, 512), F32)
    ctx.sbuf("ln_sq", (128, 512), F32)
    ctx.sbuf("l0cols", (128, 163), F32)
    ctx.sbuf("m_rows", (128, T), F32)
    ctx.sbuf("m_diag", (128, 4 * 128), BF16)
    ctx.sbuf("m_xmq", (128, 8 * 512), BF16)
    ctx.sbuf("m_zq", (128, 16 * 512), BF16)
    ctx.sbuf("m_xbc", (128, 24 * 512), BF16)
    ctx.sbuf("m_xcp", (128, 24 * 515), BF16)
    ctx.sbuf("m_dt", (128, 6 * 512), F32)
    ctx.sbuf("m_thp", (128, 64), F32)
    ctx.sbuf("m_rowR", (128, 4096), F32)
    ctx.sbuf("m_yq", (128, 16 * 512), BF16)
    ctx.sbuf("m_wexp", (128, 256), F32)
    ctx.sbuf("m_wbf", (128, 128), BF16)
    ctx.sbuf("m_xtok", (128, 64), BF16)
    ctx.sbuf("m_btok", (128, 8 * 64), BF16)
    ctx.sbuf("m_chat", (128, 192), BF16)
    ctx.sbuf("m_gsc", (128, 512), F32)
    ctx.sbuf("m_S", (128, 32 * 64), F32)
    ctx.sbuf("m_Sb", (128, 32 * 64), BF16)
    ctx.sbuf("wbuf", (128, 4096), BF16)
    for i in range(7):
        ctx.psum(f"ps{i}")
    ctx.psum("ps7", (128, 1024), BF16)
    register_common_consts(ctx, skip=("mask_triu_incl", "mask_triu_strict",
                                      "mask_tril_strict", "iota_row"))
    packs = ctx.pack_consts()
    p.dma("sync", ctx.sb["l0cols"][:, :], COLS[:, :], writes=["l0cols"], tag="ldc")
    emit_ln_stream(ctx, XT, "xn", "m_rowR", sq_name="m_gsc", row_name="m_rows")
    emit_resid_out(ctx, XT, XOI, ["xn"], xbuf_name="m_rowR")

    def epi(m, q, pb, bank):
        p.op("scalar", lambda e, pb=pb: e.activation(
            ctx.sb["oevb"][:, :], pb[:, :], AF.Identity),
            reads=[bank], writes=["oevb"], tag="oev")
        p.dma("gpsimd", XOI[m * 128:(m + 1) * 128, q * 512:(q + 1) * 512],
              ctx.sb["oevb"][:, :], accum_op=ALU.add,
              reads=["oevb"], writes=["XOUT"], tag="acc")
    emit_mamba(ctx, "xn", INPJ, OUTPJ, ZSP, "l0cols", epi)
    for k in range(KD):
        for tc2 in range(4):
            p.dma("sync", ctx.sb["m_rowR"][:, 0:512],
                  XOI[k * 128:(k + 1) * 128, tc2 * 512:(tc2 + 1) * 512],
                  reads=["XOUT"], writes=["m_rowR"], tag="cpl")
            p.dma("sync", XO[k * 128:(k + 1) * 128, tc2 * 512:(tc2 + 1) * 512],
                  ctx.sb["m_rowR"][:, 0:512], reads=["m_rowR"],
                  writes=["XOF"], tag="st", final=True)
    stats = p.finalize()
    es.close()
    return nc, packs


def build_l1a():
    nc, p, es, ctx = new_ctx()
    XT = ctx.dram_t("XT", (1024, T), F32, kind="ExternalInput")
    WQ = ctx.dram_t("WQ", (1024, 1024), F32, kind="ExternalInput")
    WK = ctx.dram_t("WK", (1024, 1024), F32, kind="ExternalInput")
    WV = ctx.dram_t("WV", (1024, 1024), F32, kind="ExternalInput")
    WB = ctx.dram_t("WB", (128, 8), F32, kind="ExternalInput")
    LNG = ctx.dram_t("LNG", (1, 2048), F32, kind="ExternalInput")
    XO = ctx.dram_t("XO", (1024, T), F32, kind="ExternalOutput")
    ctx.sbuf("xn", (128, KD * T), BF16)
    ctx.sbuf("contrib", (128, KD * T), BF16)
    ctx.sbuf("ln_xb", (128, KD * 512), F32)
    ctx.sbuf("ln_sq", (128, 512), F32)
    ctx.sbuf("ln_rows", (128, 2048), F32)
    ctx.sbuf("d_wq", (128, 8 * 1024), BF16)
    ctx.sbuf("d_wk", (128, 8 * 1024), BF16)
    ctx.sbuf("d_wv", (128, 8 * 1024), BF16)
    ctx.sbuf("d_wb", (128, 8), BF16)
    ctx.sbuf("d_S", (128, 8 * 1024), BF16)
    ctx.sbuf("d_kt", (128, 1024), BF16)
    ctx.sbuf("d_vt", (128, 1024), BF16)
    ctx.sbuf("d_qt", (128, 1024), BF16)
    ctx.sbuf("d_ktT", (128, 8 * 128), BF16)
    ctx.sbuf("d_qtT", (128, 8 * 128), BF16)
    ctx.sbuf("d_R", (128, 1024), BF16)
    ctx.sbuf("d_P", (128, 2 * 128), BF16)
    ctx.sbuf("d_M", (128, 128), BF16)
    ctx.sbuf("d_osb", (128, 1024), F32)
    ctx.sbuf("d_cols", (128, 8), F32)
    ctx.sbuf("d_lngb", (128, 2048), F32)
    ctx.sbuf("d_lngrow", (128, 2048), F32)
    ctx.sbuf("d_brow", (128, 256), F32)
    for i in range(7):
        ctx.psum(f"ps{i}")
    ctx.psum("ps7", (128, 1024), BF16)
    register_common_consts(ctx)
    packs = ctx.pack_consts()
    emit_ln_stream(ctx, XT, "xn", "ln_xb")
    p.op("vector", lambda e: e.memset(ctx.sb["contrib"][:, :], 0.0),
         writes=["contrib"])
    emit_deltanet(ctx, "xn", WQ, WK, WV, WB, LNG, "contrib", lngb_done=[False])
    emit_resid_out(ctx, XT, XO, ["contrib"])
    p.finalize()
    es.close()
    return nc, packs


def build_l1f():
    nc, p, es, ctx = new_ctx()
    XT = ctx.dram_t("XT", (1024, T), F32, kind="ExternalInput")
    XP = ctx.dram_t("XP", (1024, T), F32, kind="ExternalInput")
    WR = ctx.dram_t("WR", (1024, 1024), F32, kind="ExternalInput")
    WK = ctx.dram_t("WK", (1024, 1024), F32, kind="ExternalInput")
    WV = ctx.dram_t("WV", (1024, 1024), F32, kind="ExternalInput")
    MU = ctx.dram_t("MU", (128, 16), F32, kind="ExternalInput")
    XO = ctx.dram_t("XO", (1024, T), F32, kind="ExternalOutput")
    ctx.sbuf("xn", (128, KD * T), BF16)
    ctx.sbuf("ln_xb", (128, KD * 512), F32)
    ctx.sbuf("ln_sq", (128, 512), F32)
    ctx.sbuf("l1_mu", (128, 16), F32)
    ctx.sbuf("l1_xr", (128, KD * T), BF16)
    ctx.sbuf("l1_xk", (128, KD * T), BF16)
    ctx.sbuf("l1_xko", (128, KD * T), BF16)
    ctx.sbuf("l1_rbig", (128, KD * T), BF16)
    ctx.sbuf("l1_d", (128, T), F32)
    ctx.sbuf("l1_r", (128, T), F32)
    ctx.sbuf("wbuf", (128, 2 * 1024), BF16)
    for i in range(4):
        ctx.psum(f"ps{i}")
    register_common_consts(ctx, skip=("mask_sl_neg", "mask_triu_incl",
                                      "mask_triu_strict", "mask_tril_strict",
                                      "iota_row"))
    packs = ctx.pack_consts()
    p.dma("sync", ctx.sb["l1_mu"][:, :], MU[:, :], writes=["l1_mu"], tag="ldmu")
    emit_ln_stream(ctx, XP, "xn", "ln_xb", row_name="l1_d")
    emit_l1ffn(ctx, "xn", "l1_xr", WR, WK, WV)
    emit_resid_out(ctx, XT, XO, ["l1_xr"])
    p.finalize()
    es.close()
    return nc, packs


def build_attn(with_bias):
    nc, p, es, ctx = new_ctx()
    XT = ctx.dram_t("XT", (1024, T), F32, kind="ExternalInput")
    WQ = ctx.dram_t("WQ", (1024, 1024), F32, kind="ExternalInput")
    WK = ctx.dram_t("WK", (1024, 1024), F32, kind="ExternalInput")
    WV = ctx.dram_t("WV", (1024, 1024), F32, kind="ExternalInput")
    WO = ctx.dram_t("WO", (1024, 1024), F32, kind="ExternalInput")
    BIAS = ctx.dram_t("BIAS", (128, 32), F32, kind="ExternalInput")
    XO = ctx.dram_t("XO", (1024, T), F32, kind="ExternalOutput")
    ctx.sbuf("xn", (128, KD * T), BF16)
    ctx.sbuf("ln_xb", (128, KD * 512), F32)
    ctx.sbuf("ln_sq", (128, 512), F32)
    ctx.sbuf("ln_rows", (128, 2048), F32)
    ctx.sbuf("attn", (128, KD * T), BF16)
    ctx.sbuf("res", (128, KD * T), BF16)
    ctx.sbuf("mha_q", (128, 4 * T), BF16)
    ctx.sbuf("mha_k", (128, 4 * T), BF16)
    ctx.sbuf("mha_v", (128, 4 * T), F32)
    ctx.sbuf("mha_vt", (128, 16 * 64), BF16)
    ctx.sbuf("mha_pA", (128, 512), BF16)
    ctx.sbuf("mha_pB", (128, 512), BF16)
    ctx.sbuf("mha_den", (128, 1024), F32)
    ctx.sbuf("bias_sb", (128, 32), F32)
    ctx.sbuf("wbuf", (128, 2 * 1024), BF16)
    for i in range(8):
        ctx.psum(f"ps{i}")
    register_common_consts(ctx, skip=("mask_sl_neg", "mask_triu_incl",
                                      "mask_triu_strict", "mask_tril_strict",
                                      "iota_row", "ident_bf"))
    register_attn_consts(ctx)
    packs = ctx.pack_consts()
    p.dma("sync", ctx.sb["bias_sb"][:, :], BIAS[:, :], writes=["bias_sb"],
          tag="ldb")
    emit_ln_stream(ctx, XT, "xn", "ln_xb")
    p.op("vector", lambda e: e.memset(ctx.sb["res"][:, :], 0.0),
         writes=["res"], tag="zero")
    scale = 0.125 if with_bias else 1.0 / 32.0
    biases = ("bias_sb", [0, 8, 16]) if with_bias else None
    if with_bias:
        p.op("vector", lambda e: e.tensor_scalar(
            ctx.sb["bias_sb"][:, 0:8], ctx.sb["bias_sb"][:, 0:8], scale, None,
            ALU.mult), reads=["bias_sb"], writes=["bias_sb"], tag="bscale")
    emit_mha(ctx, "xn", "attn", [(WQ, 0), (WK, 0), (WV, 0)], scale,
             biases=biases)
    if with_bias:
        bias = ("bias_sb", lambda m: ctx.sb["bias_sb"][:, 24 + m:24 + m + 1])
    else:
        bias = None
    emit_proj(ctx, "res", "attn", WO, 8, 8, wslot="wbuf",
              banks=("ps0", "ps1", "ps2", "ps3"), bias=bias, dst_dt_bf=False)
    emit_resid_out(ctx, XT, XO, ["res"])
    p.finalize()
    es.close()
    return nc, packs


def build_l3f():
    nc, p, es, ctx = new_ctx()
    XT = ctx.dram_t("XT", (1024, T), F32, kind="ExternalInput")
    XP = ctx.dram_t("XP", (1024, T), F32, kind="ExternalInput")
    WG = ctx.dram_t("WG", (2816, 1024), F32, kind="ExternalInput")
    WU = ctx.dram_t("WU", (2816, 1024), F32, kind="ExternalInput")
    WD = ctx.dram_t("WD", (1024, 2816), F32, kind="ExternalInput")
    WGQ = ctx.dram_t("WGQ", (1024, 2816), BF16)
    WUQ = ctx.dram_t("WUQ", (1024, 2816), BF16)
    WDQ = ctx.dram_t("WDQ", (2816, 1024), BF16)
    XOI = ctx.dram_t("XOI", (1024, T), F32)
    XO = ctx.dram_t("XO", (1024, T), F32, kind="ExternalOutput")
    ctx.sbuf("xn", (128, KD * T), BF16)
    ctx.sbuf("ln_xb", (128, KD * 512), F32)
    ctx.sbuf("xq", (128, KD * T), BF16)
    ctx.sbuf("l3w", (128, 1408), F32)
    ctx.sbuf("l3wq", (128, 2816), BF16)
    ctx.sbuf("l3st", (128, 2816), BF16)
    ctx.sbuf("l3xt", (128, 1024), BF16)
    ctx.sbuf("l3xq", (128, 1024), BF16)
    ctx.sbuf("l3hq", (128, 22 * 512), BF16)
    ctx.sbuf("l3hsc", (128, 4 * 512), F32)
    ctx.sbuf("l3hbuf", (128, 512), F32)
    ctx.sbuf("l3hql", (128, 512), BF16)
    ctx.sbuf("qa", (128, 1408), F32)
    ctx.sbuf("qs", (128, 1408), F32)
    ctx.sbuf("qf", (128, 6 * 1408), F16)
    ctx.sbuf("qbm", (128, 3 * 88), F32)
    ctx.sbuf("wbuf", (128, 2 * 4096), BF16)
    ctx.sbuf("wbufd", (128, 2 * 22 * 128), BF16)
    for i in range(6):
        ctx.psum(f"ps{i}")
    ctx.psum("psbf", (128, 1024), BF16)
    register_common_consts(ctx)
    register_quant_consts(ctx)
    packs = ctx.pack_consts()
    emit_ln_stream(ctx, XP, "xn", "ln_xb", sq_name="l3hbuf",
                   row_name="l3hsc")
    emit_resid_out(ctx, XT, XOI, [])

    def epi(dt, tc, pb, bank):
        p.op("scalar", lambda e, pb=pb: e.activation(
            ctx.sb["l3hbuf"][:, :], pb[:, :], AF.Identity),
            reads=[bank], writes=["l3hbuf"], tag="oev")
        p.dma("gpsimd", XOI[dt * 128:(dt + 1) * 128, tc * 512:(tc + 1) * 512],
              ctx.sb["l3hbuf"][:, :], accum_op=ALU.add,
              reads=["l3hbuf"], writes=["XOUT"], tag="acc")
    emit_l3ffn(ctx, "xn", WG, WU, WD, WGQ, WUQ, WDQ, epi)
    for k in range(KD):
        for tc2 in range(4):
            p.dma("sync", ctx.sb["l3hbuf"][:, 0:512],
                  XOI[k * 128:(k + 1) * 128, tc2 * 512:(tc2 + 1) * 512],
                  reads=["XOUT"], writes=["l3hbuf"], tag="cpl")
            p.dma("sync", XO[k * 128:(k + 1) * 128, tc2 * 512:(tc2 + 1) * 512],
                  ctx.sb["l3hbuf"][:, 0:512], reads=["l3hbuf"],
                  writes=["XOF"], tag="st", final=True)
    p.finalize()
    es.close()
    return nc, packs


def build_l0f():
    E, K, CAP = 8, 2, 4096
    Eloc = 1
    NAT = (T * K) // 128
    nc, p, es, ctx = new_ctx()
    XT = ctx.dram_t("XT", (1024, T), F32, kind="ExternalInput")
    XP = ctx.dram_t("XP", (1024, T), F32, kind="ExternalInput")
    RWD = ctx.dram_t("RWD", (1024, E), F32, kind="ExternalInput")
    WFC1R = ctx.dram_t("WFC1R", (1024, 4096), F32, kind="ExternalInput")
    WFC2R = ctx.dram_t("WFC2R", (4096, 1024), F32, kind="ExternalInput")
    B1 = ctx.dram_t("B1", (128, 32), F32, kind="ExternalInput")
    B2 = ctx.dram_t("B2", (128, 8), F32, kind="ExternalInput")
    RANKIN = ctx.dram_t("RANKIN", (128, 40), F32, kind="ExternalInput")
    XO = ctx.dram_t("XO", (1024, T), F32, kind="ExternalOutput")
    spec = dict(E=E, K=K, CAP=CAP, kind="l0",
                RW=RWD,
                CNTI=ctx.dram_t("CNTI", (1, 16), F32),
                CNTO=ctx.dram_t("CNTO", (8, 16), F32),
                MAPI=ctx.dram_t("MAPI", (E * CAP + 128, 1), F32),
                MAPD=ctx.dram_t("MAPD", (Eloc * CAP, 1), F32),
                BARI=ctx.dram_t("BARI", (1, 128), F32),
                BARO=ctx.dram_t("BARO", (1, 128), F32),
                XAGI=ctx.dram_t("XAGI", (T, 1024), BF16),
                XALL=ctx.dram_t("XALL", (8 * T, 1024), BF16),
                EOI=ctx.dram_t("EOI", (Eloc * CAP, 1024), BF16),
                EOALL=ctx.dram_t("EOALL", (8 * Eloc * CAP, 1024), BF16),
                COMB=ctx.dram_t("COMB", (T, 1024), BF16),
                WFC1=ctx.dram_t("WFC1", (1024, 4096), BF16),
                WFC2=ctx.dram_t("WFC2", (4096, 1024), BF16),
                EADR=None, XP=XP)
    ctx.sbuf("x", (128, KD * T), F32)
    ctx.sbuf("xn", (128, KD * T), BF16)
    ctx.sbuf("e_mr", (128, 2 * T + 2048), F32)
    ctx.sbuf("e_xrt", (128, 256), F32)
    ctx.sbuf("e_sc", (128, 16 * 8 + 128), F32)
    ctx.sbuf("e_lg", (128, 16 * 8), F32)
    ctx.sbuf("e_easg", (128, 16 * 2), F32)
    ctx.sbuf("e_wasg", (128, 16 * 2), F32)
    ctx.sbuf("e_rw", (128, 8 * 8), F32)
    ctx.sbuf("e_eq", (128, 16 * 8), F32)
    ctx.sbuf("e_eq2", (128, 16 * 8), F32)
    ctx.sbuf("e_bf", (128, 2 * 16 * 2), BF16)
    ctx.sbuf("e_easgA", (128, 2 * NAT), F32)
    ctx.sbuf("e_oh4", (128, 4 * 8), F32)
    ctx.sbuf("e_slots", (128, NAT), I32)
    ctx.sbuf("e_slotg", (128, NAT), I32)
    ctx.sbuf("e_wfin", (128, NAT), F32)
    ctx.sbuf("e_rows", (128, 16 * 8), F32)
    ctx.sbuf("e_pc", (128, 32), F32)
    ctx.sbuf("e_pp", (128, 4 * 8), F32)
    ctx.sbuf("e_ioffi", (128, 8), I32)
    ctx.sbuf("e_tokpat", (128, 12), F32)
    ctx.sbuf("e_cnts", (128, 16), F32)
    ctx.sbuf("e_rmask", (128, 1), F32)
    ctx.sbuf("e_xtk", (128, 1024), BF16)
    ctx.sbuf("e_mapf", (128, 128), F32)
    ctx.sbuf("e_mapc", (128, CAP // 128), I32)
    ctx.sbuf("e_dtok", (128, 1024), BF16)
    ctx.sbuf("e_dspf", (128, 8 * 512), BF16)
    ctx.sbuf("e_hsb", (128, 32 * 512), BF16)
    ctx.sbuf("e_eosb", (128, 8 * 512), BF16)
    ctx.sbuf("e_b1", (128, 32), F32)
    ctx.sbuf("e_b2", (128, 8), F32)
    ctx.sbuf("e_combT", (128, T), BF16)
    ctx.sbuf("wbuf", (128, 4096), BF16)
    ctx.sbuf("outf", (128, 512), F32)
    for i in range(7):
        ctx.psum(f"ps{i}")
    ctx.psum("ps7", (128, 1024), BF16)
    register_common_consts(ctx, skip=("mask_sl_neg", "mask_triu_incl",
                                      "mask_tril_strict"))
    register_moe_consts(ctx)
    packs = ctx.pack_consts()
    emit_ln_stream(ctx, XP, "xn", "x", sq_name="outf",
                   row_name="e_mr", row_off=2 * T, stats_name="e_mr")
    for k in range(KD):
        p.dma("sync", ctx.sb["x"][:, k * T:(k + 1) * T],
              XT[k * 128:(k + 1) * 128, :], writes=["x"], tag="ldx")
    p.dma("sync", ctx.sb["e_b1"][:, :], B1[:, :], writes=["e_b1"], tag="ldb1")
    p.dma("sync", ctx.sb["e_b2"][:, :], B2[:, :], writes=["e_b2"], tag="ldb2")
    p.dma("sync", ctx.sb["e_rmask"][:, 0:1], RANKIN[:, 0:1],
          writes=["e_rmask"], allow_slow_non_contiguous=True)
    p.dma("sync", ctx.sb["e_tokpat"][:, 0:12], RANKIN[:, 2:14],
          writes=["e_tokpat"], allow_slow_non_contiguous=True)
    p.dma("gpsimd", spec["WFC1"][:, :], WFC1R[:, :], writes=["WFC1b"], tag="wc1")
    p.dma("gpsimd", spec["WFC2"][:, :], WFC2R[:, :], writes=["WFC2b"], tag="wc2")
    spec["ffn"] = make_moe_dispatch_ffn(ctx, spec)
    emit_moe(ctx, "xn", "x", ctx.sb["e_mr"][0:1, 0:T],
             ctx.sb["e_mr"][0:1, T:2 * T], spec)
    emit_moe_exec(ctx, "xn", "x", spec)
    for k in range(KD):
        for tc in range(4):
            p.dma("sync", XO[k * 128:(k + 1) * 128, tc * 512:(tc + 1) * 512],
                  ctx.sb["x"][:, k * T + tc * 512:k * T + (tc + 1) * 512],
                  reads=["x"], writes=["XOUT"], tag="st", final=True)
    p.finalize()
    es.close()
    return nc, packs


def build_l2f():
    E, K, CAP = 32, 8, 2048
    Eloc = 4
    NAT = (T * K) // 128
    NT = 16
    nc, p, es, ctx = new_ctx()
    XT = ctx.dram_t("XT", (1024, T), F32, kind="ExternalInput")
    XP = ctx.dram_t("XP", (1024, T), F32, kind="ExternalInput")
    RWD = ctx.dram_t("RWD", (1024, E), F32, kind="ExternalInput")
    BIASR = ctx.dram_t("BIASR", (1, E), F32, kind="ExternalInput")
    WGUR = ctx.dram_t("WGUR", (1024, Eloc * 1024), F32, kind="ExternalInput")
    WDNR = ctx.dram_t("WDNR", (512, Eloc * 1024), F32, kind="ExternalInput")
    SGD = ctx.dram_t("SGD", (1024, 2048), F32, kind="ExternalInput")
    SUD = ctx.dram_t("SUD", (1024, 2048), F32, kind="ExternalInput")
    SDTD = ctx.dram_t("SDTD", (2048, 1024), F32, kind="ExternalInput")
    RANKIN = ctx.dram_t("RANKIN", (128, 40), F32, kind="ExternalInput")
    XO = ctx.dram_t("XO", (1024, T), F32, kind="ExternalOutput")
    spec = dict(E=E, K=K, CAP=CAP, kind="l2",
                RW=RWD, BIASR=BIASR,
                CNTI=ctx.dram_t("CNTI", (1, E), F32),
                CNTO=ctx.dram_t("CNTO", (8, E), F32),
                MAPI=ctx.dram_t("MAPI", (E * CAP + 128, 1), F32),
                MAPD=ctx.dram_t("MAPD", (Eloc * CAP, 1), F32),
                BARI=ctx.dram_t("BARI", (1, 128), F32),
                BARO=ctx.dram_t("BARO", (1, 128), F32),
                EADR=ctx.dram_t("EADR", (2 * NT, 128, K), BF16),
                XAGI=ctx.dram_t("XAGI", (T, 1024), BF16),
                XALL=ctx.dram_t("XALL", (8 * T, 1024), BF16),
                EOI=ctx.dram_t("EOI", (Eloc * CAP, 1024), BF16),
                EOALL=ctx.dram_t("EOALL", (8 * Eloc * CAP, 1024), BF16),
                COMB=ctx.dram_t("COMB", (T, 1024), BF16),
                WGU=ctx.dram_t("WGU", (1024, Eloc * 1024), BF16),
                WDN=ctx.dram_t("WDN", (512, Eloc * 1024), BF16), XP=XP)
    SGB = ctx.dram_t("SGB", (1024, 2048), BF16)
    SUB = ctx.dram_t("SUB", (1024, 2048), BF16)
    SDB = ctx.dram_t("SDB", (2048, 1024), BF16)
    ctx.sbuf("x", (128, KD * T), F32)
    ctx.sbuf("xn", (128, KD * T), BF16)
    ctx.sbuf("e_mr", (128, 2 * T), F32)
    ctx.sbuf("ln_sq", (128, 512), F32)
    ctx.sbuf("e_xrt", (128, 256), F32)
    ctx.sbuf("e_sc", (128, 16 * E + 128), F32)
    ctx.sbuf("e_lg", (128, NT * E), F32)
    ctx.sbuf("e_sel", (128, NT * E + E), F32)
    ctx.sbuf("e_easg", (128, NT * K), F32)
    ctx.sbuf("e_wasg", (128, NT * K), F32)
    ctx.sbuf("e_rw", (128, 8 * E), F32)
    ctx.sbuf("e_eq", (128, NT * E), F32)
    ctx.sbuf("e_eq2", (128, NT * E), F32)
    ctx.sbuf("e_bf", (128, 2 * NT * K), BF16)
    ctx.sbuf("e_easgA", (128, 2 * NAT), BF16)
    ctx.sbuf("e_oh4", (128, 4 * E), F32)
    ctx.sbuf("e_slots", (128, NAT), I32)
    ctx.sbuf("e_slotg", (128, NAT), I32)
    ctx.sbuf("e_wfin", (128, NAT), F32)
    ctx.sbuf("e_rows", (128, 16 * E), F32)
    ctx.sbuf("e_pc", (128, 32), F32)
    ctx.sbuf("e_pp", (128, 4 * E), F32)
    ctx.sbuf("e_ioffi", (128, 8), I32)
    ctx.sbuf("e_tokpat", (128, 12), F32)
    ctx.sbuf("e_cnts", (128, E), F32)
    ctx.sbuf("e_rmask", (128, 1), F32)
    ctx.sbuf("e_xtk", (128, 1024), BF16)
    ctx.sbuf("e_mapf", (128, 128), F32)
    ctx.sbuf("e_mapc", (128, CAP // 128), I32)
    ctx.sbuf("e_dtok", (128, 1024), BF16)
    ctx.sbuf("e_dspf", (128, 8 * 512), BF16)
    ctx.sbuf("e_hsb", (128, 16 * 512), BF16)
    ctx.sbuf("e_eosb", (128, 8 * 512), BF16)
    ctx.sbuf("e_gbuf", (128, 4 * 512), F32)
    ctx.sbuf("e_gw", (128, 16), BF16)
    ctx.sbuf("e_stg", (128, 1024), BF16)
    ctx.sbuf("e_gath2", (128, 1024), BF16)
    ctx.sbuf("e_combT", (128, T), BF16)
    ctx.sbuf("wbuf", (128, 8192), BF16)
    for i in range(7):
        ctx.psum(f"ps{i}")
    ctx.psum("ps7", (128, 1024), BF16)
    register_common_consts(ctx, skip=("mask_sl_neg", "mask_triu_incl",
                                      "mask_tril_strict"))
    register_moe_consts(ctx)
    packs = ctx.pack_consts()
    emit_ln_stream(ctx, XP, "xn", "x", sq_name="ln_sq",
                   row_name="e_gbuf", stats_name="e_mr")
    for k in range(KD):
        p.dma("sync", ctx.sb["x"][:, k * T:(k + 1) * T],
              XT[k * 128:(k + 1) * 128, :], writes=["x"], tag="ldx")
    p.dma("sync", ctx.sb["e_rmask"][:, 0:1], RANKIN[:, 0:1],
          writes=["e_rmask"], allow_slow_non_contiguous=True)
    p.dma("sync", ctx.sb["e_tokpat"][:, 0:12], RANKIN[:, 2:14],
          writes=["e_tokpat"], allow_slow_non_contiguous=True)
    p.dma("gpsimd", spec["WGU"][:, :], WGUR[:, :], writes=["WGUb"], tag="wcg")
    p.dma("gpsimd", spec["WDN"][:, :], WDNR[:, :], writes=["WDNb"], tag="wcd")
    spec["ffn"] = make_moe_dispatch_ffn(ctx, spec)
    emit_moe(ctx, "xn", "x", ctx.sb["e_mr"][0:1, 0:T],
             ctx.sb["e_mr"][0:1, T:2 * T], spec)
    emit_moe_exec(ctx, "xn", "x", spec)
    emit_l2shared(ctx, "xn", "x", SGD, SUD, SDTD, SGB=SGB, SUB=SUB, SDB=SDB)
    for k in range(KD):
        for tc in range(4):
            p.dma("sync", XO[k * 128:(k + 1) * 128, tc * 512:(tc + 1) * 512],
                  ctx.sb["x"][:, k * T + tc * 512:k * T + (tc + 1) * 512],
                  reads=["x"], writes=["XOUT"], tag="st", final=True)
    p.finalize()
    es.close()
    return nc, packs


_BUILDS = {}


def _get_build(name):
    if name not in _BUILDS:
        print("KBUILD", name, flush=True)
        _BUILDS[name] = {
            "l0a": build_l0a, "l0f": build_l0f, "l1a": build_l1a,
            "l1f": build_l1f, "l2a": lambda: build_attn(False),
            "l3a": lambda: build_attn(True), "l2f": build_l2f,
            "l3f": build_l3f}[name]()
    return _BUILDS[name]


def _prep_mamba_cols(ins):
    C = np.zeros((128, 163), np.float32)
    cwr = np.asarray(ins["L0_attn_conv_w"], np.float32)
    for ct in range(24):
        for kk in range(4):
            C[:, ct * 4 + kk] = cwr[ct * 128:(ct + 1) * 128, kk]
    cb = ins["L0_attn_conv_b"]
    for ct in range(24):
        C[:, 96 + ct] = cb[ct * 128:(ct + 1) * 128]
    C[0:32, 120] = ins["L0_attn_dt_bias"]
    C[0:32, 121] = ins["L0_attn_A_log"]
    D = ins["L0_attn_D"]
    for mt in range(16):
        C[:, 122 + mt] = D[(mt * 128 + np.arange(128)) // 64]
    gnw = ins["L0_attn_gate_norm_w"]
    for mt in range(16):
        C[:, 138 + mt] = gnw[mt * 128:(mt + 1) * 128]
    nw = ins["L0_attn_norm_w"]
    for k in range(8):
        C[:, 154 + k] = nw[k * 128:(k + 1) * 128]
    return C


def _rankin(r):
    p_arr = np.arange(128)
    R = np.zeros((128, 40), np.float32)
    R[0:8, 0] = (np.arange(8) < r).astype(np.float32)
    for b in range(4):
        R[:, 2 + b] = r * 2048 + b * 16 + p_arr // 8
        R[:, 6 + b] = r * 2048 + p_arr + b * 128
    return R


def kernel(**ins):
    x = np.asarray(ins["L0_attn_hidden_states"], np.float32)
    xT = [np.ascontiguousarray(x[c].T) for c in range(8)]

    def run(name, extra_fn):
        nonlocal xT
        nc, packs = _get_build(name)
        in_maps = []
        for c in range(8):
            d = {"XT": xT[c]}
            d.update(extra_fn(c))
            d.update(packs)
            in_maps.append(d)
        res = run_bass_kernel_spmd(nc, in_maps, core_ids=list(range(8)))
        xT = [np.ascontiguousarray(res.results[c]["XO"]) for c in range(8)]

    mcols = _prep_mamba_cols(ins)
    xprev = xT
    run("l0a", lambda c: {"INPJ": ins["L0_attn_in_proj_w"],
                          "OUTPJ": ins["L0_attn_out_proj_w"],
                          "COLS": mcols})
    run("l0f", lambda c: {"XP": xprev[c],
        "RWD": np.ascontiguousarray(ins["L0_ffn_gate_w"].T.astype(np.float32)),
        "WFC1R": np.ascontiguousarray(ins["L0_ffn_fc1_w"][c].T.astype(np.float32)),
        "WFC2R": np.ascontiguousarray(ins["L0_ffn_fc2_w"][c].astype(np.float32)),
        "B1": np.ascontiguousarray(
            ins["L0_ffn_fc1_b"][c, 0].astype(np.float32).reshape(32, 128).T),
        "B2": np.ascontiguousarray(
            ins["L0_ffn_fc2_b"][c, 0].astype(np.float32).reshape(8, 128).T),
        "RANKIN": _rankin(c)})
    xprev = xT
    lng = np.concatenate([np.asarray(ins["L1_attn_ln_g"], np.float32),
                          np.asarray(ins["L1_attn_ln_b"], np.float32)])[None, :]
    run("l1a", lambda c: {
        "WQ": np.ascontiguousarray(ins["L1_attn_Wq"].astype(np.float32)),
        "WK": np.ascontiguousarray(ins["L1_attn_Wk"].astype(np.float32)),
        "WV": np.ascontiguousarray(ins["L1_attn_Wv"].astype(np.float32)),
        "WB": np.ascontiguousarray(
            ins["L1_attn_Wb"].astype(np.float32).reshape(8, 128).T),
        "LNG": np.ascontiguousarray(lng)})
    MU = np.zeros((128, 16), np.float32)
    MU[:, 0:8] = np.asarray(ins["L1_ffn_mu_r"], np.float32).reshape(8, 128).T
    MU[:, 8:16] = np.asarray(ins["L1_ffn_mu_k"], np.float32).reshape(8, 128).T
    run("l1f", lambda c: {"XP": xprev[c],
        "WR": np.ascontiguousarray(ins["L1_ffn_W_r"].astype(np.float32).T),
        "WK": np.ascontiguousarray(ins["L1_ffn_W_k"].astype(np.float32).T),
        "WV": np.ascontiguousarray(ins["L1_ffn_W_v"].astype(np.float32).T),
        "MU": MU})
    xprev = xT
    wqkv = np.asarray(ins["L2_attn_w_qkv"], np.float32)
    idx_q = np.concatenate([np.arange(h * 192, h * 192 + 64) for h in range(16)])
    run("l2a", lambda c: {
        "WQ": np.ascontiguousarray(wqkv[:, idx_q]),
        "WK": np.ascontiguousarray(wqkv[:, idx_q + 64]),
        "WV": np.ascontiguousarray(wqkv[:, idx_q + 128]),
        "WO": np.ascontiguousarray(np.asarray(ins["L2_attn_w_o"], np.float32)),
        "BIAS": np.zeros((128, 32), np.float32)})
    gup = np.asarray(ins["L2_ffn_gate_up_proj"], np.float32)
    dnp = np.asarray(ins["L2_ffn_down_proj"], np.float32)
    run("l2f", lambda c: {"XP": xprev[c],
        "RWD": np.ascontiguousarray(
            np.asarray(ins["L2_ffn_router_weight"], np.float32).T),
        "BIASR": np.asarray(
            ins["L2_ffn_e_score_correction_bias"], np.float32)[None, :],
        "WGUR": np.ascontiguousarray(np.concatenate(
            [gup[c * 4 + le].T for le in range(4)], axis=1)),
        "WDNR": np.ascontiguousarray(np.concatenate(
            [dnp[c * 4 + le].T for le in range(4)], axis=1)),
        "SGD": np.ascontiguousarray(
            np.asarray(ins["L2_ffn_shared_gate"], np.float32).T),
        "SUD": np.ascontiguousarray(
            np.asarray(ins["L2_ffn_shared_up"], np.float32).T),
        "SDTD": np.ascontiguousarray(
            np.asarray(ins["L2_ffn_shared_down"], np.float32).T),
        "RANKIN": _rankin(c)})
    xprev = xT
    BIAS = np.zeros((128, 32), np.float32)
    for i, nm in enumerate(["bq", "bk", "bv", "bp"]):
        BIAS[:, i * 8:(i + 1) * 8] = np.asarray(
            ins[f"L3_attn_{nm}"], np.float32).reshape(8, 128).T
    run("l3a", lambda c: {
        "WQ": np.ascontiguousarray(np.asarray(ins["L3_attn_Wq"], np.float32).T),
        "WK": np.ascontiguousarray(np.asarray(ins["L3_attn_Wk"], np.float32).T),
        "WV": np.ascontiguousarray(np.asarray(ins["L3_attn_Wv"], np.float32).T),
        "WO": np.ascontiguousarray(np.asarray(ins["L3_attn_Wp"], np.float32).T),
        "BIAS": BIAS})
    run("l3f", lambda c: {"XP": xprev[c],
        "WG": np.ascontiguousarray(np.asarray(ins["L3_ffn_w_gate"], np.float32)),
        "WU": np.ascontiguousarray(np.asarray(ins["L3_ffn_w_up"], np.float32)),
        "WD": np.ascontiguousarray(np.asarray(ins["L3_ffn_w_down"], np.float32))})
    out = np.stack([xT[c].T for c in range(8)]).astype(np.float32)
    return out


# version 24
# speedup vs baseline: 1.0194x; 1.0194x over previous
import numpy as np
import ml_dtypes
from contextlib import ExitStack
import concourse.bass as bass
import concourse.mybir as mybir
from concourse.bass_utils import run_bass_kernel_spmd

import numpy as np
import concourse.bass as bass
import concourse.mybir as mybir

F32 = mybir.dt.float32
BF16 = mybir.dt.bfloat16
F16 = mybir.dt.float16
I32 = mybir.dt.int32
U8 = mybir.dt.uint8
AF = mybir.ActivationFunctionType
ALU = mybir.AluOpType
AX = mybir.AxisListType
DS = bass.ds
TS = bass.ts
IOffs = bass.IndirectOffsetOnAxis

ENGINES = ("tensor", "vector", "scalar", "gpsimd", "sync")


class Prog:
    def __init__(self, nc):
        self.nc = nc
        self.ops = []
        self.last_write = {}
        self.readers = {}
        self.final_ops = []

    def op(self, engine, fn, reads=(), writes=(), kind="compute", tag=""):
        deps = set()
        for r in reads:
            w = self.last_write.get(r)
            if w is not None:
                deps.add(w)
        for w in writes:
            lw = self.last_write.get(w)
            if lw is not None:
                deps.add(lw)
            for rd in self.readers.get(w, ()):
                deps.add(rd)
        opid = len(self.ops)
        deps.discard(opid)
        self.ops.append(dict(engine=engine, fn=fn, deps=deps, kind=kind, tag=tag))
        for r in reads:
            self.readers.setdefault(r, []).append(opid)
        for w in writes:
            self.last_write[w] = opid
            self.readers[w] = []
        return opid

    def dma(self, queue, out, in_, reads=(), writes=(), tag="", final=False, **kw):
        assert queue in ("sync", "scalar", "gpsimd")
        i = self.op(queue, lambda e: e.dma_start(out=out, in_=in_, **kw),
                    reads=reads, writes=writes, kind="dma", tag=tag)
        if final:
            self.final_ops.append(i)
        return i

    def finalize(self, trace=None, paranoid=False, paranoid_tags=()):
        nc = self.nc
        ops = self.ops
        if paranoid:
            for i, o in enumerate(ops):
                o["deps"] = set(range(i))
        elif paranoid_tags:
            for i, o in enumerate(ops):
                if o["tag"] in paranoid_tags:
                    o["deps"] = set(range(i))

        def semkey(o):
            return (o["engine"],
                    "dma" if o["kind"] == "dma" else
                    "coll" if o["kind"] == "coll" else "c")

        need_inc = [False] * len(ops)
        for o in ops:
            for d in o["deps"]:
                if ops[d]["engine"] == "tensor" and o["engine"] == "tensor":
                    continue
                need_inc[d] = True
        for i in self.final_ops:
            need_inc[i] = True
        for i, o in enumerate(ops):
            if o["kind"] in ("dma", "coll"):
                need_inc[i] = True

        semval = [None] * len(ops)
        cum = {}
        for i, o in enumerate(ops):
            if need_inc[i]:
                k = semkey(o)
                cum[k] = cum.get(k, 0) + (16 if o["kind"] == "dma" else 1)
                semval[i] = cum[k]

        sem_keys = sorted(set(semkey(ops[i]) for i in range(len(ops)) if need_inc[i]))
        sem_cms = [nc.semaphore(f"sem_{k[0]}_{k[1]}") for k in sem_keys]
        sems = {k: cm.__enter__() for k, cm in zip(sem_keys, sem_cms)}

        eng_ids = {e: [i for i, o in enumerate(ops) if o["engine"] == e]
                   for e in ENGINES}
        final_by_eng = {}
        for i in self.final_ops:
            k = semkey(ops[i])
            final_by_eng[k] = max(final_by_eng.get(k, 0), semval[i])

        def make_runner(e):
            def runner(eng_obj):
                frontier = {}
                for i in eng_ids[e]:
                    o = ops[i]
                    waits = {}
                    for d in sorted(o["deps"]):
                        p = ops[d]
                        if p["engine"] == "tensor" and e == "tensor":
                            continue
                        k = semkey(p)
                        v = semval[d]
                        assert v is not None, (o["tag"], p["tag"])
                        if waits.get(k, 0) < v:
                            waits[k] = v
                    for k, v in sorted(waits.items()):
                        if frontier.get(k, 0) >= v:
                            continue
                        eng_obj.wait_ge(sems[k], v)
                        frontier[k] = v
                        if trace is not None:
                            trace.append((e, "WAIT", k, v, o["tag"]))
                    try:
                        inst = o["fn"](eng_obj)
                    except Exception:
                        import traceback
                        traceback.print_exc()
                        raise
                    if need_inc[i]:
                        k = semkey(o)
                        inst.then_inc(sems[k], 16 if o["kind"] == "dma" else 1)
                        if trace is not None:
                            trace.append((e, "OP+INC", k, semval[i], o["tag"]))
                    elif trace is not None:
                        trace.append((e, "OP", None, None, o["tag"]))
                for k, v in sorted(final_by_eng.items()):
                    if frontier.get(k, 0) < v:
                        eng_obj.wait_ge(sems[k], v)
            return runner

        block_cm = nc.Block()
        block = block_cm.__enter__()
        for e in ENGINES:
            if eng_ids[e] or any(k[0] == e for k in final_by_eng):
                getattr(block, e)(make_runner(e))
        block_cm.__exit__(None, None, None)
        for cm in sem_cms:
            cm.__exit__(None, None, None)
        return dict(n_ops=len(ops),
                    n_incs=sum(need_inc),
                    per_engine={e: len(v) for e, v in eng_ids.items()})


"""Layer emitters for the mk_prb_d4 megakernel. Feature-major convention:
activations live in SBUF as [128 part, K*T free] where column block k holds
features [128k,128k+128) for all T tokens: sb[p, k*T+t] = act[t, 128k+p].
"""
import numpy as np

T = 2048
KD = 8
NCH = 4


class Ctx:
    def __init__(self, nc, prog, es):
        self.nc = nc
        self.p = prog
        self.es = es
        self.sb = {}
        self.ps = {}
        self.consts = {}
        self._cf32 = []
        self._ci32 = []
        self.dram = {}

    def sbuf(self, name, shape, dtype):
        h = self.es.enter_context(self.nc.sbuf_tensor(name, list(shape), dtype))
        self.sb[name] = h
        return h

    def psum(self, name, shape=(128, 512), dtype=F32):
        h = self.es.enter_context(self.nc.psum_tensor(name, list(shape), dtype))
        self.ps[name] = h
        return h

    def dram_t(self, name, shape, dtype, kind="Internal", **kw):
        h = self.nc.dram_tensor(name, list(shape), dtype, kind=kind, **kw).ap()
        self.dram[name] = h
        return h

    def const_f32(self, name, arr):
        arr = np.asarray(arr, np.float32)
        assert arr.shape[0] == 128
        self.consts[("f32", name)] = arr
        return name

    def const_i32(self, name, arr):
        arr = np.asarray(arr, np.int32)
        assert arr.shape[0] == 128
        self.consts[("i32", name)] = arr
        return name

    def const_bf16(self, name, arr):
        import ml_dtypes
        arr = np.asarray(arr, ml_dtypes.bfloat16)
        assert arr.shape[0] == 128
        self.consts[("bf16", name)] = arr
        return name

    def pack_consts(self):
        packs = {}
        for tag, dt in (("f32", F32), ("i32", I32), ("bf16", BF16)):
            items = [(k[1], v) for k, v in self.consts.items() if k[0] == tag]
            if not items:
                continue
            tot = sum(v.shape[1] for _, v in items)
            dname = "CONST_" + tag
            d = self.dram_t(dname, (128, tot), dt, kind="ExternalInput")
            s = self.sbuf(dname + "_sb", (128, tot), dt)
            self.p.dma("sync", s[:, :], d[:, :], writes=["consts"], tag="consts")
            off = 0
            self.caps = getattr(self, "caps", {})
            for name, v in items:
                self.caps[name] = s[:, off:off + v.shape[1]]
                off += v.shape[1]
            packs[dname] = np.concatenate([v for _, v in items], axis=1)
        self.const_inputs = packs
        return packs

    def cap(self, name):
        return self.caps[name]


def register_common_consts(ctx, skip=()):
    i = np.arange(128)
    reg = {
        "ident": lambda: ctx.const_f32("ident", np.eye(128)),
        "ones_col_bf": lambda: ctx.const_bf16("ones_col_bf", np.ones((128, 2))),
        "ident_bf": lambda: ctx.const_bf16("ident_bf", np.eye(128)),
        "ones_col": lambda: ctx.const_f32("ones_col", np.ones((128, 1))),
        "ones_row": lambda: ctx.const_f32("ones_row", np.ones((128, 128))),
        "mask_sl_neg": lambda: ctx.const_f32(
            "mask_sl_neg", np.where(i[:, None] > i[None, :], -1e30, 0.0)),
        "mask_triu_incl": lambda: ctx.const_f32(
            "mask_triu_incl", (i[:, None] <= i[None, :]).astype(np.float32)),
        "mask_triu_strict": lambda: ctx.const_f32(
            "mask_triu_strict", (i[:, None] < i[None, :]).astype(np.float32)),
        "mask_tril_strict": lambda: ctx.const_f32(
            "mask_tril_strict", (i[:, None] > i[None, :]).astype(np.float32)),
        "iota_row": lambda: ctx.const_f32(
            "iota_row", np.broadcast_to(i[None, :], (128, 128)).copy()),
        "eps_col": lambda: ctx.const_f32("eps_col", np.full((128, 1), 1e-5)),
        "zero_col": lambda: ctx.const_f32("zero_col", np.zeros((128, 1))),
    }
    for k, fn in reg.items():
        if k not in skip:
            fn()


def emit_ln(ctx, x_name, xn_name, sq_name="ln_sq", row_name="ln_rows",
            banks=("ps0", "ps1", "ps2", "ps3"), stats_name=None):
    p = ctx.p
    x = ctx.sb[x_name]
    xn = ctx.sb[xn_name]
    sq = ctx.sb[sq_name]
    rows = ctx.sb[row_name]
    ones_col = ctx.cap("ones_col")[:, 0:1]
    ones_row = ctx.cap("ones_row")[0:1, :]
    eps_col = ctx.cap("eps_col")
    TC = T // NCH
    pm, pq, pmb, prb = (ctx.ps[b] for b in banks)
    kpm, kpq, kmb, krb = banks
    for tc in range(NCH):
        for k in range(KD):
            xs = x[:, k * T + tc * TC: k * T + (tc + 1) * TC]
            p.op("tensor", lambda e, xs=xs, k=k: e.matmul(
                pm[0:1, :], ones_col, xs, start=(k == 0), stop=(k == KD - 1)),
                reads=[x_name, "consts"], writes=[kpm], tag="ln_sum")
            p.op("scalar", lambda e, xs=xs: e.activation(sq[:, :], xs, AF.Square),
                 reads=[x_name], writes=[sq_name], tag="ln_sq")
            p.op("tensor", lambda e, k=k: e.matmul(
                pq[0:1, :], ones_col, sq[:, :], start=(k == 0), stop=(k == KD - 1)),
                reads=[sq_name, "consts"], writes=[kpq], tag="ln_sum2")
        m = rows[0:1, 0:TC]
        s2 = rows[0:1, TC:2 * TC]
        r = rows[0:1, 2 * TC:3 * TC]
        mr = rows[0:1, 3 * TC:4 * TC]
        p.op("vector", lambda e, m=m: e.tensor_scalar(
            m, pm[0:1, :], 1.0 / 1024, None, ALU.mult),
            reads=[kpm], writes=[row_name], tag="ln_m")
        p.op("vector", lambda e, s2=s2: e.tensor_scalar(
            s2, pq[0:1, :], 1.0 / 1024, None, ALU.mult),
            reads=[kpq], writes=[row_name], tag="ln_s2")
        p.op("vector", lambda e, m=m, mr=mr: e.tensor_tensor(
            mr, m, m, ALU.mult), reads=[row_name], writes=[row_name], tag="ln_msq")
        p.op("vector", lambda e, s2=s2, mr=mr: e.tensor_tensor(
            s2, s2, mr, ALU.subtract), reads=[row_name], writes=[row_name], tag="ln_v")
        p.op("scalar", lambda e, s2=s2, r=r: e.activation(
            r, s2, AF.Sqrt, bias=eps_col[0:1, 0:1]),
            reads=[row_name, "consts"], writes=[row_name], tag="ln_sqrt")
        p.op("vector", lambda e, r=r: e.reciprocal(r, r),
             reads=[row_name], writes=[row_name], tag="ln_rstd")
        if stats_name is not None:
            st = ctx.sb[stats_name]
            p.op("vector", lambda e, m=m, tc=tc: e.tensor_copy(
                st[0:1, tc * TC:(tc + 1) * TC], m),
                reads=[row_name], writes=[stats_name], tag="ln_stm")
            p.op("vector", lambda e, r=r, tc=tc: e.tensor_copy(
                st[0:1, T + tc * TC:T + (tc + 1) * TC], r),
                reads=[row_name], writes=[stats_name], tag="ln_str")
        p.op("vector", lambda e, m=m, r=r, mr=mr: e.tensor_tensor(
            mr, m, r, ALU.mult), reads=[row_name], writes=[row_name], tag="ln_mr")
        p.op("vector", lambda e, mr=mr: e.tensor_scalar(
            mr, mr, -1.0, None, ALU.mult), reads=[row_name], writes=[row_name],
            tag="ln_mrneg")
        p.op("tensor", lambda e, mr=mr: e.matmul(pmb[:, :], ones_row, mr,
             start=True, stop=True), reads=[row_name, "consts"],
             writes=[kmb], tag="ln_bcast_m")
        p.op("tensor", lambda e, r=r: e.matmul(prb[:, :], ones_row, r,
             start=True, stop=True), reads=[row_name, "consts"],
             writes=[krb], tag="ln_bcast_r")
        for k in range(KD):
            xs = x[:, k * T + tc * TC: k * T + (tc + 1) * TC]
            xns = xn[:, k * T + tc * TC: k * T + (tc + 1) * TC]
            p.op("vector", lambda e, xs=xs: e.tensor_tensor(
                sq[:, :], xs, prb[:, :], ALU.mult),
                reads=[x_name, krb], writes=[sq_name], tag="ln_n1")
            p.op("vector", lambda e, xns=xns: e.tensor_tensor(
                xns, sq[:, :], pmb[:, :], ALU.add),
                reads=[sq_name, kmb], writes=[xn_name], tag="ln_n2")


def emit_proj(ctx, dst_name, src_name, w_dram, K_tiles, M_tiles,
              wslot="wbuf", banks=("ps0", "ps1", "ps2", "ps3"),
              w_row0=0, w_col0=0, src_t0=0, dst_t0=0,
              bias=None, act=None, scale=None, dst_dt_bf=True,
              add_to=None, add_bank="ps4", n_tok=T, wq="gpsimd",
              w_name=None, epilogue=None):
    p = ctx.p
    src = ctx.sb[src_name]
    dst = ctx.sb[dst_name] if dst_name else None
    wsl = ctx.sb[wslot]
    wcols = K_tiles * 128
    M_total = M_tiles * 128
    nch = n_tok // 512
    ps = [ctx.ps[b] for b in banks]
    for m in range(M_tiles):
        slot = m % 2
        wkey = f"{wslot}{slot}"
        wv = wsl[:, slot * wcols:(slot + 1) * wcols]
        src_ap = w_dram[w_row0:w_row0 + K_tiles * 128,
                        w_col0 + m * 128: w_col0 + (m + 1) * 128]
        p.dma("gpsimd", wv, src_ap.rearrange("(k p) c -> p k c", p=128),
              reads=[], writes=[wkey], tag=f"w_{dst_name}_{m}")
        for k in range(K_tiles):
            lhsT = wv[:, k * 128:(k + 1) * 128]
            for tc in range(nch):
                rhs = src[:, src_t0 + k * T + tc * 512: src_t0 + k * T + tc * 512 + 512]
                p.op("tensor", lambda e, lhsT=lhsT, rhs=rhs, k=k, tc=tc: e.matmul(
                    ps[tc][:, :], lhsT, rhs,
                    start=(k == 0), stop=(k == K_tiles - 1)),
                    reads=[wkey, src_name], writes=[banks[tc]],
                    tag=f"mm_{dst_name}")
        for tc in range(nch):
            pb = ps[tc]
            if epilogue is not None:
                epilogue(m, tc, pb, banks[tc])
                continue
            if add_to is not None:
                tgt = ctx.sb[add_to]
                out_ap = tgt[:, dst_t0 + m * T + tc * 512: dst_t0 + m * T + tc * 512 + 512]
                p.op("vector", lambda e, out_ap=out_ap, pb=pb: e.tensor_tensor(
                    out_ap, out_ap, pb[:, :], ALU.add),
                    reads=[banks[tc], add_to], writes=[add_to],
                    tag=f"acc_{add_to}")
                continue
            out_ap = dst[:, dst_t0 + m * T + tc * 512: dst_t0 + m * T + tc * 512 + 512]
            kw = {}
            if bias is not None:
                bname, bget = bias
                kw["bias"] = bget(m)
                rd = [banks[tc], bname]
            else:
                rd = [banks[tc]]
            if scale is not None:
                kw["scale"] = scale
            fn = act if act is not None else AF.Identity
            p.op("scalar", lambda e, out_ap=out_ap, pb=pb, fn=fn, kw=dict(kw): e.activation(
                out_ap, pb[:, :], fn, **kw),
                reads=rd, writes=[dst_name], tag=f"ev_{dst_name}")


def register_attn_consts(ctx):
    i = np.arange(128)[:, None]
    j = np.arange(512)[None, :]
    for r in range(4):
        ctx.const_f32(f"cmask{r}", np.where(128 * r + i > j, -1e30, 0.0))


def emit_mha(ctx, xn_name, out_name, wspecs, att_scale, biases=None,
             qb="mha_q", kb="mha_k", vb="mha_v", vtb="mha_vt",
             probs=("mha_pA", "mha_pB"), den="mha_den",
             banks=("ps0", "ps1", "ps2", "ps3", "ps4", "ps5", "ps6", "ps7")):
    p = ctx.p
    ident = ctx.cap("ident")
    ones_col = ctx.cap("ones_col_bf")[:, 0:1]
    ones_row = ctx.cap("ones_row")[0:1, :]
    b_tr, b_sc0, b_sc1, b_out, b_den = banks[4], banks[5], banks[6], banks[7], banks[0]
    ps_tr, ps_sc0, ps_sc1, ps_out, ps_den = (ctx.ps[b] for b in
                                             (b_tr, b_sc0, b_sc1, b_out, b_den))
    for hg in range(2):
        for name, buf, dt_scale, bidx in (("q", qb, att_scale, 0),
                                          ("k", kb, None, 1),
                                          ("v", vb, None, 2)):
            dram, col0 = wspecs[bidx]
            bias = None
            if biases is not None:
                bsb, boffs = biases
                bias = (bsb, lambda m, bo=boffs[bidx], hg=hg:
                        ctx.sb[bsb][:, bo + hg * 4 + m: bo + hg * 4 + m + 1])
            emit_proj(ctx, buf, xn_name, dram, 8, 4,
                      wslot="wbuf", banks=banks[0:4],
                      w_col0=col0 + hg * 512, bias=bias,
                      scale=dt_scale)
        for hl in range(8):
            base = (hl % 2) * 64
            vt = ctx.sb[vtb]
            for kt in range(16):
                src = ctx.sb[vb][base:base + 64,
                                 (hl // 2) * T + kt * 128:(hl // 2) * T + (kt + 1) * 128]
                idn = ident[base:base + 64, base:base + 64]
                p.op("tensor", lambda e, src=src, idn=idn: e.transpose(
                    ps_tr[:, 0:64], src, idn),
                    reads=[vb, "consts"], writes=[b_tr], tag="mha_tr")
                p.op("scalar", lambda e, kt=kt: e.activation(
                    vt[:, kt * 64:(kt + 1) * 64], ps_tr[:, 0:64], AF.Identity),
                    reads=[b_tr], writes=[vtb], tag="mha_vt_ev")
            for qc in range(4):
                for kt in range(4 * qc + 4):
                    psc = ps_sc0 if kt % 2 == 0 else ps_sc1
                    bsc = b_sc0 if kt % 2 == 0 else b_sc1
                    lhsT = ctx.sb[kb][base:base + 64,
                                      (hl // 2) * T + kt * 128:(hl // 2) * T + (kt + 1) * 128]
                    rhs = ctx.sb[qb][base:base + 64,
                                     (hl // 2) * T + qc * 512:(hl // 2) * T + (qc + 1) * 512]
                    p.op("tensor", lambda e, lhsT=lhsT, rhs=rhs, psc=psc: e.matmul(
                        psc[:, :], lhsT, rhs, start=True, stop=True),
                        reads=[kb, qb], writes=[bsc], tag="mha_qk")
                    r = kt - 4 * qc
                    if r >= 0:
                        cm = ctx.cap(f"cmask{r}")
                        p.op("vector", lambda e, psc=psc, cm=cm: e.tensor_tensor(
                            psc[:, :], psc[:, :], cm, ALU.add),
                            reads=[bsc, "consts"], writes=[bsc], tag="mha_mask")
                    pr = ctx.sb[probs[kt % 2]]
                    prk = probs[kt % 2]
                    p.op("scalar", lambda e, psc=psc, pr=pr: e.activation(
                        pr[:, :], psc[:, :], AF.Exp),
                        reads=[bsc], writes=[prk], tag="mha_exp")
                    p.op("tensor", lambda e, kt=kt, pr=pr: e.matmul(
                        ps_out[0:64, :], vt[:, kt * 64:(kt + 1) * 64], pr[:, :],
                        start=(kt == 0), stop=(kt == 4 * qc + 3)),
                        reads=[vtb, prk], writes=[b_out], tag="mha_pv")
                    p.op("tensor", lambda e, kt=kt, pr=pr, qc=qc: e.matmul(
                        ps_den[64:65, :], ones_col, pr[:, :],
                        start=(kt == 0), stop=(kt == 4 * qc + 3)),
                        reads=["consts", prk], writes=[b_den], tag="mha_sum")
                dn = ctx.sb[den]
                p.op("vector", lambda e: e.reciprocal(dn[0:1, 0:512], ps_den[64:65, :]),
                     reads=[b_den], writes=[den], tag="mha_recip")
                p.op("tensor", lambda e: e.matmul(ps_den[0:64, :], ones_row[0:1, 0:64],
                     dn[0:1, 0:512], start=True, stop=True),
                     reads=[den, "consts"], writes=[b_den], tag="mha_denb")
                p.op("scalar", lambda e: e.activation(dn[0:64, 512:1024],
                     ps_out[0:64, :], AF.Identity),
                     reads=[b_out], writes=[den], tag="mha_oev")
                h = hg * 8 + hl
                osl = ctx.sb[out_name][(h % 2) * 64:(h % 2) * 64 + 64,
                                       (h // 2) * T + qc * 512:(h // 2) * T + (qc + 1) * 512]
                p.op("vector", lambda e, osl=osl: e.tensor_tensor(
                    osl, dn[0:64, 512:1024], ps_den[0:64, :], ALU.mult),
                    reads=[den, b_den], writes=[out_name], tag="mha_onorm")


def emit_l1ffn(ctx, xn_name, dst_name, Wr, Wk, Wv, mu_name="l1_mu",
               xr="l1_xr", xk="l1_xk", dbuf="l1_d", rbuf="l1_r",
               banks=("ps0", "ps1", "ps2", "ps3")):
    p = ctx.p
    xn = ctx.sb[xn_name]
    mu = ctx.sb[mu_name]
    d = ctx.sb[dbuf]
    for k in range(KD):
        xs = xn[:, k * T:(k + 1) * T]
        p.op("vector", lambda e, xs=xs: e.tensor_tensor(
            d[:, 1:T], xs[:, 1:T], xs[:, 0:T - 1], ALU.subtract),
            reads=[xn_name], writes=[dbuf], tag="l1_d")
        p.op("vector", lambda e, xs=xs: e.tensor_copy(d[:, 0:1], xs[:, 0:1]),
             reads=[xn_name], writes=[dbuf], tag="l1_d0")
        for which, buf, mc in (("r", xr, k), ("k", xk, 8 + k)):
            xb = ctx.sb[buf]
            p.op("vector", lambda e, mc=mc: e.tensor_scalar(
                ctx.sb[rbuf][:, 0:T], d[:, 0:T], mu[:, mc:mc + 1], None, ALU.mult),
                reads=[dbuf, mu_name], writes=[rbuf], tag="l1_dm")
            p.op("vector", lambda e, xs=xs, xb=xb, k=k: e.tensor_tensor(
                xb[:, k * T + 1:(k + 1) * T], ctx.sb[rbuf][:, 1:T],
                xs[:, 0:T - 1], ALU.add),
                reads=[rbuf, xn_name], writes=[buf], tag="l1_xr")
            p.op("vector", lambda e, xb=xb, k=k: e.tensor_copy(
                xb[:, k * T:k * T + 1], ctx.sb[rbuf][:, 0:1]),
                reads=[rbuf], writes=[buf], tag="l1_xr0")
    emit_proj(ctx, rbuf + "big", xr, Wr, 8, 8, wslot="wbuf", banks=banks,
              act=AF.Sigmoid)
    emit_proj(ctx, xk + "o", xk, Wk, 8, 8, wslot="wbuf", banks=banks,
              act=AF.Relu)
    ko = ctx.sb[xk + "o"]
    for k in range(KD):
        p.op("vector", lambda e, k=k: e.tensor_tensor(
            ko[:, k * T:(k + 1) * T], ko[:, k * T:(k + 1) * T],
            ko[:, k * T:(k + 1) * T], ALU.mult),
            reads=[xk + "o"], writes=[xk + "o"], tag="l1_sq")
    rb = ctx.sb[rbuf + "big"]
    dst = ctx.sb[dst_name]

    def epi(m, tc, pb, bank):
        osl = dst[:, m * T + tc * 512:m * T + (tc + 1) * 512]
        rsl = rb[:, m * T + tc * 512:m * T + (tc + 1) * 512]
        p.op("vector", lambda e, osl=osl, rsl=rsl, pb=pb: e.tensor_tensor(
            osl, rsl, pb[:, :], ALU.mult),
            reads=[bank, rbuf + "big"], writes=[dst_name], tag="l1_out")

    emit_proj(ctx, None, xk + "o", Wv, 8, 8, wslot="wbuf", banks=banks,
              epilogue=epi)


def register_moe_consts(ctx):
    i = np.arange(128)
    ctx.const_f32("iota_d", np.broadcast_to(i[None, :] * 1e-11, (128, 128)).copy())
    ctx.const_f32("gmask16", (i[:, None] // 8 == np.arange(16)[None, :])
                  .astype(np.float32))


def register_quant_consts(ctx):
    ctx.const_f32("qc512", np.full((128, 1), 1024.0))
    ctx.const_f32("qcm", np.full((128, 1), -3590.0))
    ctx.const_f32("qc6", np.full((128, 1), 6.0))


def emit_quant(ctx, src, dst, W, scr="q", srckey="", dstkey="", sgn_src=None):
    p = ctx.p
    qa = ctx.sb[scr + "a"]
    qs = ctx.sb[scr + "s"]
    qf = ctx.sb[scr + "f"]
    qbm = ctx.sb[scr + "bm"]
    ka, ks, kf, kbm = scr + "a", scr + "s", scr + "f", scr + "bm"
    NB = W // 16
    c512 = ctx.cap("qc512")[:, 0:1]
    cm = ctx.cap("qcm")[:, 0:1]
    sgn_in = sgn_src if sgn_src is not None else src
    p.op("scalar", lambda e: e.activation(qa[:, 0:W], src, AF.Abs),
         reads=[srckey], writes=[ka], tag="q_abs")
    p.op("vector", lambda e: e.tensor_reduce(
        qbm[:, 0:NB], qa[:, 0:W].rearrange("p (b s) -> p b s", s=16), AX.X, ALU.max),
        reads=[ka], writes=[kbm], tag="q_bm")
    p.op("vector", lambda e: e.tensor_scalar(
        qbm[:, 0:NB], qbm[:, 0:NB], 6e-12, None, ALU.max),
        reads=[kbm], writes=[kbm], tag="q_bmc")
    p.op("vector", lambda e: e.reciprocal(qbm[:, NB:2 * NB], qbm[:, 0:NB]),
         reads=[kbm], writes=[kbm], tag="q_rcp")
    p.op("vector", lambda e: e.tensor_scalar(
        qbm[:, 2 * NB:3 * NB], qbm[:, 0:NB], 1.0 / 6.0, None, ALU.mult),
        reads=[kbm], writes=[kbm], tag="q_bm6")
    rbc = qbm[:, NB:2 * NB].unsqueeze(-1).broadcast_to([128, NB, 16])
    bmbc = qbm[:, 2 * NB:3 * NB].unsqueeze(-1).broadcast_to([128, NB, 16])
    p.op("vector", lambda e: e.tensor_tensor(
        qa[:, 0:W].rearrange("p (b s) -> p b s", s=16),
        qa[:, 0:W].rearrange("p (b s) -> p b s", s=16), rbc, ALU.mult),
        reads=[ka, kbm], writes=[ka], tag="q_a")
    for i, sc in enumerate((12.0, 6.0, 3.0)):
        p.op("scalar", lambda e, i=i, sc=sc: e.activation(
            qf[:, i * W:(i + 1) * W], qa[:, 0:W], AF.Identity, bias=c512, scale=sc),
            reads=[ka, "consts"], writes=[kf], tag="q_r")
    p.op("vector", lambda e: e.tensor_scalar(
        qf[:, 3 * W:4 * W], qf[:, 0:W], 0.5, 514.0, ALU.mult, ALU.min),
        reads=[kf], writes=[kf], tag="q_A")
    p.op("vector", lambda e: e.tensor_scalar(
        qf[:, 4 * W:5 * W], qf[:, W:2 * W], 1026.0, 1028.0, ALU.max, ALU.min),
        reads=[kf], writes=[kf], tag="q_B")
    p.op("vector", lambda e: e.tensor_scalar(
        qf[:, 5 * W:6 * W], qf[:, 2 * W:3 * W], 1026.0, 2.0, ALU.max, ALU.mult),
        reads=[kf], writes=[kf], tag="q_C")
    p.op("scalar", lambda e: e.activation(qa[:, 0:W], sgn_in, AF.Sign),
         reads=[srckey], writes=[ka], tag="q_sgn")
    p.op("vector", lambda e: e.tensor_tensor(
        qs[:, 0:W], qf[:, 3 * W:4 * W], qf[:, 4 * W:5 * W], ALU.add),
        reads=[kf], writes=[ks], tag="q_s1")
    p.op("vector", lambda e: e.tensor_tensor(
        qs[:, 0:W], qs[:, 0:W], qf[:, 5 * W:6 * W], ALU.add),
        reads=[kf, ks], writes=[ks], tag="q_s2")
    p.op("scalar", lambda e: e.activation(
        qs[:, 0:W], qs[:, 0:W], AF.Identity, bias=cm, scale=1.0),
        reads=[ks, "consts"], writes=[ks], tag="q_d1")
    p.op("vector", lambda e: e.tensor_tensor(
        qs[:, 0:W], qs[:, 0:W], qa[:, 0:W], ALU.mult),
        reads=[ka, ks], writes=[ks], tag="q_d2")
    p.op("vector", lambda e: e.tensor_tensor(
        dst.rearrange("p (b s) -> p b s", s=16),
        qs[:, 0:W].rearrange("p (b s) -> p b s", s=16), bmbc, ALU.mult),
        reads=[ks, kbm], writes=[dstkey], tag="q_d3")


def emit_l3ffn(ctx, xn_name, WG, WU, WD, WGQ, WUQ, WDQ, epilogue,
               skip_wquant=False):
    p = ctx.p
    identb = ctx.cap("ident_bf")
    psbf = ctx.ps["psbf"]
    IC = [(i * 512, min(512, 2816 - i * 512)) for i in range(6)]
    if not skip_wquant:
        for W_nat, WQT, nrows, ncols in ((WG, WGQ, 2816, 1024),
                                         (WU, WUQ, 2816, 1024),
                                         (WD, WDQ, 1024, 2816)):
            for rt in range(nrows // 128):
                for half in range(ncols // 1408 if ncols > 1408 else 1):
                    h0 = half * 1408
                    hw = min(1408, ncols - h0)
                    p.dma("sync", ctx.sb["l3w"][:, 0:hw],
                          W_nat[rt * 128:(rt + 1) * 128, h0:h0 + hw],
                          writes=["l3w"], tag="l3wload")
                    emit_quant(ctx, ctx.sb["l3w"][:, 0:hw],
                               ctx.sb["l3wq"][:, h0:h0 + hw], hw,
                               srckey="l3w", dstkey="l3wq")
                for kc in range(ncols // 128):
                    p.op("tensor", lambda e, kc=kc: e.transpose(
                        psbf[:, 0:128], ctx.sb["l3wq"][:, kc * 128:(kc + 1) * 128],
                        identb), reads=["l3wq", "consts"], writes=["psbf"],
                        tag="l3wtr")
                    p.op("scalar", lambda e, kc=kc: e.activation(
                        ctx.sb["l3st"][:, kc * 128:(kc + 1) * 128], psbf[:, 0:128],
                        AF.Identity), reads=["psbf"], writes=["l3st"], tag="l3wtrev")
                p.dma("sync",
                      WQT[:, rt * 128:(rt + 1) * 128].rearrange(
                          "(kc p) c -> p kc c", p=128),
                      ctx.sb["l3st"][:, 0:ncols].rearrange(
                          "p (kc c) -> p kc c", c=128),
                      reads=["l3st"], writes=["WQT"], tag="l3wspill")
    xn = ctx.sb[xn_name]
    for tt in range(16):
        for k in range(KD):
            p.op("tensor", lambda e, k=k, tt=tt: e.transpose(
                psbf[:, k * 128:(k + 1) * 128],
                xn[:, k * T + tt * 128:k * T + (tt + 1) * 128], identb),
                reads=[xn_name, "consts"], writes=["psbf"], tag="l3xtr")
        p.op("scalar", lambda e: e.activation(
            ctx.sb["l3xt"][:, 0:1024], psbf[:, 0:1024], AF.Identity),
            reads=["psbf"], writes=["l3xt"], tag="l3xtrev")
        emit_quant(ctx, ctx.sb["l3xt"][:, 0:1024], ctx.sb["l3xq"][:, 0:1024],
                   1024, srckey="l3xt", dstkey="l3xq")
        for k in range(KD):
            p.op("tensor", lambda e, k=k: e.transpose(
                psbf[:, k * 128:(k + 1) * 128],
                ctx.sb["l3xq"][:, k * 128:(k + 1) * 128], identb),
                reads=["l3xq", "consts"], writes=["psbf"], tag="l3xqtr")
        for k in range(KD):
            p.op("scalar", lambda e, k=k, tt=tt: e.activation(
                ctx.sb["xq"][:, k * T + tt * 128:k * T + (tt + 1) * 128],
                psbf[:, k * 128:(k + 1) * 128], AF.Identity),
                reads=["psbf"], writes=["xq"], tag="l3xqev")
    xq = ctx.sb["xq"]
    hsc = ctx.sb["l3hsc"]
    hbuf = ctx.sb["l3hbuf"]
    hq = ctx.sb["l3hq"]
    wb = ctx.sb["wbuf"]
    wbd = ctx.sb["wbufd"]
    ps = [ctx.ps[f"ps{i}"] for i in range(6)]
    for tc in range(4):
        for ici, (icol0, iw) in enumerate(IC):
            nit = iw // 128
            for wi, (WQT, wkey) in enumerate(((WGQ, "wbufA"), (WUQ, "wbufB"))):
                wsl = wb[:, wi * 4096:wi * 4096 + 8 * 512]
                p.dma("gpsimd", wsl[:, 0:8 * iw].rearrange("p (k c) -> p k c", c=iw),
                      WQT[:, icol0:icol0 + iw].rearrange("(k p) c -> p k c", p=128),
                      reads=["WQT"], writes=[wkey], tag="l3wg_ld")
                for k in range(KD):
                    lhs_all = [xq[:, k * T + tc * 512 + tt * 128:
                                  k * T + tc * 512 + (tt + 1) * 128]
                               for tt in range(4)]
                    for tt in range(4):
                        p.op("tensor", lambda e, k=k, tt=tt, wsl=wsl, iw=iw,
                             lh=lhs_all[tt]: e.matmul(
                                 ps[tt][:, 0:iw], lh, wsl[:, k * iw:(k + 1) * iw],
                                 start=(k == 0), stop=(k == KD - 1)),
                            reads=[wkey, "xq"], writes=[f"ps{tt}"], tag="l3gu")
                if wi == 0:
                    for tt in range(4):
                        p.op("scalar", lambda e, tt=tt, iw=iw: e.activation(
                            hsc[:, tt * 512:tt * 512 + iw], ps[tt][:, 0:iw],
                            AF.Silu), reads=[f"ps{tt}"], writes=["l3hsc"],
                            tag="l3silu")
            for tt in range(4):
                p.op("vector", lambda e, tt=tt, iw=iw: e.tensor_tensor(
                    hbuf[:, 0:iw], hsc[:, tt * 512:tt * 512 + iw],
                    ps[tt][:, 0:iw], ALU.mult),
                    reads=["l3hsc", f"ps{tt}"], writes=["l3hbuf"], tag="l3h")
                emit_quant(ctx, hbuf[:, 0:iw], ctx.sb["l3hql"][:, 0:iw], iw,
                           srckey="l3hbuf", dstkey="l3hql")
                for j in range(nit):
                    p.op("tensor", lambda e, j=j: e.transpose(
                        psbf[:, j * 128:(j + 1) * 128],
                        ctx.sb["l3hql"][:, j * 128:(j + 1) * 128], identb),
                        reads=["l3hql", "consts"], writes=["psbf"], tag="l3htr")
                for j in range(nit):
                    it = ici * 4 + j
                    p.op("scalar", lambda e, j=j, it=it, tt=tt: e.activation(
                        hq[:, it * 512 + tt * 128:it * 512 + (tt + 1) * 128],
                        psbf[:, j * 128:(j + 1) * 128], AF.Identity),
                        reads=["psbf"], writes=["l3hq"], tag="l3hqev")
        for dt in range(KD):
            slot = dt % 2
            wsl = wbd[:, slot * 22 * 128:(slot + 1) * 22 * 128]
            p.dma("gpsimd", wsl.rearrange("p (it c) -> p it c", c=128),
                  WDQ[:, dt * 128:(dt + 1) * 128].rearrange("(it p) c -> p it c", p=128),
                  reads=["WQT"], writes=[f"wbufd{slot}"], tag="l3wd_ld")
            pb = ps[4 + slot]
            for it in range(22):
                p.op("tensor", lambda e, it=it, wsl=wsl, pb=pb: e.matmul(
                    pb[:, :], wsl[:, it * 128:(it + 1) * 128],
                    hq[:, it * 512:(it + 1) * 512],
                    start=(it == 0), stop=(it == 21)),
                    reads=[f"wbufd{slot}", "l3hq"], writes=[f"ps{4 + slot}"],
                    tag="l3dn")
            epilogue(dt, tc, pb, f"ps{4 + slot}")


def emit_mamba(ctx, xn_name, INPJ, OUTPJ, ZSP, cols_name, epilogue):
    p = ctx.p
    nq = T // 512
    ident = ctx.cap("ident")
    identb = ctx.cap("ident_bf")
    psb7 = ctx.ps["ps7"]
    ones_row = ctx.cap("ones_row")
    ones_col = ctx.cap("ones_col")
    cols = ctx.sb[cols_name]
    ck = cols_name
    CW0, CB0, DTB, ALOG, DEXP, GNW, NRMW, ACOL = 0, 96, 120, 121, 122, 138, 154, 162
    eps_col = ctx.cap("eps_col")
    p.op("scalar", lambda e: e.activation(cols[:, ACOL:ACOL + 1],
         cols[:, ALOG:ALOG + 1], AF.Exp), reads=[ck], writes=[ck], tag="m_A")
    p.op("vector", lambda e: e.tensor_scalar(cols[:, ACOL:ACOL + 1],
         cols[:, ACOL:ACOL + 1], -1.0, None, ALU.mult), reads=[ck], writes=[ck])
    dg = ctx.sb["m_diag"]
    xn = ctx.sb[xn_name]
    sq = ctx.sb["ln_sq"]
    rows = ctx.sb["m_rows"]
    pm = ctx.ps["ps6"]
    for tc in range(NCH):
        for k in range(KD):
            p.op("scalar", lambda e, k=k, tc=tc: e.activation(
                sq[:, :], xn[:, k * T + tc * 512:k * T + (tc + 1) * 512],
                AF.Square), reads=[xn_name], writes=["ln_sq"], tag="m_sq")
            p.op("tensor", lambda e, k=k: e.matmul(
                pm[0:1, :], ones_col[:, 0:1], sq[:, :],
                start=(k == 0), stop=(k == KD - 1)),
                reads=["ln_sq", "consts"], writes=["ps6"], tag="m_sum2")
        r = rows[0:1, tc * 512:(tc + 1) * 512]
        p.op("vector", lambda e, r=r: e.tensor_scalar(
            r, pm[0:1, :], 1.0 / 1024, None, ALU.mult),
            reads=["ps6"], writes=["m_rows"], tag="m_ms")
        p.op("scalar", lambda e, r=r: e.activation(
            r, r, AF.Sqrt, bias=eps_col[0:1, 0:1]),
            reads=["m_rows", "consts"], writes=["m_rows"], tag="m_msq")
        p.op("vector", lambda e, r=r: e.reciprocal(r, r),
             reads=["m_rows"], writes=["m_rows"], tag="m_rr")
    S = ctx.sb["m_S"]
    Sb = ctx.sb["m_Sb"]
    p.op("vector", lambda e: e.memset(S[:, :], 0.0), writes=["m_S"])
    p.op("vector", lambda e: e.memset(Sb[:, :], 0.0), writes=["m_Sb"])
    xcp = ctx.sb["m_xcp"]
    for ct in range(24):
        p.op("vector", lambda e, ct=ct: e.memset(xcp[:, ct * 515:ct * 515 + 3], 0.0),
             writes=["m_xcp"], tag="m_pad0")
    wsl = ctx.sb["wbuf"]
    xmq = ctx.sb["m_xmq"]
    zq = ctx.sb["m_zq"]
    xbc = ctx.sb["m_xbc"]
    dts = ctx.sb["m_dt"]
    thp = ctx.sb["m_thp"]
    rowR = ctx.sb["m_rowR"]
    CUMDR = ctx.dram_t("M_CUMB", (32, 128), F32)
    yq = ctx.sb["m_yq"]
    wexp = ctx.sb["m_wexp"]
    wbf = ctx.sb["m_wbf"]
    xtok = ctx.sb["m_xtok"]
    btok = ctx.sb["m_btok"]
    chat = ctx.sb["m_chat"]
    gsc = ctx.sb["m_gsc"]
    ps0, ps1, ps2, ps3, ps4, ps5 = (ctx.ps[f"ps{i}"] for i in range(6))
    for q in range(nq):
        t0 = q * 512
        mbq = ctx.ps["ps6"]
        p.op("tensor", lambda e, t0=t0: e.matmul(mbq[:, :], ones_row[0:1, :],
             rows[0:1, t0:t0 + 512], start=True, stop=True),
             reads=["m_rows", "consts"], writes=["ps6"], tag="m_rbc")
        for k in range(KD):
            p.op("vector", lambda e, k=k, t0=t0: e.tensor_tensor(
                sq[:, :], xn[:, k * T + t0:k * T + t0 + 512], mbq[:, :], ALU.mult),
                reads=[xn_name, "ps6"], writes=["ln_sq"], tag="m_xm1")
            p.op("vector", lambda e, k=k: e.tensor_scalar(
                xmq[:, k * 512:(k + 1) * 512], sq[:, :],
                cols[:, NRMW + k:NRMW + k + 1], None, ALU.mult),
                reads=["ln_sq", ck], writes=["m_xmq"], tag="m_xm2")
        for m in range(41):
            mw = 128 if m < 40 else 32
            slot = m % 2
            wk = f"wb{slot}"
            wv = wsl[:, slot * 1024:slot * 1024 + 1024]
            p.dma("gpsimd", wv[:, 0:8 * mw].rearrange("p (k c) -> p k c", c=mw),
                  INPJ[:, m * 128:m * 128 + mw].rearrange("(k p) c -> p k c", p=128),
                  writes=[wk], tag="m_wld")
            pb = ps0 if slot == 0 else ps1
            pk = f"ps{slot}"
            for k in range(KD):
                p.op("tensor", lambda e, k=k, wv=wv, mw=mw, pb=pb: e.matmul(
                    pb[0:mw, :], wv[:, k * mw:(k + 1) * mw],
                    xmq[:, k * 512:(k + 1) * 512],
                    start=(k == 0), stop=(k == KD - 1)),
                    reads=[wk, "m_xmq"], writes=[pk], tag="m_inpj")
            if m < 16:
                p.op("scalar", lambda e, m=m, pb=pb: e.activation(
                    zq[:, m * 512:(m + 1) * 512], pb[:, :], AF.Identity),
                    reads=[pk], writes=["m_zq"], tag="m_zev")
            elif m < 40:
                ct = m - 16
                p.op("scalar", lambda e, ct=ct, pb=pb: e.activation(
                    xcp[:, ct * 515 + 3:ct * 515 + 515], pb[:, :], AF.Identity),
                    reads=[pk], writes=["m_xcp"], tag="m_xcev")
            else:
                p.op("scalar", lambda e, pb=pb: e.activation(
                    dts[0:32, 0:512], pb[0:32, :], AF.Identity),
                    reads=[pk], writes=["m_dt"], tag="m_dtev")
        p.dma("sync", ZSP[:, t0:t0 + 512].rearrange("(m p) c -> p m c", p=128),
              zq[:, :].rearrange("p (m c) -> p m c", c=512), reads=["m_zq"],
              writes=["ZSP"], tag="m_zsp")
        for ct in range(24):
            for k in range(4):
                p.op("vector", lambda e, ct=ct, k=k: e.tensor_scalar(
                    dg[:, k * 128:(k + 1) * 128],
                    ident, cols[:, CW0 + ct * 4 + k:CW0 + ct * 4 + k + 1], None,
                    ALU.mult), reads=[ck, "consts"], writes=["m_diag"], tag="m_dg")
            for k in range(4):
                p.op("tensor", lambda e, ct=ct, k=k: e.matmul(
                    ps2[:, :], dg[:, k * 128:(k + 1) * 128],
                    xcp[:, ct * 515 + k:ct * 515 + k + 512],
                    start=(k == 0), stop=(k == 3)),
                    reads=["m_diag", "m_xcp"], writes=["ps2"], tag="m_conv")
            p.op("scalar", lambda e, ct=ct: e.activation(
                xbc[:, ct * 512:(ct + 1) * 512], ps2[:, :], AF.Silu,
                bias=cols[:, CB0 + ct:CB0 + ct + 1]),
                reads=["ps2", ck], writes=["m_xbc"], tag="m_convev")
            p.op("vector", lambda e, ct=ct: e.tensor_copy(
                xcp[:, ct * 515:ct * 515 + 3],
                xcp[:, ct * 515 + 512:ct * 515 + 515]),
                reads=["m_xcp"], writes=["m_xcp"], tag="m_tail")
        D0, D1, D2, D3, D4, D5 = (dts[0:32, i * 512:(i + 1) * 512] for i in range(6))
        p.op("scalar", lambda e: e.activation(D1, D0, AF.Identity,
             bias=cols[0:32, DTB:DTB + 1]), reads=["m_dt", ck], writes=["m_dt"])
        p.op("scalar", lambda e: e.activation(D2, D1, AF.Abs),
             reads=["m_dt"], writes=["m_dt"])
        p.op("scalar", lambda e: e.activation(D2, D2, AF.Exp, scale=-1.0),
             reads=["m_dt"], writes=["m_dt"])
        p.op("scalar", lambda e: e.activation(D2, D2, AF.Ln,
             bias=ones_col[0:32, 0:1]), reads=["m_dt", "consts"],
             writes=["m_dt"])
        p.op("scalar", lambda e: e.activation(D3, D1, AF.Relu),
             reads=["m_dt"], writes=["m_dt"])
        p.op("vector", lambda e: e.tensor_add(D3, D3, D2),
             reads=["m_dt"], writes=["m_dt"])
        p.op("vector", lambda e: e.tensor_scalar(D4, D3,
             cols[0:32, ACOL:ACOL + 1], None, ALU.mult),
             reads=["m_dt", ck], writes=["m_dt"])
        for c in range(4):
            zb = ctx.cap("zero_col")[0:32, 0:1].broadcast_to([32, 128])
            p.op("vector", lambda e, c=c, zb=zb: e.tensor_tensor_scan(
                D5[:, c * 128:(c + 1) * 128], D4[:, c * 128:(c + 1) * 128],
                zb, 0.0, ALU.add, ALU.add),
                reads=["m_dt", "consts"], writes=["m_dt"], tag="m_cum")
        p.op("scalar", lambda e: e.activation(dts[32:64, 0:512],
             D5, AF.Exp), reads=["m_dt"], writes=["m_dt"], tag="m_acum")
        for c in range(4):
            tc0 = c * 128
            p.dma("sync", CUMDR[:, :], D5[:, tc0:tc0 + 128],
                  reads=["m_dt"], writes=["M_CUMB"], tag="m_cb1")
            p.dma("sync", rowR[0:1, 0:4096].rearrange("q (h t) -> q h t", t=128),
                  CUMDR[:, :].unsqueeze(0), reads=["M_CUMB"], writes=["m_rowRa"],
                  tag="m_cb2")
            p.dma("sync", CUMDR[:, :], dts[32:64, tc0:tc0 + 128],
                  reads=["m_dt"], writes=["M_CUMB"], tag="m_cb3")
            p.dma("sync", rowR[64:65, 0:4096].rearrange("q (h t) -> q h t", t=128),
                  CUMDR[:, :].unsqueeze(0), reads=["M_CUMB"], writes=["m_rowRb"],
                  tag="m_cb4")
            p.op("tensor", lambda e, tc0=tc0: e.transpose(
                ps3[0:128, 0:32], D5[:, tc0:tc0 + 128], ident[0:32, 0:32]),
                reads=["m_dt", "consts"], writes=["ps3"], tag="m_cT")
            p.op("vector", lambda e: e.tensor_scalar(
                thp[:, 0:32], ps3[:, 0:32], -1.0, None, ALU.mult),
                reads=["ps3"], writes=["m_thp"], tag="m_cTn")
            p.op("tensor", lambda e, tc0=tc0: e.transpose(
                ps3[0:128, 32:64], D3[:, tc0:tc0 + 128], ident[0:32, 0:32]),
                reads=["m_dt", "consts"], writes=["ps3"], tag="m_dT")
            p.op("vector", lambda e: e.tensor_copy(thp[:, 32:64], ps3[:, 32:64]),
                 reads=["ps3"], writes=["m_thp"], tag="m_dTev")
            for g in range(8):
                bt = 16 + g // 2
                bb = (g % 2) * 64
                p.op("tensor", lambda e, bt=bt, bb=bb, tc0=tc0: e.transpose(
                    psb7[0:128, 0:64],
                    xbc[bb:bb + 64, bt * 512 + tc0:bt * 512 + tc0 + 128],
                    identb[bb:bb + 64, bb:bb + 64]),
                    reads=["m_xbc", "consts"], writes=["ps7"], tag="m_btr")
                p.op("scalar", lambda e, g=g: e.activation(
                    btok[:, g * 64:(g + 1) * 64], psb7[:, 0:64], AF.Identity),
                    reads=["ps7"], writes=["m_btok"], tag="m_btev")
            for h in range(32):
                g = h // 4
                ht = h // 2
                hb = (h % 2) * 64
                bt = 16 + g // 2
                gb = (g % 2) * 64
                ctt = 20 + g // 2
                p.op("tensor", lambda e, bt=bt, gb=gb, ctt=ctt, tc0=tc0: e.matmul(
                    ps4[0:128, 0:128],
                    xbc[gb:gb + 64, bt * 512 + tc0:bt * 512 + tc0 + 128],
                    xbc[gb:gb + 64, ctt * 512 + tc0:ctt * 512 + tc0 + 128],
                    start=True, stop=True),
                    reads=["m_xbc"], writes=["ps4"], tag="m_G")
                p.op("tensor", lambda e, h=h: e.matmul(
                    ps5[:, 0:128], ones_row[0:1, :],
                    rowR[0:1, h * 128:(h + 1) * 128],
                    start=True, stop=True),
                    reads=["m_rowRa", "consts"], writes=["ps5"], tag="m_R")
                p.op("vector", lambda e: e.tensor_tensor(
                    wexp[:, 0:128], ps5[:, 0:128], ctx.cap("mask_sl_neg"),
                    ALU.add), reads=["ps5", "consts"], writes=["m_wexp"],
                    tag="m_mask")
                p.op("scalar", lambda e, h=h: e.activation(
                    wexp[:, 128:256], wexp[:, 0:128], AF.Exp,
                    bias=thp[:, h:h + 1]),
                    reads=["m_wexp", "m_thp"], writes=["m_wexp"], tag="m_exp")
                p.op("vector", lambda e: e.tensor_tensor(
                    wbf[:, 0:128], wexp[:, 128:256], ps4[:, 0:128], ALU.mult),
                    reads=["m_wexp", "ps4"], writes=["m_wbf"], tag="m_W")
                p.op("tensor", lambda e, ht=ht, hb=hb, tc0=tc0: e.transpose(
                    psb7[0:128, 64:128],
                    xbc[hb:hb + 64, ht * 512 + tc0:ht * 512 + tc0 + 128],
                    identb[hb:hb + 64, hb:hb + 64]),
                    reads=["m_xbc", "consts"], writes=["ps7"], tag="m_xtr")
                p.op("vector", lambda e, h=h: e.tensor_scalar(
                    xtok[:, 0:64], psb7[:, 64:128], thp[:, 32 + h:33 + h], None,
                    ALU.mult),
                    reads=["ps7", "m_thp"], writes=["m_xtok"], tag="m_xtev")
                p.op("tensor", lambda e, h=h: e.matmul(
                    ps5[0:64, 128:256], ones_row[64:65, 0:64],
                    rowR[64:65, h * 128:(h + 1) * 128],
                    start=True, stop=True),
                    reads=["m_rowRb", "consts"], writes=["ps5"], tag="m_acbc")
                p.op("vector", lambda e, gb=gb, ctt=ctt, tc0=tc0: e.tensor_tensor(
                    chat[0:64, 64:192],
                    xbc[gb:gb + 64, ctt * 512 + tc0:ctt * 512 + tc0 + 128],
                    ps5[0:64, 128:256], ALU.mult),
                    reads=["m_xbc", "ps5"], writes=["m_chat"], tag="m_chat")
                p.op("tensor", lambda e: e.matmul(
                    ps2[0:64, 0:128], xtok[:, 0:64], wbf[:, 0:128],
                    start=True, stop=False),
                    reads=["m_xtok", "m_wbf"], writes=["ps2"], tag="m_Y1")
                p.op("tensor", lambda e, h=h: e.matmul(
                    ps2[0:64, 0:128], Sb[0:64, h * 64:(h + 1) * 64],
                    chat[0:64, 64:192], start=False, stop=True),
                    reads=["m_Sb", "m_chat"], writes=["ps2"], tag="m_Y2")
                dbc = cols[hb:hb + 64, DEXP + ht:DEXP + ht + 1].broadcast_to(
                    [64, 128])
                p.op("vector", lambda e, ht=ht, hb=hb, dbc=dbc, tc0=tc0: e.tensor_tensor(
                    gsc[0:64, 0:128],
                    xbc[hb:hb + 64, ht * 512 + tc0:ht * 512 + tc0 + 128],
                    dbc, ALU.mult),
                    reads=["m_xbc", ck], writes=["m_gsc"], tag="m_dxs")
                p.op("vector", lambda e, ht=ht, hb=hb, tc0=tc0: e.tensor_tensor(
                    yq[hb:hb + 64, ht * 512 + tc0:ht * 512 + tc0 + 128],
                    gsc[0:64, 0:128], ps2[0:64, 0:128], ALU.add),
                    reads=["m_gsc", "ps2"], writes=["m_yq"], tag="m_yev")
                if hasattr(ctx, "mamba_head_hook"):
                    ctx.mamba_head_hook(q, c, h)
                p.op("vector", lambda e, g=g: e.tensor_scalar(
                    chat[:, 0:64], btok[:, g * 64:(g + 1) * 64],
                    wexp[:, 255:256], None, ALU.mult),
                    reads=["m_btok", "m_wexp"], writes=["m_chat"], tag="m_bhat")
                p.op("tensor", lambda e: e.matmul(
                    ps3[0:64, 0:64], chat[:, 0:64],
                    xtok[:, 0:64], start=True, stop=True),
                    reads=["m_chat", "m_xtok"], writes=["ps3"], tag="m_Smm")
                p.op("vector", lambda e, h=h: e.tensor_scalar(
                    S[0:64, h * 64:(h + 1) * 64], S[0:64, h * 64:(h + 1) * 64],
                    ps5[0:64, 255:256], None, ALU.mult),
                    reads=["m_S", "ps5"], writes=["m_S"], tag="m_Ssc")
                p.op("vector", lambda e, h=h: e.tensor_tensor(
                    S[0:64, h * 64:(h + 1) * 64], S[0:64, h * 64:(h + 1) * 64],
                    ps3[0:64, 0:64], ALU.add),
                    reads=["m_S", "ps3"], writes=["m_S"], tag="m_Sadd")
                p.op("vector", lambda e, h=h: e.tensor_copy(
                    Sb[0:64, h * 64:(h + 1) * 64], S[0:64, h * 64:(h + 1) * 64]),
                    reads=["m_S"], writes=["m_Sb"], tag="m_Sbf")
        if hasattr(ctx, "mamba_pre_hook"):
            ctx.mamba_pre_hook(q)
        p.dma("sync", zq[:, :].rearrange("p (m c) -> p m c", c=512),
              ZSP[:, t0:t0 + 512].rearrange("(m p) c -> p m c", p=128),
              reads=["ZSP"], writes=["m_zq"], tag="m_zrl")
        for mt in range(16):
            p.op("scalar", lambda e, mt=mt: e.activation(
                gsc[:, 0:512], zq[:, mt * 512:(mt + 1) * 512], AF.Silu),
                reads=["m_zq"], writes=["m_gsc"], tag="m_silz")
            p.op("vector", lambda e, mt=mt: e.tensor_tensor(
                yq[:, mt * 512:(mt + 1) * 512], yq[:, mt * 512:(mt + 1) * 512],
                gsc[:, 0:512], ALU.mult),
                reads=["m_gsc", "m_yq"], writes=["m_yq"], tag="m_gmul")
        for gi in range(8):
            for j in range(2):
                mt = gi * 2 + j
                p.op("scalar", lambda e, mt=mt: e.activation(
                    sq[:, :], yq[:, mt * 512:(mt + 1) * 512], AF.Square),
                    reads=["m_yq"], writes=["ln_sq"], tag="m_gq")
                p.op("tensor", lambda e, j=j: e.matmul(
                    ps2[0:1, :], ones_col[:, 0:1], sq[:, :],
                    start=(j == 0), stop=(j == 1)),
                    reads=["ln_sq", "consts"], writes=["ps2"], tag="m_gsum")
            p.op("vector", lambda e: e.tensor_scalar(
                gsc[0:1, 0:512], ps2[0:1, :], 1.0 / 256, None, ALU.mult),
                reads=["ps2"], writes=["m_gsc"], tag="m_gms")
            p.op("scalar", lambda e: e.activation(
                gsc[0:1, 0:512], gsc[0:1, 0:512], AF.Sqrt,
                bias=eps_col[0:1, 0:1]), reads=["m_gsc", "consts"],
                writes=["m_gsc"], tag="m_gsq")
            p.op("vector", lambda e: e.reciprocal(gsc[0:1, 0:512], gsc[0:1, 0:512]),
                 reads=["m_gsc"], writes=["m_gsc"], tag="m_grr")
            p.op("tensor", lambda e: e.matmul(ps2[:, :], ones_row[0:1, :],
                 gsc[0:1, 0:512], start=True, stop=True),
                 reads=["m_gsc", "consts"], writes=["ps2"], tag="m_gbc")
            for j in range(2):
                mt = gi * 2 + j
                p.op("vector", lambda e, mt=mt: e.tensor_tensor(
                    sq[:, :], yq[:, mt * 512:(mt + 1) * 512], ps2[:, :], ALU.mult),
                    reads=["m_yq", "ps2"], writes=["ln_sq"], tag="m_gn1")
                p.op("vector", lambda e, mt=mt: e.tensor_scalar(
                    yq[:, mt * 512:(mt + 1) * 512], sq[:, :],
                    cols[:, GNW + mt:GNW + mt + 1], None, ALU.mult),
                    reads=["ln_sq", ck], writes=["m_yq"], tag="m_gn2")
        for m in range(KD):
            slot = m % 2
            wkeys = ["wb0", "wb1"] if slot == 0 else ["wb2", "wb3"]
            wv = wsl[:, slot * 2048:slot * 2048 + 2048]
            p.dma("gpsimd", wv.rearrange("p (k c) -> p k c", c=128),
                  OUTPJ[:, m * 128:(m + 1) * 128].rearrange("(k p) c -> p k c", p=128),
                  writes=wkeys, tag="m_wold")
            pb = ps0 if slot == 0 else ps1
            for k in range(16):
                p.op("tensor", lambda e, k=k, wv=wv, pb=pb: e.matmul(
                    pb[:, :], wv[:, k * 128:(k + 1) * 128],
                    yq[:, k * 512:(k + 1) * 512],
                    start=(k == 0), stop=(k == 15)),
                    reads=wkeys + ["m_yq"], writes=[f"ps{slot}"],
                    tag="m_opj")
            epilogue(m, q, pb, f"ps{slot}")


def emit_deltanet(ctx, xn_name, WQ, WK, WV, WB, LNG, dst_name,
                  lngb_done=[False]):
    p = ctx.p
    identb = ctx.cap("ident_bf")
    ones_row = ctx.cap("ones_row")
    mlow = ctx.cap("mask_triu_strict")
    minc = ctx.cap("mask_triu_incl")
    eps_col = ctx.cap("eps_col")
    xn = ctx.sb[xn_name]
    psb = ctx.ps["ps7"]
    ps0, ps1, ps2, ps3, ps4, ps5 = (ctx.ps[f"ps{i}"] for i in range(6))
    wq, wk, wv = ctx.sb["d_wq"], ctx.sb["d_wk"], ctx.sb["d_wv"]
    S = ctx.sb["d_S"]
    kt, vt, qt = ctx.sb["d_kt"], ctx.sb["d_vt"], ctx.sb["d_qt"]
    ktT, qtT = ctx.sb["d_ktT"], ctx.sb["d_qtT"]
    Rb = ctx.sb["d_R"]
    Pb = ctx.sb["d_P"]
    Mb = ctx.sb["d_M"]
    osb = ctx.sb["d_osb"]
    dcol = ctx.sb["d_cols"]
    lngb = ctx.sb["d_lngb"]
    brow = ctx.sb["d_brow"]
    dst = ctx.sb[dst_name]
    for k in range(KD):
        p.dma("gpsimd", wq[:, k * 1024:(k + 1) * 1024],
              WQ[k * 128:(k + 1) * 128, :], writes=["d_wq"], tag="d_ldq")
        p.dma("gpsimd", wk[:, k * 1024:(k + 1) * 1024],
              WK[k * 128:(k + 1) * 128, :], writes=["d_wk"], tag="d_ldk")
        p.dma("gpsimd", wv[:, k * 1024:(k + 1) * 1024],
              WV[k * 128:(k + 1) * 128, :], writes=["d_wv"], tag="d_ldv")
    p.dma("gpsimd", ctx.sb["d_wb"][:, 0:8], WB[:, :], writes=["d_wb"],
          tag="d_ldb")
    p.dma("sync", ctx.sb["d_lngrow"][0:1, 0:2048], LNG[0:1, :],
          writes=["d_lngrow"], tag="d_ldg")
    for j in range(4):
        p.op("tensor", lambda e, j=j: e.matmul(
            ctx.ps["ps6"][:, :], ones_row[0:1, :],
            ctx.sb["d_lngrow"][0:1, j * 512:(j + 1) * 512],
            start=True, stop=True),
            reads=["d_lngrow", "consts"], writes=["ps6"], tag="d_lngbc")
        p.op("vector", lambda e, j=j: e.tensor_copy(
            lngb[:, j * 512:(j + 1) * 512], ctx.ps["ps6"][:, :]),
            reads=["ps6"], writes=["d_lngb"], tag="d_lngev")
    p.op("vector", lambda e: e.memset(S[:, :], 0.0), writes=["d_S"])
    nchunks = T // 128
    for ci in range(nchunks):
        t0 = ci * 128
        for which, wmat, wkey, dstb, dkey in (
                ("k", wk, "d_wk", kt, "d_kt"), ("v", wv, "d_wv", vt, "d_vt"),
                ("q", wq, "d_wq", qt, "d_qt")):
            for half in range(2):
                pb = ps0 if half == 0 else ps1
                pk = f"ps{0 if half == 0 else 1}"
                for k in range(KD):
                    p.op("tensor", lambda e, k=k, half=half, wmat=wmat, pb=pb,
                         t0=t0: e.matmul(
                             pb[:, :],
                             xn[:, k * T + t0:k * T + t0 + 128],
                             wmat[:, k * 1024 + half * 512:k * 1024 + (half + 1) * 512],
                             start=(k == 0), stop=(k == KD - 1)),
                        reads=[wkey, xn_name], writes=[pk], tag=f"d_p{which}")
            if which == "k":
                for half in range(2):
                    pb = ps0 if half == 0 else ps1
                    p.op("scalar", lambda e, half=half, pb=pb: e.activation(
                        osb[:, half * 512:(half + 1) * 512], pb[:, :],
                        AF.Identity), reads=[f"ps{half}"], writes=["d_osb"],
                        tag="d_kev")
                p.op("scalar", lambda e: e.activation(
                    ctx.sb["ln_sq"][:, 0:512], osb[:, 0:512], AF.Square),
                    reads=["d_osb"], writes=["ln_sq"], tag="d_k2a")
                p.op("vector", lambda e: e.tensor_reduce(
                    dcol[:, 0:1], ctx.sb["ln_sq"][:, 0:512], AX.X, ALU.add),
                    reads=["ln_sq"], writes=["d_cols"], tag="d_k2r")
                p.op("scalar", lambda e: e.activation(
                    ctx.sb["ln_sq"][:, 0:512], osb[:, 512:1024], AF.Square),
                    reads=["d_osb"], writes=["ln_sq"], tag="d_k2b")
                p.op("vector", lambda e: e.tensor_reduce(
                    dcol[:, 1:2], ctx.sb["ln_sq"][:, 0:512], AX.X, ALU.add),
                    reads=["ln_sq"], writes=["d_cols"], tag="d_k2r2")
                p.op("vector", lambda e: e.tensor_add(
                    dcol[:, 0:1], dcol[:, 0:1], dcol[:, 1:2]),
                    reads=["d_cols"], writes=["d_cols"], tag="d_k2s")
                p.op("scalar", lambda e: e.activation(
                    dcol[:, 1:2], dcol[:, 0:1], AF.Sqrt),
                    reads=["d_cols"], writes=["d_cols"], tag="d_ksq")
                p.op("vector", lambda e: e.tensor_scalar(
                    dcol[:, 1:2], dcol[:, 1:2], 1e-12, None, ALU.max),
                    reads=["d_cols"], writes=["d_cols"], tag="d_kmax")
                p.op("vector", lambda e: e.reciprocal(dcol[:, 2:3], dcol[:, 1:2]),
                     reads=["d_cols"], writes=["d_cols"], tag="d_krec")
                p.op("vector", lambda e: e.tensor_scalar(
                    kt[:, 0:1024], osb[:, 0:1024], dcol[:, 2:3], None, ALU.mult),
                    reads=["d_osb", "d_cols"], writes=["d_kt"], tag="d_kn")
            elif which == "v":
                for half in range(2):
                    pb = ps0 if half == 0 else ps1
                    p.op("scalar", lambda e, half=half, pb=pb: e.activation(
                        vt[:, half * 512:(half + 1) * 512], pb[:, :], AF.Silu),
                        reads=[f"ps{half}"], writes=["d_vt"], tag="d_vev")
            else:
                for half in range(2):
                    pb = ps0 if half == 0 else ps1
                    p.op("scalar", lambda e, half=half, pb=pb: e.activation(
                        qt[:, half * 512:(half + 1) * 512], pb[:, :], AF.Identity,
                        scale=0.03125), reads=[f"ps{half}"], writes=["d_qt"],
                        tag="d_qev")
        for k in range(KD):
            p.op("tensor", lambda e, k=k, t0=t0: e.matmul(
                ps2[0:1, 0:128], ctx.sb["d_wb"][:, k:k + 1],
                xn[:, k * T + t0:k * T + t0 + 128],
                start=(k == 0), stop=(k == KD - 1)),
                reads=["d_wb", xn_name], writes=["ps2"], tag="d_brow")
        p.op("scalar", lambda e: e.activation(brow[0:1, 0:128], ps2[0:1, 0:128],
             AF.Sigmoid), reads=["ps2"], writes=["d_brow"], tag="d_bsig")
        p.op("tensor", lambda e: e.matmul(
            ctx.ps["ps6"][:, 0:1], brow[0:1, 0:128], ones_row[0:1, 0:1],
            start=True, stop=True),
            reads=["d_brow", "consts"], writes=["ps6"], tag="d_bcol")
        p.op("vector", lambda e: e.tensor_copy(dcol[:, 4:5], ctx.ps["ps6"][:, 0:1]),
             reads=["ps6"], writes=["d_cols"], tag="d_bcol2")
        p.op("vector", lambda e: e.tensor_scalar(
            dcol[:, 5:6], dcol[:, 4:5], -1.0, None, ALU.mult),
            reads=["d_cols"], writes=["d_cols"], tag="d_bneg")
        for k in range(KD):
            p.op("tensor", lambda e, k=k: e.transpose(
                psb[:, 0:128], kt[:, k * 128:(k + 1) * 128], identb),
                reads=["d_kt", "consts"], writes=["ps7"], tag="d_ktr")
            p.op("scalar", lambda e, k=k: e.activation(
                ktT[:, k * 128:(k + 1) * 128], psb[:, 0:128], AF.Identity),
                reads=["ps7"], writes=["d_ktT"], tag="d_ktrev")
            p.op("tensor", lambda e, k=k: e.transpose(
                psb[:, 0:128], qt[:, k * 128:(k + 1) * 128], identb),
                reads=["d_qt", "consts"], writes=["ps7"], tag="d_qtr")
            p.op("scalar", lambda e, k=k: e.activation(
                qtT[:, k * 128:(k + 1) * 128], psb[:, 0:128], AF.Identity),
                reads=["ps7"], writes=["d_qtT"], tag="d_qtrev")
        for k in range(KD):
            p.op("tensor", lambda e, k=k: e.matmul(
                ps3[:, 0:128], ktT[:, k * 128:(k + 1) * 128],
                ktT[:, k * 128:(k + 1) * 128],
                start=(k == 0), stop=(k == KD - 1)),
                reads=["d_ktT"], writes=["ps3"], tag="d_kkt")
        p.op("vector", lambda e: e.tensor_tensor(
            osb[:, 0:128], ps3[:, 0:128], mlow, ALU.mult),
            reads=["ps3", "consts"], writes=["d_osb"], tag="d_p0a")
        p.op("vector", lambda e: e.tensor_scalar(
            Pb[:, 0:128], osb[:, 0:128], dcol[:, 5:6], None, ALU.mult),
            reads=["d_osb", "d_cols"], writes=["d_P"], tag="d_p0b")
        for dv in range(2):
            pb = ps4 if dv == 0 else ps5
            for k in range(KD):
                p.op("tensor", lambda e, k=k, dv=dv, pb=pb: e.matmul(
                    pb[:, :], ktT[:, k * 128:(k + 1) * 128],
                    S[:, k * 1024 + dv * 512:k * 1024 + (dv + 1) * 512],
                    start=(k == 0), stop=(k == KD - 1)),
                    reads=["d_ktT", "d_S"], writes=[f"ps{4 + dv}"], tag="d_ks0")
        for dv in range(2):
            pb = ps4 if dv == 0 else ps5
            p.op("vector", lambda e, dv=dv, pb=pb: e.tensor_tensor(
                Rb[:, dv * 512:(dv + 1) * 512], vt[:, dv * 512:(dv + 1) * 512],
                pb[:, :], ALU.subtract),
                reads=["d_vt", f"ps{4 + dv}"], writes=["d_R"], tag="d_R")
        for lvl in range(7):
            for dv in range(2):
                pb = ps4 if dv == 0 else ps5
                p.op("tensor", lambda e, dv=dv, pb=pb, lvl=lvl: e.matmul(
                    pb[:, :], Pb[:, 0:128], Rb[:, dv * 512:(dv + 1) * 512],
                    start=True, stop=True),
                    reads=["d_P", "d_R"], writes=[f"ps{4 + dv}"], tag="d_ap")
            for dv in range(2):
                pb = ps4 if dv == 0 else ps5
                p.op("vector", lambda e, dv=dv, pb=pb: e.tensor_tensor(
                    Rb[:, dv * 512:(dv + 1) * 512], Rb[:, dv * 512:(dv + 1) * 512],
                    pb[:, :], ALU.add),
                    reads=[f"ps{4 + dv}", "d_R"], writes=["d_R"], tag="d_apadd")
            if lvl < 6:
                p.op("tensor", lambda e: e.transpose(psb[:, 0:128], Pb[:, 0:128],
                     identb), reads=["d_P", "consts"], writes=["ps7"],
                     tag="d_ptr")
                p.op("scalar", lambda e: e.activation(
                    Pb[:, 128:256], psb[:, 0:128], AF.Identity),
                    reads=["ps7"], writes=["d_P"], tag="d_ptrev")
                p.op("tensor", lambda e: e.matmul(
                    ps3[:, 0:128], Pb[:, 128:256], Pb[:, 0:128],
                    start=True, stop=True),
                    reads=["d_P"], writes=["ps3"], tag="d_psq")
                p.op("scalar", lambda e: e.activation(
                    Pb[:, 0:128], ps3[:, 0:128], AF.Identity),
                    reads=["ps3"], writes=["d_P"], tag="d_psqev")
        for dv in range(2):
            p.op("vector", lambda e, dv=dv: e.tensor_scalar(
                Rb[:, dv * 512:(dv + 1) * 512], Rb[:, dv * 512:(dv + 1) * 512],
                dcol[:, 4:5], None, ALU.mult),
                reads=["d_R", "d_cols"], writes=["d_R"], tag="d_vadj")
        for k in range(KD):
            p.op("tensor", lambda e, k=k: e.matmul(
                ps3[:, 0:128], ktT[:, k * 128:(k + 1) * 128],
                qtT[:, k * 128:(k + 1) * 128],
                start=(k == 0), stop=(k == KD - 1)),
                reads=["d_ktT", "d_qtT"], writes=["ps3"], tag="d_kqt")
        p.op("vector", lambda e: e.tensor_tensor(
            Mb[:, 0:128], ps3[:, 0:128], minc, ALU.mult),
            reads=["ps3", "consts"], writes=["d_M"], tag="d_minc")
        for dv in range(2):
            pb = ps4 if dv == 0 else ps5
            p.op("tensor", lambda e, dv=dv, pb=pb: e.matmul(
                pb[:, :], Mb[:, 0:128], Rb[:, dv * 512:(dv + 1) * 512],
                start=True, stop=False),
                reads=["d_M", "d_R"], writes=[f"ps{4 + dv}"], tag="d_O1")
            for k in range(KD):
                p.op("tensor", lambda e, k=k, dv=dv, pb=pb: e.matmul(
                    pb[:, :], qtT[:, k * 128:(k + 1) * 128],
                    S[:, k * 1024 + dv * 512:k * 1024 + (dv + 1) * 512],
                    start=False, stop=(k == KD - 1)),
                    reads=["d_qtT", "d_S"], writes=[f"ps{4 + dv}"], tag="d_O2")
        for kk in range(KD):
            p.op("tensor", lambda e, kk=kk: e.matmul(
                ps3[:, 0:512], kt[:, kk * 128:(kk + 1) * 128], Rb[:, 0:512],
                start=True, stop=True),
                reads=["d_kt", "d_R"], writes=["ps3"], tag="d_su1")
            p.op("vector", lambda e, kk=kk: e.tensor_tensor(
                S[:, kk * 1024:kk * 1024 + 512], S[:, kk * 1024:kk * 1024 + 512],
                ps3[:, 0:512], ALU.add),
                reads=["ps3", "d_S"], writes=["d_S"], tag="d_su1a")
            p.op("tensor", lambda e, kk=kk: e.matmul(
                ps3[:, 0:512], kt[:, kk * 128:(kk + 1) * 128], Rb[:, 512:1024],
                start=True, stop=True),
                reads=["d_kt", "d_R"], writes=["ps3"], tag="d_su2")
            p.op("vector", lambda e, kk=kk: e.tensor_tensor(
                S[:, kk * 1024 + 512:(kk + 1) * 1024],
                S[:, kk * 1024 + 512:(kk + 1) * 1024],
                ps3[:, 0:512], ALU.add),
                reads=["ps3", "d_S"], writes=["d_S"], tag="d_su2a")
        for dv in range(2):
            pb = ps4 if dv == 0 else ps5
            p.op("scalar", lambda e, dv=dv, pb=pb: e.activation(
                osb[:, dv * 512:(dv + 1) * 512], pb[:, :], AF.Identity),
                reads=[f"ps{4 + dv}"], writes=["d_osb"], tag="d_oev")
        p.op("vector", lambda e: e.tensor_reduce(
            dcol[:, 0:1], osb[:, 0:1024], AX.X, ALU.add),
            reads=["d_osb"], writes=["d_cols"], tag="d_om")
        p.op("scalar", lambda e: e.activation(
            ctx.sb["ln_sq"][:, 0:512], osb[:, 0:512], AF.Square),
            reads=["d_osb"], writes=["ln_sq"], tag="d_osq1")
        p.op("vector", lambda e: e.tensor_reduce(
            dcol[:, 1:2], ctx.sb["ln_sq"][:, 0:512], AX.X, ALU.add),
            reads=["ln_sq"], writes=["d_cols"], tag="d_osr1")
        p.op("scalar", lambda e: e.activation(
            ctx.sb["ln_sq"][:, 0:512], osb[:, 512:1024], AF.Square),
            reads=["d_osb"], writes=["ln_sq"], tag="d_osq2")
        p.op("vector", lambda e: e.tensor_reduce(
            dcol[:, 2:3], ctx.sb["ln_sq"][:, 0:512], AX.X, ALU.add),
            reads=["ln_sq"], writes=["d_cols"], tag="d_osr2")
        p.op("vector", lambda e: e.tensor_add(dcol[:, 1:2], dcol[:, 1:2],
             dcol[:, 2:3]), reads=["d_cols"], writes=["d_cols"], tag="d_os")
        p.op("vector", lambda e: e.tensor_scalar(
            dcol[:, 0:1], dcol[:, 0:1], 1.0 / 1024, None, ALU.mult),
            reads=["d_cols"], writes=["d_cols"], tag="d_om2")
        p.op("vector", lambda e: e.tensor_scalar(
            dcol[:, 1:2], dcol[:, 1:2], 1.0 / 1024, None, ALU.mult),
            reads=["d_cols"], writes=["d_cols"], tag="d_ov")
        p.op("vector", lambda e: e.tensor_tensor(
            dcol[:, 3:4], dcol[:, 0:1], dcol[:, 0:1], ALU.mult),
            reads=["d_cols"], writes=["d_cols"], tag="d_om3")
        p.op("vector", lambda e: e.tensor_tensor(
            dcol[:, 1:2], dcol[:, 1:2], dcol[:, 3:4], ALU.subtract),
            reads=["d_cols"], writes=["d_cols"], tag="d_ov2")
        p.op("scalar", lambda e: e.activation(
            dcol[:, 1:2], dcol[:, 1:2], AF.Sqrt, bias=eps_col[:, 0:1]),
            reads=["d_cols", "consts"], writes=["d_cols"], tag="d_ost")
        p.op("vector", lambda e: e.reciprocal(dcol[:, 1:2], dcol[:, 1:2]),
             reads=["d_cols"], writes=["d_cols"], tag="d_orc")
        p.op("vector", lambda e: e.tensor_tensor(
            dcol[:, 3:4], dcol[:, 0:1], dcol[:, 1:2], ALU.mult),
            reads=["d_cols"], writes=["d_cols"], tag="d_onm")
        p.op("vector", lambda e: e.tensor_scalar(
            dcol[:, 3:4], dcol[:, 3:4], -1.0, None, ALU.mult),
            reads=["d_cols"], writes=["d_cols"], tag="d_onm2")
        p.op("vector", lambda e: e.tensor_scalar(
            osb[:, 0:1024], osb[:, 0:1024], dcol[:, 3:4], dcol[:, 1:2],
            ALU.bypass, ALU.mult) if False else e.tensor_scalar(
            osb[:, 0:1024], osb[:, 0:1024], dcol[:, 1:2], dcol[:, 3:4],
            ALU.mult, ALU.add),
            reads=["d_cols", "d_osb"], writes=["d_osb"], tag="d_onorm")
        p.op("vector", lambda e: e.tensor_tensor(
            osb[:, 0:1024], osb[:, 0:1024], lngb[:, 0:1024], ALU.mult),
            reads=["d_osb", "d_lngb"], writes=["d_osb"], tag="d_og")
        p.op("vector", lambda e: e.tensor_tensor(
            osb[:, 0:1024], osb[:, 0:1024], lngb[:, 1024:2048], ALU.add),
            reads=["d_osb", "d_lngb"], writes=["d_osb"], tag="d_ob")
        p.op("vector", lambda e: e.tensor_copy(vt[:, 0:1024], osb[:, 0:1024]),
             reads=["d_osb"], writes=["d_vt"], tag="d_ocast")
        for k in range(KD):
            p.op("tensor", lambda e, k=k: e.transpose(
                psb[:, 0:128], vt[:, k * 128:(k + 1) * 128], identb),
                reads=["d_vt", "consts"], writes=["ps7"], tag="d_otr")
            p.op("vector", lambda e, k=k, t0=t0: e.tensor_tensor(
                dst[:, k * T + t0:k * T + t0 + 128],
                dst[:, k * T + t0:k * T + t0 + 128],
                psb[:, 0:128], ALU.add),
                reads=["ps7", dst_name], writes=[dst_name], tag="d_oadd")


def emit_moe(ctx, xn_name, x_name, mrow_ap, rrow_ap, spec):
    p = ctx.p
    E, K, CAP = spec["E"], spec["K"], spec["CAP"]
    Eloc = E // 8
    kind = spec["kind"]
    NT = 16
    A = T * K
    ident = ctx.cap("ident")
    identb = ctx.cap("ident_bf")
    ones_row = ctx.cap("ones_row")
    ones_col = ctx.cap("ones_col")
    psb = ctx.ps["ps7"]
    ps0, ps1, ps2, ps3, ps4, ps5, ps6 = (ctx.ps[f"ps{i}"] for i in range(7))
    xn = ctx.sb[xn_name]
    RW = spec["RW"]
    sc = ctx.sb["e_sc"]
    lg = ctx.sb["e_lg"]
    easg = ctx.sb["e_easg"]
    wasg = ctx.sb["e_wasg"]
    rwsb = ctx.sb["e_rw"]
    iota_e = ctx.cap("iota_row")
    XP = spec["XP"]
    xrt = ctx.sb["e_xrt"]
    for k in range(KD):
        p.dma("sync", rwsb[:, k * E:(k + 1) * E], RW[k * 128:(k + 1) * 128, :],
              writes=["e_rw"], tag="e_ldrw")
    for k in range(KD):
        p.op("tensor", lambda e, k=k: e.matmul(
            ps6[0:1, 0:E], ones_col[:, 0:1], rwsb[:, k * E:(k + 1) * E],
            start=(k == 0), stop=(k == KD - 1)),
            reads=["e_rw", "consts"], writes=["ps6"], tag="e_csr")
    p.op("vector", lambda e: e.tensor_copy(sc[0:1, 0:E], ps6[0:1, 0:E]),
         reads=["ps6"], writes=["e_sc"], tag="e_csr2")
    for tl in range(NT):
        for k in range(KD):
            sl = k % 2
            p.dma("sync", xrt[:, sl * 128:(sl + 1) * 128],
                  XP[k * 128:(k + 1) * 128, tl * 128:(tl + 1) * 128],
                  writes=[f"e_xrt{sl}"], tag="e_xrl")
            p.op("tensor", lambda e, tl=tl, k=k, sl=sl: e.matmul(
                ps0[:, 0:E], xrt[:, sl * 128:(sl + 1) * 128],
                rwsb[:, k * E:(k + 1) * E],
                start=(k == 0), stop=(k == KD - 1)),
            reads=["e_rw", f"e_xrt{sl}"], writes=["ps0"], tag="e_lg1")
        p.op("vector", lambda e, tl=tl: e.tensor_tensor(
            sc[0:1, 16 * E:16 * E + 128], mrow_ap[0:1, tl * 128:(tl + 1) * 128],
            rrow_ap[0:1, tl * 128:(tl + 1) * 128], ALU.mult),
            reads=["e_mr"], writes=["e_sc"], tag="e_mr")
        p.op("tensor", lambda e, tl=tl: e.matmul(
            ps1[:, 0:E], sc[0:1, 16 * E:16 * E + 128], sc[0:1, 0:E],
            start=True, stop=True),
            reads=["e_sc"], writes=["ps1"], tag="e_out")
        p.op("tensor", lambda e, tl=tl: e.transpose(
            ps2[:, 0:1], rrow_ap[0:1, tl * 128:(tl + 1) * 128], ident[0:1, 0:1]),
            reads=["e_mr", "consts"], writes=["ps2"], tag="e_rT")
        p.op("vector", lambda e: e.tensor_copy(sc[:, 2 * E:2 * E + 1], ps2[:, 0:1]),
             reads=["ps2"], writes=["e_sc"], tag="e_rT2")
        p.op("vector", lambda e, tl=tl: e.tensor_scalar(
            lg[:, tl * E:(tl + 1) * E], ps0[:, 0:E], sc[:, 2 * E:2 * E + 1],
            None, ALU.mult), reads=["ps0", "e_sc"], writes=["e_lg"], tag="e_lgs")
        p.op("vector", lambda e, tl=tl: e.tensor_tensor(
            lg[:, tl * E:(tl + 1) * E], lg[:, tl * E:(tl + 1) * E], ps1[:, 0:E],
            ALU.subtract), reads=["ps1", "e_lg"], writes=["e_lg"], tag="e_lgf")
    if kind == "l0":
        p.op("scalar", lambda e: e.activation(lg[:, 0:NT * E], lg[:, 0:NT * E],
             AF.Exp), reads=["e_lg"], writes=["e_lg"], tag="e_sm1")
        p.op("vector", lambda e: e.tensor_reduce(
            sc[:, 0:NT], lg[:, 0:NT * E].rearrange("p (t e) -> p t e", e=E),
            AX.X, ALU.add), reads=["e_lg"], writes=["e_sc"], tag="e_sm2")
        p.op("vector", lambda e: e.reciprocal(sc[:, NT:2 * NT], sc[:, 0:NT]),
             reads=["e_sc"], writes=["e_sc"], tag="e_sm3")
        p.op("vector", lambda e: e.tensor_tensor(
            lg[:, 0:NT * E].rearrange("p (t e) -> p t e", e=E),
            lg[:, 0:NT * E].rearrange("p (t e) -> p t e", e=E),
            sc[:, NT:2 * NT].unsqueeze(-1).broadcast_to([128, NT, E]), ALU.mult),
            reads=["e_sc", "e_lg"], writes=["e_lg"], tag="e_sm4")
        sel_off = 0
    else:
        p.op("scalar", lambda e: e.activation(lg[:, 0:NT * E], lg[:, 0:NT * E],
             AF.Sigmoid), reads=["e_lg"], writes=["e_lg"], tag="e_sg")
        p.dma("sync", sc[0:1, 3 * E:4 * E], spec["BIASR"][0:1, :],
              writes=["e_sc"], tag="e_ldbias")
        p.op("tensor", lambda e: e.matmul(ps2[:, 0:E], ones_row[0:1, :],
             sc[0:1, 3 * E:4 * E], start=True, stop=True),
             reads=["e_sc", "consts"], writes=["ps2"], tag="e_bb")
        p.op("vector", lambda e: e.tensor_copy(ctx.sb["e_sel"][:, NT * E:NT * E + E],
             ps2[:, 0:E]), reads=["ps2"], writes=["e_sel"], tag="e_bb2")
        for tl in range(NT):
            p.op("vector", lambda e, tl=tl: e.tensor_tensor(
                ctx.sb["e_sel"][:, tl * E:(tl + 1) * E], lg[:, tl * E:(tl + 1) * E],
                ctx.sb["e_sel"][:, NT * E:NT * E + E], ALU.add),
                reads=["e_lg", "e_sel"], writes=["e_sel"], tag="e_selb")
        sel_off = None
    selbuf = lg if kind == "l0" else ctx.sb["e_sel"]
    selkey = "e_lg" if kind == "l0" else "e_sel"
    iota_d = ctx.cap("iota_d")
    p.op("vector", lambda e: e.tensor_tensor(
        selbuf[:, 0:NT * E].rearrange("p (t e) -> p t e", e=E),
        selbuf[:, 0:NT * E].rearrange("p (t e) -> p t e", e=E),
        iota_d[:, 0:E].unsqueeze(1).broadcast_to([128, NT, E]), ALU.subtract),
        reads=[selkey, "consts"], writes=[selkey], tag="e_tb")
    for j in range(K):
        p.op("vector", lambda e: e.tensor_reduce(
            sc[:, 0:NT], selbuf[:, 0:NT * E].rearrange("p (t e) -> p t e", e=E),
            AX.X, ALU.max), reads=[selkey], writes=["e_sc"], tag="e_tmax")
        p.op("vector", lambda e: e.tensor_tensor(
            ctx.sb["e_eq"][:, 0:NT * E].rearrange("p (t e) -> p t e", e=E),
            selbuf[:, 0:NT * E].rearrange("p (t e) -> p t e", e=E),
            sc[:, 0:NT].unsqueeze(-1).broadcast_to([128, NT, E]), ALU.is_equal),
            reads=[selkey, "e_sc"], writes=["e_eq"], tag="e_eq")
        p.op("vector", lambda e: e.tensor_tensor(
            ctx.sb["e_eq2"][:, 0:NT * E].rearrange("p (t e) -> p t e", e=E),
            ctx.sb["e_eq"][:, 0:NT * E].rearrange("p (t e) -> p t e", e=E),
            iota_e[:, 0:E].unsqueeze(1).broadcast_to([128, NT, E]),
            ALU.mult), reads=["e_eq", "consts"], writes=["e_eq2"], tag="e_ix1")
        p.op("vector", lambda e, j=j: e.tensor_reduce(
            easg[:, j * NT:(j + 1) * NT],
            ctx.sb["e_eq2"][:, 0:NT * E].rearrange("p (t e) -> p t e", e=E),
            AX.X, ALU.add), reads=["e_eq2"], writes=["e_easg"], tag="e_ix2")
        p.op("vector", lambda e: e.tensor_tensor(
            ctx.sb["e_eq2"][:, 0:NT * E], ctx.sb["e_eq"][:, 0:NT * E],
            lg[:, 0:NT * E], ALU.mult),
            reads=["e_eq", "e_lg"], writes=["e_eq2"], tag="e_w1")
        p.op("vector", lambda e, j=j: e.tensor_reduce(
            wasg[:, j * NT:(j + 1) * NT],
            ctx.sb["e_eq2"][:, 0:NT * E].rearrange("p (t e) -> p t e", e=E),
            AX.X, ALU.add), reads=["e_eq2"], writes=["e_wasg"], tag="e_w2")
        if j < K - 1:
            p.op("vector", lambda e: e.tensor_scalar(
                ctx.sb["e_eq"][:, 0:NT * E], ctx.sb["e_eq"][:, 0:NT * E],
                1e30, None, ALU.mult), reads=["e_eq"], writes=["e_eq"],
                tag="e_big")
            p.op("vector", lambda e: e.tensor_tensor(
                selbuf[:, 0:NT * E], selbuf[:, 0:NT * E],
                ctx.sb["e_eq"][:, 0:NT * E], ALU.subtract),
                reads=["e_eq", selkey], writes=[selkey], tag="e_sub")
    if kind == "l0":
        p.op("vector", lambda e: e.tensor_tensor(
            sc[:, 0:NT], wasg[:, 0:NT], wasg[:, NT:2 * NT], ALU.add),
            reads=["e_wasg"], writes=["e_sc"], tag="e_gn1")
        p.op("vector", lambda e: e.reciprocal(sc[:, NT:2 * NT], sc[:, 0:NT]),
             reads=["e_sc"], writes=["e_sc"], tag="e_gn2")
        for j in range(2):
            p.op("vector", lambda e, j=j: e.tensor_tensor(
                wasg[:, j * NT:(j + 1) * NT], wasg[:, j * NT:(j + 1) * NT],
                sc[:, NT:2 * NT], ALU.mult),
                reads=["e_sc", "e_wasg"], writes=["e_wasg"], tag="e_gn3")
    else:
        p.op("vector", lambda e: e.tensor_tensor(
            sc[:, 0:NT], wasg[:, 0:NT], wasg[:, NT:2 * NT], ALU.add),
            reads=["e_wasg"], writes=["e_sc"], tag="e_wn1")
        for j in range(2, K):
            p.op("vector", lambda e, j=j: e.tensor_tensor(
                sc[:, 0:NT], sc[:, 0:NT], wasg[:, j * NT:(j + 1) * NT],
                ALU.add), reads=["e_wasg", "e_sc"], writes=["e_sc"],
                tag="e_wn1a")
        p.op("vector", lambda e: e.tensor_scalar(
            sc[:, 0:NT], sc[:, 0:NT], 1e-20, None, ALU.add),
            reads=["e_sc"], writes=["e_sc"], tag="e_wn1b")
        p.op("vector", lambda e: e.reciprocal(sc[:, NT:2 * NT], sc[:, 0:NT]),
             reads=["e_sc"], writes=["e_sc"], tag="e_wn2")
        p.op("vector", lambda e: e.tensor_scalar(
            sc[:, NT:2 * NT], sc[:, NT:2 * NT], 2.5, None, ALU.mult),
            reads=["e_sc"], writes=["e_sc"], tag="e_wn3")
        for j in range(K):
            p.op("vector", lambda e, j=j: e.tensor_tensor(
                wasg[:, j * NT:(j + 1) * NT], wasg[:, j * NT:(j + 1) * NT],
                sc[:, NT:2 * NT], ALU.mult),
                reads=["e_sc", "e_wasg"], writes=["e_wasg"], tag="e_wn4")
    return dict(easg=easg, wasg=wasg)


def emit_moe_exec(ctx, xn_name, x_name, spec):
    p = ctx.p
    E, K, CAP = spec["E"], spec["K"], spec["CAP"]
    Eloc = E // 8
    kind = spec["kind"]
    NT = 16
    A = T * K
    NAT = A // 128
    ident = ctx.cap("ident")
    identb = ctx.cap("ident_bf")
    ones_row = ctx.cap("ones_row")
    ones_col = ctx.cap("ones_col")
    iota_e = ctx.cap("iota_row")
    triu_s = ctx.cap("mask_triu_strict")
    psb = ctx.ps["ps7"]
    ps0, ps1, ps2, ps3, ps4, ps5, ps6 = (ctx.ps[f"ps{i}"] for i in range(7))
    xn = ctx.sb[xn_name]
    easg, wasg = ctx.sb["e_easg"], ctx.sb["e_wasg"]
    sc = ctx.sb["e_sc"]
    ebf = ctx.sb["e_bf"]
    easgA = ctx.sb["e_easgA"]
    oh4 = ctx.sb["e_oh4"]
    slots = ctx.sb["e_slots"]
    wfin = ctx.sb["e_wfin"]
    rows2 = ctx.sb["e_rows"]
    pc = ctx.sb["e_pc"]
    ncg = 2 if kind == "l0" else 1
    CNT_W = E * ncg
    def _bar(tag, reads, writes):
        if not spec.get("bars", False):
            return
        p.op("gpsimd", lambda e: e.collective_compute(
            "AllReduce", ALU.add, replica_groups=[list(range(8))],
            ins=[spec["BARI"][:, :].opt()], outs=[spec["BARO"][:, :].opt()]),
            reads=["BARI"] + reads, writes=writes, kind="coll", tag=tag)
    if kind == "l2":
        for j in range(K):
            p.op("vector", lambda e, j=j: e.tensor_copy(
                ebf[:, 0:NT * K].rearrange("p (t j) -> p t j", j=K)[:, :, j],
                easg[:, j * NT:(j + 1) * NT]),
                reads=["e_easg"], writes=["e_bf"], tag="e_bf1")
            p.op("vector", lambda e, j=j: e.tensor_copy(
                ebf[:, NT * K:2 * NT * K].rearrange("p (t j) -> p t j", j=K)[:, :, j],
                wasg[:, j * NT:(j + 1) * NT]),
                reads=["e_wasg"], writes=["e_bf"], tag="e_bf2")
        p.dma("sync", spec["EADR"][0:NT, :, :].rearrange("t p j -> p t j"),
              ebf[:, 0:NT * K].rearrange("p (t j) -> p t j", j=K),
              reads=["e_bf"], writes=["EADR"], tag="e_bo1")
        p.dma("sync", spec["EADR"][NT:2 * NT, :, :].rearrange("t p j -> p t j"),
              ebf[:, NT * K:2 * NT * K].rearrange("p (t j) -> p t j", j=K),
              reads=["e_bf"], writes=["EADR"], tag="e_bo2")
        EADRf = spec["EADR"].rearrange("t p j -> (t p j)")
        p.dma("scalar", easgA[:, 0:NAT],
              EADRf[0:A].rearrange("(a q) -> a q", q=128), transpose=True,
              reads=["EADR"], writes=["e_easgA"], tag="e_bt1")
        p.dma("scalar", easgA[:, NAT:2 * NAT],
              EADRf[A:2 * A].rearrange("(a q) -> a q", q=128), transpose=True,
              reads=["EADR"], writes=["e_easgA"], tag="e_bt2")
    else:
        p.op("vector", lambda e: e.tensor_copy(easgA[:, 0:NAT], easg[:, 0:NAT]),
             reads=["e_easg"], writes=["e_easgA"], tag="e_cA")
        p.op("vector", lambda e: e.tensor_copy(easgA[:, NAT:2 * NAT],
             wasg[:, 0:NAT]), reads=["e_wasg"], writes=["e_easgA"], tag="e_cA2")
    for tl in range(NT):
        for k in range(KD):
            p.op("tensor", lambda e, tl=tl, k=k: e.transpose(
                psb[:, k * 128:(k + 1) * 128],
                xn[:, k * T + tl * 128:k * T + (tl + 1) * 128], identb),
                reads=[xn_name, "consts"], writes=["ps7"], tag="e_xtr")
        p.op("scalar", lambda e: e.activation(
            ctx.sb["e_xtk"][:, 0:1024], psb[:, 0:1024], AF.Identity),
            reads=["ps7"], writes=["e_xtk"], tag="e_xtev")
        p.dma("sync", spec["XAGI"][tl * 128:(tl + 1) * 128, :],
              ctx.sb["e_xtk"][:, 0:1024], reads=["e_xtk"], writes=["XAGI"],
              tag="e_xout")
    _bar("e_barx0", ["XAGI"], ["XAGI"])
    p.op("gpsimd", lambda e: e.collective_compute(
        "AllGather", ALU.bypass, replica_groups=[list(range(8))],
        ins=[spec["XAGI"][:, :].opt()], outs=[spec["XALL"][:, :].opt()]),
        reads=["XAGI"], writes=["XALL"], kind="coll", tag="e_agx")
    TPG = NAT // ncg
    for g in range(ncg):
        for bi in range(TPG // 4):
            t0 = g * TPG + bi * 4
            p.op("vector", lambda e, t0=t0: e.tensor_tensor(
                oh4[:, 0:4 * E].rearrange("p (b e) -> p b e", e=E),
                easgA[:, t0:t0 + 4].unsqueeze(-1).broadcast_to([128, 4, E]),
                iota_e[:, 0:E].unsqueeze(1).broadcast_to([128, 4, E]),
                ALU.is_equal),
                reads=["e_easgA", "consts"], writes=["e_oh4"], tag="e_oh")
            p.op("tensor", lambda e, bi=bi: e.matmul(
                ps6[0:1, 0:4 * E], ones_col[:, 0:1], oh4[:, 0:4 * E],
                start=(bi == 0), stop=(bi == TPG // 4 - 1)),
                reads=["e_oh4", "consts"], writes=["ps6"], tag="e_cnt")
        p.op("vector", lambda e, g=g: e.tensor_reduce(
            sc[0:1, 8 * E + g * E:8 * E + (g + 1) * E],
            ps6[0:1, 0:4 * E].rearrange("p (b e) -> p e b", e=E), AX.X, ALU.add),
            reads=["ps6"], writes=["e_sc"], tag="e_cnt2")
    p.dma("sync", spec["CNTI"][0:1, :], sc[0:1, 8 * E:8 * E + CNT_W],
          reads=["e_sc"], writes=["CNTI"], tag="e_cnti")
    _bar("e_barc0", ["CNTI"], ["CNTI"])
    p.op("gpsimd", lambda e: e.collective_compute(
        "AllGather", ALU.bypass, replica_groups=[list(range(8))],
        ins=[spec["CNTI"][:, :].opt()], outs=[spec["CNTO"][:, :].opt()]),
        reads=["CNTI"], writes=["CNTO"], kind="coll", tag="e_agc")
    _bar("e_barc1", ["CNTO"], ["CNTO"])
    cnts = ctx.sb["e_cnts"]
    p.dma("sync", cnts[0:8, 0:CNT_W], spec["CNTO"][:, :], reads=["CNTO"],
          writes=["e_cnts"], tag="e_ldcnt")
    rmask = ctx.sb["e_rmask"]
    p.op("tensor", lambda e: e.matmul(ps6[0:1, 0:CNT_W], rmask[0:8, 0:1],
         cnts[0:8, 0:CNT_W], start=True, stop=True),
         reads=["e_rmask", "e_cnts"], writes=["ps6"], tag="e_strt")
    p.op("vector", lambda e: e.tensor_copy(sc[0:1, 10 * E:10 * E + CNT_W],
         ps6[0:1, 0:CNT_W]), reads=["ps6"], writes=["e_sc"], tag="e_strt2")
    if kind == "l0":
        p.op("tensor", lambda e: e.matmul(ps6[0:1, 0:E], ones_col[0:8, 0:1],
             cnts[0:8, 0:E], start=True, stop=True),
             reads=["e_cnts", "consts"], writes=["ps6"], tag="e_tj0")
        p.op("vector", lambda e: e.tensor_copy(sc[0:1, 12 * E:13 * E],
             ps6[0:1, 0:E]), reads=["ps6"], writes=["e_sc"], tag="e_tj0c")
        p.op("vector", lambda e: e.tensor_tensor(
            sc[0:1, 11 * E:12 * E], sc[0:1, 11 * E:12 * E],
            sc[0:1, 12 * E:13 * E], ALU.add), reads=["e_sc"], writes=["e_sc"],
            tag="e_tj0a")
    for g in range(ncg):
        for b in range(4):
            p.op("vector", lambda e, g=g, b=b: e.tensor_copy(
                rows2[0:1, g * 4 * E + b * E:g * 4 * E + (b + 1) * E],
                sc[0:1, 10 * E + g * E:10 * E + (g + 1) * E]),
                reads=["e_sc"], writes=["e_rows"], tag="e_gsc")
    nmapz = (E * CAP + 128) // 128
    p.op("vector", lambda e: e.memset(ctx.sb["e_xtk"][:, 0:nmapz], 0.0),
         writes=["e_xtk"], tag="e_ctz0")
    p.dma("gpsimd", spec["MAPI"][0:E * CAP + 128, :].rearrange("(p a) c -> p (a c)", p=128),
          ctx.sb["e_xtk"][:, 0:nmapz], reads=["e_xtk"],
          writes=["MAPI"], tag="e_ctz")
    for g in range(ncg):
        p.op("vector", lambda e, g=g: e.memset(rows2[0:1, 8 * E:12 * E], 0.0),
             writes=["e_rows"], tag="e_car0")
        p.op("tensor", lambda e, g=g: e.matmul(
            ps5[:, 0:4 * E], ones_row[0:1, :],
            rows2[0:1, g * 4 * E:(g + 1) * 4 * E], start=True, stop=True),
            reads=["e_rows", "consts"], writes=["ps5"], tag="e_sbc")
        for bi in range(TPG // 4):
            t0 = g * TPG + bi * 4
            p.op("vector", lambda e, t0=t0: e.tensor_tensor(
                oh4[:, 0:4 * E].rearrange("p (b e) -> p b e", e=E),
                easgA[:, t0:t0 + 4].unsqueeze(-1).broadcast_to([128, 4, E]),
                iota_e[:, 0:E].unsqueeze(1).broadcast_to([128, 4, E]),
                ALU.is_equal),
                reads=["e_easgA", "consts"], writes=["e_oh4"], tag="e_oh2")
            p.op("tensor", lambda e: e.matmul(ps6[0:1, 0:4 * E], ones_col[:, 0:1],
                 oh4[:, 0:4 * E], start=True, stop=True),
                 reads=["e_oh4", "consts"], writes=["ps6"], tag="e_tot")
            p.op("vector", lambda e: e.tensor_copy(rows2[0:1, 8 * E:9 * E],
                 rows2[0:1, 12 * E:13 * E]) if False else e.tensor_copy(
                 rows2[0:1, 12 * E:13 * E], rows2[0:1, 8 * E:9 * E]),
                 reads=["e_rows"], writes=["e_rows"], tag="e_cc0")
            for b in range(3):
                p.op("vector", lambda e, b=b: e.tensor_tensor(
                    rows2[0:1, (13 + b) * E:(14 + b) * E],
                    rows2[0:1, (12 + b) * E:(13 + b) * E],
                    ps6[0:1, b * E:(b + 1) * E], ALU.add),
                    reads=["ps6", "e_rows"], writes=["e_rows"], tag="e_cc")
            p.op("vector", lambda e: e.tensor_tensor(
                rows2[0:1, 8 * E:9 * E], rows2[0:1, 15 * E:16 * E],
                ps6[0:1, 3 * E:4 * E], ALU.add),
                reads=["ps6", "e_rows"], writes=["e_rows"], tag="e_cup")
            p.op("tensor", lambda e: e.matmul(ps4[:, 0:4 * E], triu_s,
                 oh4[:, 0:4 * E], start=True, stop=False),
                 reads=["e_oh4", "consts"], writes=["ps4"], tag="e_pos1")
            p.op("tensor", lambda e: e.matmul(ps4[:, 0:4 * E], ones_row[0:1, :],
                 rows2[0:1, 12 * E:16 * E], start=False, stop=True),
                 reads=["e_rows", "consts"], writes=["ps4"], tag="e_pos2")
            p.op("vector", lambda e: e.tensor_tensor(
                ctx.sb["e_pp"][:, 0:4 * E], ps4[:, 0:4 * E], oh4[:, 0:4 * E],
                ALU.mult), reads=["ps4", "e_oh4"], writes=["e_pp"], tag="e_ps1")
            p.op("vector", lambda e: e.tensor_reduce(
                pc[:, 0:4],
                ctx.sb["e_pp"][:, 0:4 * E].rearrange("p (b e) -> p b e", e=E),
                AX.X, ALU.add), reads=["e_pp"], writes=["e_pc"], tag="e_ps2")
            p.op("vector", lambda e: e.tensor_tensor(
                ctx.sb["e_pp"][:, 0:4 * E], ps5[:, 0:4 * E], oh4[:, 0:4 * E],
                ALU.mult), reads=["ps5", "e_oh4"], writes=["e_pp"], tag="e_ss1")
            p.op("vector", lambda e: e.tensor_reduce(
                pc[:, 4:8],
                ctx.sb["e_pp"][:, 0:4 * E].rearrange("p (b e) -> p b e", e=E),
                AX.X, ALU.add), reads=["e_pp"], writes=["e_pc"], tag="e_ss2")
            p.op("vector", lambda e: e.tensor_tensor(
                pc[:, 8:12], pc[:, 0:4], pc[:, 4:8], ALU.add),
                reads=["e_pc"], writes=["e_pc"], tag="e_pg")
            p.op("vector", lambda e: e.tensor_scalar(
                pc[:, 12:16], pc[:, 8:12], float(CAP), None, ALU.is_lt),
                reads=["e_pc"], writes=["e_pc"], tag="e_keep")
            p.op("vector", lambda e, t0=t0: e.tensor_scalar(
                pc[:, 16:20], easgA[:, t0:t0 + 4], float(CAP), None, ALU.mult),
                reads=["e_easgA", "e_pc"], writes=["e_pc"], tag="e_sl1")
            p.op("vector", lambda e: e.tensor_tensor(
                pc[:, 16:20], pc[:, 16:20], pc[:, 8:12], ALU.add),
                reads=["e_pc"], writes=["e_pc"], tag="e_sl2")
            p.op("vector", lambda e: e.tensor_scalar(
                pc[:, 20:24], pc[:, 12:16], 1e9, -1e9, ALU.mult, ALU.add),
                reads=["e_pc"], writes=["e_pc"], tag="e_sl3")
            p.op("vector", lambda e: e.tensor_tensor(
                pc[:, 16:20], pc[:, 16:20], pc[:, 20:24], ALU.subtract),
                reads=["e_pc"], writes=["e_pc"], tag="e_sl4")
            p.op("vector", lambda e: e.tensor_scalar(
                pc[:, 16:20], pc[:, 16:20], float(E * CAP), None, ALU.min),
                reads=["e_pc"], writes=["e_pc"], tag="e_sl4b")
            p.op("vector", lambda e, t0=t0: e.tensor_copy(
                slots[:, t0:t0 + 4], pc[:, 16:20]),
                reads=["e_pc"], writes=["e_slots"], tag="e_sl5")
            p.op("vector", lambda e, t0=t0: e.tensor_scalar(
                ctx.sb["e_slotg"][:, t0:t0 + 4], pc[:, 16:20],
                float(E * CAP - 1), None, ALU.min),
                reads=["e_pc"], writes=["e_slotg"], tag="e_sl6")
            p.op("vector", lambda e, t0=t0: e.tensor_tensor(
                wfin[:, t0:t0 + 4], easgA[:, NAT + t0:NAT + t0 + 4],
                pc[:, 12:16], ALU.mult),
                reads=["e_easgA", "e_pc"], writes=["e_wfin"], tag="e_wf")
            if kind == "l2":
                p.op("vector", lambda e, bi2=bi: e.tensor_scalar(
                    pc[:, 28:32], ctx.sb["e_tokpat"][:, 0:4], float(bi2 * 64),
                    None, ALU.add),
                    reads=["e_tokpat", "e_pc"], writes=["e_pc"], tag="e_tid")
            else:
                ti = (t0 % TPG)
                p.op("vector", lambda e, ti=ti: e.tensor_scalar(
                    pc[:, 28:32], ctx.sb["e_tokpat"][:, 4:8],
                    float(ti * 128), None, ALU.add),
                    reads=["e_tokpat", "e_pc"], writes=["e_pc"], tag="e_tid0")
            for b in range(4):
                p.op("gpsimd", lambda e, b=b, t0=t0: e.indirect_dma_start(
                    out=spec["MAPI"][:, :],
                    out_offset=IOffs(slots[:, t0 + b:t0 + b + 1], 0),
                    in_=pc[:, 28 + b:29 + b], in_offset=None),
                    reads=["e_pc", "e_slots"], writes=["MAPI"], kind="dma",
                    tag="e_ctsc")
    _bar("e_barm0", ["MAPI"], ["MAPI"])
    p.op("gpsimd", lambda e: e.collective_compute(
        "ReduceScatter", ALU.add, replica_groups=[list(range(8))],
        ins=[spec["MAPI"][0:E * CAP, :].opt()], outs=[spec["MAPD"][:, :].opt()]),
        reads=["MAPI"], writes=["MAPD"], kind="coll", tag="e_rsm")
    p.op("gpsimd", lambda e: e.collective_compute(
        "AllReduce", ALU.add, replica_groups=[list(range(8))],
        ins=[spec["BARI"][:, :].opt()], outs=[spec["BARO"][:, :].opt()]),
        reads=["BARI"], writes=["XALL", "MAPD"], kind="coll", tag="e_bar1")
    if spec.get("delay1", 0):
        for dd in range(spec["delay1"]):
            p.op("vector", lambda e: e.memset(ctx.sb["e_hsb"][:, 0:8192], 0.0),
                 reads=["XALL", "MAPD"], writes=["XALL", "MAPD", "e_hsb"],
                 tag="e_dly1")
    NCAP = CAP // 128
    mapc = ctx.sb["e_mapc"]
    for le in range(Eloc):
        p.dma("sync", ctx.sb["e_mapf"][0:NCAP, 0:128],
              spec["MAPD"][le * CAP:(le + 1) * CAP, 0:1]
              .rearrange("(a q) c -> a (q c)", q=128),
              reads=["MAPD"], writes=["e_mapf"], tag="e_mld")
        p.op("tensor", lambda e: e.transpose(ps4[:, 0:NCAP],
             ctx.sb["e_mapf"][0:NCAP, 0:128], ident[0:NCAP, 0:NCAP]),
             reads=["e_mapf", "consts"], writes=["ps4"], tag="e_mtr")
        p.op("vector", lambda e: e.tensor_copy(mapc[:, 0:NCAP], ps4[:, 0:NCAP]),
             reads=["ps4"], writes=["e_mapc"], tag="e_mcast")
        spec["ffn"](ctx, le, None)
    _bar("e_bare0", ["EOI"], ["EOI"])
    p.op("gpsimd", lambda e: e.collective_compute(
        "AllGather", ALU.bypass, replica_groups=[list(range(8))],
        ins=[spec["EOI"][:, :].opt()], outs=[spec["EOALL"][:, :].opt()]),
        reads=["EOI"], writes=["EOALL"], kind="coll", tag="e_ageo")
    if "mid" in spec:
        spec["mid"]()
    p.op("gpsimd", lambda e: e.collective_compute(
        "AllReduce", ALU.add, replica_groups=[list(range(8))],
        ins=[spec["BARI"][:, :].opt()], outs=[spec["BARO"][:, :].opt()]),
        reads=["BARI", "EOALL"], writes=["EOALL"], kind="coll", tag="e_bar2")
    if spec.get("delay", 0):
        for dd in range(spec["delay"]):
            p.op("vector", lambda e: e.memset(ctx.sb["e_hsb"][:, 0:8192], 0.0),
                 reads=["EOALL"], writes=["EOALL", "e_hsb"], tag="e_dly")
    gkbufs = (ctx.sb["e_dtok"], ctx.sb["e_dspf"])
    gw = ctx.sb.get("e_gw")
    stg = ctx.sb.get("e_stg")
    for b4 in range(NAT // 4):
        for bb in range(4):
            i = b4 * 4 + bb
            gk = gkbufs[i % 2]
            gkkey = "e_dtok" if i % 2 == 0 else "e_dspf"
            p.op("gpsimd", lambda e, i=i, gk=gk: e.indirect_dma_start(
                out=gk[:, 0:1024], out_offset=None,
                in_=spec["EOALL"][:, :],
                in_offset=IOffs(ctx.sb["e_slotg"][:, i:i + 1], 0)),
                reads=["EOALL", "e_slotg"], writes=[gkkey], kind="dma",
                tag="e_cgat")
            if i == 0 and "snap" in spec:
                p.op("vector", lambda e: e.tensor_copy(
                    ctx.sb["e_gk0"][:, 0:1024], gk[:, 0:1024]),
                    reads=["e_dtok"], writes=["e_gk0"], tag="e_snap")
            if kind == "l2":
                p.op("vector", lambda e, i=i: e.tensor_scalar(
                    gw[:, 0:16], ctx.cap("gmask16")[:, 0:16], wfin[:, i:i + 1],
                    None, ALU.mult),
                    reads=["consts", "e_wfin"], writes=["e_gw"], tag="e_glh")
                for nn in range(2):
                    p.op("tensor", lambda e, nn=nn, gk=gk: e.matmul(
                        ps4[0:16, 0:512] if nn == 0 else ps5[0:16, 0:512],
                        gw[:, 0:16], gk[:, nn * 512:(nn + 1) * 512],
                        start=True, stop=True),
                        reads=["e_gw", gkkey], writes=[f"ps{4 + nn}"],
                        tag="e_gmm")
                for nn in range(2):
                    p.op("scalar", lambda e, nn=nn: e.activation(
                        stg[0:16, nn * 512:(nn + 1) * 512],
                        (ps4 if nn == 0 else ps5)[0:16, 0:512], AF.Identity),
                        reads=[f"ps{4 + nn}"], writes=["e_stg"], tag="e_gev")
                p.dma("sync", spec["COMB"][i * 16:(i + 1) * 16, :],
                      stg[0:16, 0:1024], reads=["e_stg"], writes=["COMB"],
                      tag="e_gout")
            else:
                jj = i // NT
                tl = i % NT
                p.op("vector", lambda e, i=i, gk=gk: e.tensor_scalar(
                    gk[:, 0:1024], gk[:, 0:1024],
                    wfin[:, i:i + 1], None, ALU.mult),
                    reads=[gkkey, "e_wfin"], writes=[gkkey], tag="e_gmw")
                if jj == 0:
                    p.dma("gpsimd", spec["COMB"][tl * 128:(tl + 1) * 128, :],
                          gk[:, 0:1024],
                          reads=[gkkey], writes=["COMB"], tag="e_gc1")
                else:
                    p.dma("gpsimd", spec["COMB"][tl * 128:(tl + 1) * 128, :],
                          gk[:, 0:1024], accum_op=ALU.add,
                          reads=[gkkey], writes=["COMB"], tag="e_gc2")
    xdst = ctx.sb[x_name]
    for k in range(KD):
        p.dma("scalar", ctx.sb["e_combT"][:, 0:T],
              spec["COMB"][:, k * 128:(k + 1) * 128], transpose=True,
              reads=["COMB"], writes=["e_combT"], tag="e_ctr")
        p.op("vector", lambda e, k=k: e.tensor_tensor(
            xdst[:, k * T:(k + 1) * T], xdst[:, k * T:(k + 1) * T],
            ctx.sb["e_combT"][:, 0:T], ALU.add),
            reads=["e_combT", x_name], writes=[x_name], tag="e_cadd")


def make_moe_dispatch_ffn(ctx, spec):
    p = ctx.p
    E, K, CAP = spec["E"], spec["K"], spec["CAP"]
    kind = spec["kind"]
    identb = ctx.cap("ident_bf")
    ident = ctx.cap("ident")
    psb = ctx.ps["ps7"]
    ps0, ps1, ps2, ps3, ps4, ps5 = (ctx.ps[f"ps{i}"] for i in range(6))
    wsl = ctx.sb["wbuf"]
    dtik = ctx.sb["e_dtok"]
    dspf = ctx.sb["e_dspf"]
    hsb = ctx.sb["e_hsb"]
    eosb = ctx.sb["e_eosb"]
    mapc = ctx.sb["e_mapc"]

    def ffn(ctx, le, _):
        NCH_S = CAP // 512
        for scb in range(NCH_S):
            for it in range(4):
                i = scb * 4 + it
                p.op("gpsimd", lambda e, i=i: e.indirect_dma_start(
                    out=dtik[:, 0:1024], out_offset=None,
                    in_=spec["XALL"][:, :],
                    in_offset=IOffs(mapc[:, i:i + 1], 0)),
                    reads=["XALL", "e_mapc"], writes=["e_dtok"], kind="dma",
                    tag="e_dgat")
                for k in range(KD):
                    p.op("tensor", lambda e, k=k: e.transpose(
                        psb[:, k * 128:(k + 1) * 128],
                        dtik[:, k * 128:(k + 1) * 128], identb),
                        reads=["e_dtok", "consts"], writes=["ps7"], tag="e_dtr")
                p.op("scalar", lambda e, it=it: e.activation(
                    dspf[:, 0:8 * 512].rearrange("p (k s) -> p k s", s=512)
                    [:, :, it * 128:(it + 1) * 128],
                    psb[:, 0:1024].rearrange("p (k c) -> p k c", c=128),
                    AF.Identity),
                    reads=["ps7"], writes=["e_dspf"], tag="e_dev")
            if "DDSP" in spec and le == spec["E"] // 8 - 1 and scb == 0:
                p.dma("sync", spec["DDSP"][:, :], dspf[:, 0:8 * 512],
                      reads=["e_dspf"], writes=["D"], tag="e_snapd")
            if kind == "l0":
                H2 = 32
                for m in range(H2):
                    slot = m % 2
                    wv = wsl[:, slot * 1024:slot * 1024 + 1024]
                    p.dma("sync", wv.rearrange("p (k c) -> p k c", c=128),
                          spec["WFC1"][:, m * 128:(m + 1) * 128]
                          .rearrange("(k p) c -> p k c", p=128),
                          reads=["WFC1b"], writes=[f"wb{slot}"], tag="e_w1ld")
                    pb = ps0 if slot == 0 else ps1
                    for k in range(KD):
                        p.op("tensor", lambda e, k=k, wv=wv, pb=pb: e.matmul(
                            pb[:, :], wv[:, k * 128:(k + 1) * 128],
                            dspf[:, k * 512:(k + 1) * 512],
                            start=(k == 0), stop=(k == KD - 1)),
                            reads=[f"wb{slot}", "e_dspf"], writes=[f"ps{slot}"],
                            tag="e_fc1")
                    p.op("scalar", lambda e, m=m, pb=pb: e.activation(
                        hsb[:, m * 512:(m + 1) * 512], pb[:, :], AF.Relu,
                        bias=ctx.sb["e_b1"][:, m:m + 1]),
                        reads=[f"ps{slot}", "e_b1"], writes=["e_hsb"],
                        tag="e_fc1ev")
                for d in range(KD):
                    wv = wsl[:, 2048:2048 + 2048]
                    pb = ps2 if d % 2 == 0 else ps3
                    for dh in range(2):
                        p.dma("sync", wv.rearrange("p (k c) -> p k c", c=128),
                              spec["WFC2"][dh * 2048:(dh + 1) * 2048,
                                           d * 128:(d + 1) * 128]
                              .rearrange("(k p) c -> p k c", p=128),
                              reads=["WFC2b"], writes=["wc0"], tag="e_w2ld")
                        for hh in range(16):
                            p.op("tensor", lambda e, hh=hh, dh=dh, wv=wv,
                                 pb=pb: e.matmul(
                                pb[:, :], wv[:, hh * 128:(hh + 1) * 128],
                                hsb[:, (dh * 16 + hh) * 512:
                                     (dh * 16 + hh + 1) * 512],
                                start=(dh == 0 and hh == 0),
                                stop=(dh == 1 and hh == 15)),
                                reads=["wc0", "e_hsb"],
                                writes=[f"ps{2 + d % 2}"], tag="e_fc2")
                    p.op("scalar", lambda e, d=d, pb=pb: e.activation(
                        eosb[:, d * 512:(d + 1) * 512], pb[:, :], AF.Identity,
                        bias=ctx.sb["e_b2"][:, d:d + 1]),
                        reads=[f"ps{2 + d % 2}", "e_b2"], writes=["e_eosb"],
                        tag="e_fc2ev")
            else:
                for m in range(KD):
                    slot = m % 2
                    wv = wsl[:, slot * 1024:slot * 1024 + 1024]
                    p.dma("sync", wv.rearrange("p (k c) -> p k c", c=128),
                          spec["WGU"][:, le * 1024 + m * 128:le * 1024 + (m + 1) * 128]
                          .rearrange("(k p) c -> p k c", p=128),
                          reads=["WGUb"], writes=[f"wb{slot}"], tag="e_wguld")
                    pb = ps0 if slot == 0 else ps1
                    for k in range(KD):
                        p.op("tensor", lambda e, k=k, wv=wv, pb=pb: e.matmul(
                            pb[:, :], wv[:, k * 128:(k + 1) * 128],
                            dspf[:, k * 512:(k + 1) * 512],
                            start=(k == 0), stop=(k == KD - 1)),
                            reads=[f"wb{slot}", "e_dspf"], writes=[f"ps{slot}"],
                            tag="e_gu")
                    if m < 4:
                        p.op("vector", lambda e, m=m, pb=pb: e.tensor_scalar(
                            ctx.sb["e_gbuf"][:, m * 512:(m + 1) * 512],
                            pb[:, :], 7.0, None, ALU.min),
                            reads=[f"ps{slot}"], writes=["e_gbuf"], tag="e_gclip")
                    else:
                        p.op("vector", lambda e, m=m, pb=pb: e.tensor_scalar(
                            hsb[:, (m - 4) * 512:(m - 3) * 512],
                            pb[:, :], 7.0, -7.0, ALU.min, ALU.max),
                            reads=[f"ps{slot}"], writes=["e_hsb"], tag="e_uclip")
                for m in range(4):
                    p.op("scalar", lambda e, m=m: e.activation(
                        ctx.sb["e_gbuf"][:, m * 512:(m + 1) * 512],
                        ctx.sb["e_gbuf"][:, m * 512:(m + 1) * 512], AF.Silu),
                        reads=["e_gbuf"], writes=["e_gbuf"], tag="e_gsilu")
                    p.op("vector", lambda e, m=m: e.tensor_tensor(
                        hsb[:, m * 512:(m + 1) * 512],
                        hsb[:, m * 512:(m + 1) * 512],
                        ctx.sb["e_gbuf"][:, m * 512:(m + 1) * 512], ALU.mult),
                        reads=["e_gbuf", "e_hsb"], writes=["e_hsb"], tag="e_h")
                for d in range(KD):
                    slot = d % 2
                    wv = wsl[:, 2048 + slot * 512:2048 + slot * 512 + 512]
                    p.dma("sync", wv.rearrange("p (k c) -> p k c", c=128),
                          spec["WDN"][:, le * 1024 + d * 128:le * 1024 + (d + 1) * 128]
                          .rearrange("(k p) c -> p k c", p=128),
                          reads=["WDNb"], writes=[f"wc{slot}"], tag="e_wdld")
                    pb = ps2 if slot == 0 else ps3
                    for hh in range(4):
                        p.op("tensor", lambda e, hh=hh, wv=wv, pb=pb: e.matmul(
                            pb[:, :], wv[:, hh * 128:(hh + 1) * 128],
                            hsb[:, hh * 512:(hh + 1) * 512],
                            start=(hh == 0), stop=(hh == 3)),
                            reads=[f"wc{slot}", "e_hsb"], writes=[f"ps{2 + slot}"],
                            tag="e_dn")
                    p.op("scalar", lambda e, d=d, pb=pb: e.activation(
                        eosb[:, d * 512:(d + 1) * 512], pb[:, :], AF.Identity),
                        reads=[f"ps{2 + slot}"], writes=["e_eosb"], tag="e_dnev")
            for it in range(4):
                for k in range(KD):
                    p.op("tensor", lambda e, it=it, k=k: e.transpose(
                        psb[:, k * 128:(k + 1) * 128],
                        eosb[:, 0:8 * 512].rearrange("p (k s) -> p k s", s=512)
                        [:, k, it * 128:(it + 1) * 128], identb),
                        reads=["e_eosb", "consts"], writes=["ps7"], tag="e_etr")
                p.op("scalar", lambda e: e.activation(
                    dtik[:, 0:1024], psb[:, 0:1024], AF.Identity),
                    reads=["ps7"], writes=["e_dtok"], tag="e_eev")
                p.dma("sync",
                      spec["EOI"][le * CAP + scb * 512 + it * 128:
                                  le * CAP + scb * 512 + (it + 1) * 128, :],
                      dtik[:, 0:1024], reads=["e_dtok"], writes=["EOI"],
                      tag="e_eout")
            if "DEOI2" in spec and le == spec["E"] // 8 - 1 and scb == 0:
                p.dma("sync", spec["DEOI2"][:, :],
                      spec["EOI"][le * CAP:le * CAP + 512, :],
                      reads=["EOI"], writes=["D"], tag="e_snap2")
    return ffn


def emit_l2shared(ctx, xn_name, x_name, SG, SU, SDT, SGB=None, SUB=None,
                  SDB=None):
    p = ctx.p
    xn = ctx.sb[xn_name]
    xd = ctx.sb[x_name]
    wsl = ctx.sb["wbuf"]
    p.dma("gpsimd", SGB[:, :], SG[:, :], writes=["SGBb"], tag="sh_cg")
    p.dma("gpsimd", SUB[:, :], SU[:, :], writes=["SUBb"], tag="sh_cu")
    p.dma("gpsimd", SDB[:, :], SDT[:, :], writes=["SDBb"], tag="sh_cd")
    SG, SU, SDT = SGB, SUB, SDB
    hsb = ctx.sb["e_hsb"]
    gb = ctx.sb["e_gbuf"]
    ps0, ps1, ps2, ps3 = (ctx.ps[f"ps{i}"] for i in range(4))
    for tc in range(NCH):
        t0 = tc * 512
        for m in range(16):
            slot = m % 2
            wv = wsl[:, slot * 1024:slot * 1024 + 1024]
            p.dma("sync", wv.rearrange("p (k c) -> p k c", c=128),
                  SG[:, m * 128:(m + 1) * 128].rearrange("(k p) c -> p k c", p=128),
                  reads=["SGBb"], writes=[f"wb{slot}"], tag="sh_g")
            pb = ps0 if slot == 0 else ps1
            for k in range(KD):
                p.op("tensor", lambda e, k=k, wv=wv, pb=pb, t0=t0: e.matmul(
                    pb[:, :], wv[:, k * 128:(k + 1) * 128],
                    xn[:, k * T + t0:k * T + t0 + 512],
                    start=(k == 0), stop=(k == KD - 1)),
                    reads=[f"wb{slot}", xn_name], writes=[f"ps{slot}"],
                    tag="sh_gmm")
            p.op("scalar", lambda e, m=m, pb=pb: e.activation(
                hsb[:, m * 512:(m + 1) * 512], pb[:, :], AF.Silu),
                reads=[f"ps{slot}"], writes=["e_hsb"], tag="sh_gsil")
        for m in range(16):
            slot = m % 2
            wv = wsl[:, 2048 + slot * 1024:2048 + slot * 1024 + 1024]
            p.dma("sync", wv.rearrange("p (k c) -> p k c", c=128),
                  SU[:, m * 128:(m + 1) * 128].rearrange("(k p) c -> p k c", p=128),
                  reads=["SUBb"], writes=[f"wc{slot}"], tag="sh_u")
            pb = ps2 if slot == 0 else ps3
            for k in range(KD):
                p.op("tensor", lambda e, k=k, wv=wv, pb=pb, t0=t0: e.matmul(
                    pb[:, :], wv[:, k * 128:(k + 1) * 128],
                    xn[:, k * T + t0:k * T + t0 + 512],
                    start=(k == 0), stop=(k == KD - 1)),
                    reads=[f"wc{slot}", xn_name], writes=[f"ps{2 + slot}"],
                    tag="sh_umm")
            p.op("vector", lambda e, m=m, pb=pb: e.tensor_tensor(
                hsb[:, m * 512:(m + 1) * 512], hsb[:, m * 512:(m + 1) * 512],
                pb[:, :], ALU.mult),
                reads=[f"ps{2 + slot}", "e_hsb"], writes=["e_hsb"], tag="sh_h")
        for d in range(KD):
            slot = d % 2
            wv = wsl[:, 4096 + slot * 2048:4096 + slot * 2048 + 2048]
            p.dma("sync", wv.rearrange("p (k c) -> p k c", c=128),
                  SDT[:, d * 128:(d + 1) * 128].rearrange("(k p) c -> p k c", p=128),
                  reads=["SDBb"], writes=[f"wd{slot}"], tag="sh_d")
            pb = ps0 if slot == 0 else ps1
            for hh in range(16):
                p.op("tensor", lambda e, hh=hh, wv=wv, pb=pb: e.matmul(
                    pb[:, :], wv[:, hh * 128:(hh + 1) * 128],
                    hsb[:, hh * 512:(hh + 1) * 512],
                    start=(hh == 0), stop=(hh == 15)),
                    reads=[f"wd{slot}", "e_hsb"], writes=[f"ps{slot}"],
                    tag="sh_dmm")
            p.op("vector", lambda e, d=d, pb=pb, t0=t0: e.tensor_tensor(
                xd[:, d * T + t0:d * T + t0 + 512],
                xd[:, d * T + t0:d * T + t0 + 512], pb[:, :], ALU.add),
                reads=[f"ps{slot}", x_name], writes=[x_name], tag="sh_add")



MOE_PTAGS = set("e_mld,e_mtr,e_mcast,e_dgat,e_dtr,e_dev,e_wguld,e_gu,"
                "e_gclip,e_uclip,e_gsilu,e_h,e_wdld,e_dn,e_dnev,e_etr,e_eev,"
                "e_eout,e_bare0,e_ageo,e_bar2,e_cgat,e_glh,e_gmm,e_gev,"
                "e_gout,e_ctr,e_cadd,sh_g,sh_gmm,sh_gsil,sh_u,sh_umm,sh_h,"
                "sh_d,sh_dmm,sh_add,st,e_w1ld,e_fc1,e_fc1ev,e_w2ld,e_fc2,"
                "e_fc2ev,e_gmw,e_gc1,e_gc2".split(","))


def emit_ln_stream(ctx, XT, xn_name, xbuf_name, stats_name=None,
                   sq_name="ln_sq", row_name="ln_rows", row_off=0,
                   banks=("ps0", "ps1", "ps2", "ps3")):
    p = ctx.p
    xb = ctx.sb[xbuf_name]
    xn = ctx.sb[xn_name]
    sq = ctx.sb[sq_name]
    rows = ctx.sb[row_name][:, row_off:row_off + 2048]
    ones_col = ctx.cap("ones_col")[:, 0:1]
    ones_row = ctx.cap("ones_row")[0:1, :]
    eps_col = ctx.cap("eps_col")
    TC = 512
    pm, pq, pmb, prb = (ctx.ps[b] for b in banks)
    kpm, kpq, kmb, krb = banks
    for tc in range(T // TC):
        for k in range(KD):
            p.dma("sync", xb[:, k * TC:(k + 1) * TC],
                  XT[k * 128:(k + 1) * 128, tc * TC:(tc + 1) * TC],
                  writes=[xbuf_name], tag="lnx")
        for k in range(KD):
            xs = xb[:, k * TC:(k + 1) * TC]
            p.op("tensor", lambda e, xs=xs, k=k: e.matmul(
                pm[0:1, :], ones_col, xs, start=(k == 0), stop=(k == KD - 1)),
                reads=[xbuf_name, "consts"], writes=[kpm], tag="ln_sum")
            p.op("scalar", lambda e, xs=xs: e.activation(sq[:, :], xs, AF.Square),
                 reads=[xbuf_name], writes=[sq_name], tag="ln_sq")
            p.op("tensor", lambda e, k=k: e.matmul(
                pq[0:1, :], ones_col, sq[:, :], start=(k == 0), stop=(k == KD - 1)),
                reads=[sq_name, "consts"], writes=[kpq], tag="ln_sum2")
        m = rows[0:1, 0:TC]
        s2 = rows[0:1, TC:2 * TC]
        r = rows[0:1, 2 * TC:3 * TC]
        mr = rows[0:1, 3 * TC:4 * TC]
        p.op("vector", lambda e, m=m: e.tensor_scalar(
            m, pm[0:1, :], 1.0 / 1024, None, ALU.mult),
            reads=[kpm], writes=[row_name], tag="ln_m")
        p.op("vector", lambda e, s2=s2: e.tensor_scalar(
            s2, pq[0:1, :], 1.0 / 1024, None, ALU.mult),
            reads=[kpq], writes=[row_name], tag="ln_s2")
        p.op("vector", lambda e, m=m, mr=mr: e.tensor_tensor(
            mr, m, m, ALU.mult), reads=[row_name], writes=[row_name], tag="ln_msq")
        p.op("vector", lambda e, s2=s2, mr=mr: e.tensor_tensor(
            s2, s2, mr, ALU.subtract), reads=[row_name], writes=[row_name],
            tag="ln_v")
        p.op("scalar", lambda e, s2=s2, r=r: e.activation(
            r, s2, AF.Sqrt, bias=eps_col[0:1, 0:1]),
            reads=[row_name, "consts"], writes=[row_name], tag="ln_sqrt")
        p.op("vector", lambda e, r=r: e.reciprocal(r, r),
             reads=[row_name], writes=[row_name], tag="ln_rstd")
        if stats_name is not None:
            st = ctx.sb[stats_name]
            p.op("vector", lambda e, m=m, tc=tc: e.tensor_copy(
                st[0:1, tc * TC:(tc + 1) * TC], m),
                reads=[row_name], writes=[stats_name], tag="ln_stm")
            p.op("vector", lambda e, r=r, tc=tc: e.tensor_copy(
                st[0:1, T + tc * TC:T + (tc + 1) * TC], r),
                reads=[row_name], writes=[stats_name], tag="ln_str")
        p.op("vector", lambda e, m=m, r=r, mr=mr: e.tensor_tensor(
            mr, m, r, ALU.mult), reads=[row_name], writes=[row_name], tag="ln_mr")
        p.op("vector", lambda e, mr=mr: e.tensor_scalar(
            mr, mr, -1.0, None, ALU.mult), reads=[row_name], writes=[row_name],
            tag="ln_mrneg")
        p.op("tensor", lambda e, mr=mr: e.matmul(pmb[:, 0:TC], ones_row, mr,
             start=True, stop=True), reads=[row_name, "consts"],
             writes=[kmb], tag="ln_bcast_m")
        p.op("tensor", lambda e, r=r: e.matmul(prb[:, 0:TC], ones_row, r,
             start=True, stop=True), reads=[row_name, "consts"],
             writes=[krb], tag="ln_bcast_r")
        for k in range(KD):
            xs = xb[:, k * TC:(k + 1) * TC]
            xns = xn[:, k * T + tc * TC:k * T + (tc + 1) * TC]
            p.op("vector", lambda e, xs=xs: e.tensor_tensor(
                sq[:, :], xs, prb[:, 0:TC], ALU.mult),
                reads=[xbuf_name, krb], writes=[sq_name], tag="ln_n1")
            p.op("vector", lambda e, xns=xns: e.tensor_tensor(
                xns, sq[:, :], pmb[:, 0:TC], ALU.add),
                reads=[sq_name, kmb], writes=[xn_name], tag="ln_n2")


def emit_resid_out(ctx, XT, XO, terms, xbuf_name="ln_xb"):
    p = ctx.p
    xb = ctx.sb[xbuf_name]
    TC = 512
    for tc in range(T // TC):
        for k in range(KD):
            p.dma("sync", xb[:, k * TC:(k + 1) * TC],
                  XT[k * 128:(k + 1) * 128, tc * TC:(tc + 1) * TC],
                  writes=[xbuf_name], tag="rox")
        for k in range(KD):
            xs = xb[:, k * TC:(k + 1) * TC]
            for tname in terms:
                tb = ctx.sb[tname]
                p.op("vector", lambda e, xs=xs, tb=tb, k=k, tc=tc:
                     e.tensor_tensor(xs, xs,
                                     tb[:, k * T + tc * TC:k * T + (tc + 1) * TC],
                                     ALU.add),
                     reads=[tname, xbuf_name], writes=[xbuf_name], tag="roadd")
        for k in range(KD):
            p.dma("sync", XO[k * 128:(k + 1) * 128, tc * TC:(tc + 1) * TC],
                  xb[:, k * TC:(k + 1) * TC], reads=[xbuf_name],
                  writes=["XOUT"], tag="st", final=True)


def new_ctx():
    nc = bass.Bass("TRN2", target_bir_lowering=False)
    p = Prog(nc)
    es = ExitStack()
    ctx = Ctx(nc, p, es)
    return nc, p, es, ctx


def build_l0a():
    nc, p, es, ctx = new_ctx()
    XT = ctx.dram_t("XT", (1024, T), F32, kind="ExternalInput")
    INPJ = ctx.dram_t("INPJ", (1024, 5152), F32, kind="ExternalInput")
    OUTPJ = ctx.dram_t("OUTPJ", (2048, 1024), F32, kind="ExternalInput")
    COLS = ctx.dram_t("COLS", (128, 163), F32, kind="ExternalInput")
    ZSP = ctx.dram_t("ZSP", (2048, T), BF16)
    XOI = ctx.dram_t("XOI", (1024, T), F32)
    XO = ctx.dram_t("XO", (1024, T), F32, kind="ExternalOutput")
    ctx.sbuf("xn", (128, KD * T), BF16)
    ctx.sbuf("oevb", (128, 512), F32)
    ctx.sbuf("ln_sq", (128, 512), F32)
    ctx.sbuf("l0cols", (128, 163), F32)
    ctx.sbuf("m_rows", (128, T), F32)
    ctx.sbuf("m_diag", (128, 4 * 128), BF16)
    ctx.sbuf("m_xmq", (128, 8 * 512), BF16)
    ctx.sbuf("m_zq", (128, 16 * 512), BF16)
    ctx.sbuf("m_xbc", (128, 24 * 512), BF16)
    ctx.sbuf("m_xcp", (128, 24 * 515), BF16)
    ctx.sbuf("m_dt", (128, 6 * 512), F32)
    ctx.sbuf("m_thp", (128, 64), F32)
    ctx.sbuf("m_rowR", (128, 4096), F32)
    ctx.sbuf("m_yq", (128, 16 * 512), BF16)
    ctx.sbuf("m_wexp", (128, 256), F32)
    ctx.sbuf("m_wbf", (128, 128), BF16)
    ctx.sbuf("m_xtok", (128, 64), BF16)
    ctx.sbuf("m_btok", (128, 8 * 64), BF16)
    ctx.sbuf("m_chat", (128, 192), BF16)
    ctx.sbuf("m_gsc", (128, 512), F32)
    ctx.sbuf("m_S", (128, 32 * 64), F32)
    ctx.sbuf("m_Sb", (128, 32 * 64), BF16)
    ctx.sbuf("wbuf", (128, 4096), BF16)
    for i in range(7):
        ctx.psum(f"ps{i}")
    ctx.psum("ps7", (128, 1024), BF16)
    register_common_consts(ctx, skip=("mask_triu_incl", "mask_triu_strict",
                                      "mask_tril_strict", "iota_row"))
    packs = ctx.pack_consts()
    p.dma("sync", ctx.sb["l0cols"][:, :], COLS[:, :], writes=["l0cols"], tag="ldc")
    emit_ln_stream(ctx, XT, "xn", "m_rowR", sq_name="m_gsc", row_name="m_rows")
    emit_resid_out(ctx, XT, XOI, ["xn"], xbuf_name="m_rowR")

    def epi(m, q, pb, bank):
        p.op("scalar", lambda e, pb=pb: e.activation(
            ctx.sb["oevb"][:, :], pb[:, :], AF.Identity),
            reads=[bank], writes=["oevb"], tag="oev")
        p.dma("gpsimd", XOI[m * 128:(m + 1) * 128, q * 512:(q + 1) * 512],
              ctx.sb["oevb"][:, :], accum_op=ALU.add,
              reads=["oevb"], writes=["XOUT"], tag="acc")
    emit_mamba(ctx, "xn", INPJ, OUTPJ, ZSP, "l0cols", epi)
    for k in range(KD):
        for tc2 in range(4):
            p.dma("sync", ctx.sb["m_rowR"][:, 0:512],
                  XOI[k * 128:(k + 1) * 128, tc2 * 512:(tc2 + 1) * 512],
                  reads=["XOUT"], writes=["m_rowR"], tag="cpl")
            p.dma("sync", XO[k * 128:(k + 1) * 128, tc2 * 512:(tc2 + 1) * 512],
                  ctx.sb["m_rowR"][:, 0:512], reads=["m_rowR"],
                  writes=["XOF"], tag="st", final=True)
    stats = p.finalize()
    es.close()
    return nc, packs


def build_l1a():
    nc, p, es, ctx = new_ctx()
    XT = ctx.dram_t("XT", (1024, T), F32, kind="ExternalInput")
    WQ = ctx.dram_t("WQ", (1024, 1024), F32, kind="ExternalInput")
    WK = ctx.dram_t("WK", (1024, 1024), F32, kind="ExternalInput")
    WV = ctx.dram_t("WV", (1024, 1024), F32, kind="ExternalInput")
    WB = ctx.dram_t("WB", (128, 8), F32, kind="ExternalInput")
    LNG = ctx.dram_t("LNG", (1, 2048), F32, kind="ExternalInput")
    XO = ctx.dram_t("XO", (1024, T), F32, kind="ExternalOutput")
    ctx.sbuf("xn", (128, KD * T), BF16)
    ctx.sbuf("contrib", (128, KD * T), BF16)
    ctx.sbuf("ln_xb", (128, KD * 512), F32)
    ctx.sbuf("ln_sq", (128, 512), F32)
    ctx.sbuf("ln_rows", (128, 2048), F32)
    ctx.sbuf("d_wq", (128, 8 * 1024), BF16)
    ctx.sbuf("d_wk", (128, 8 * 1024), BF16)
    ctx.sbuf("d_wv", (128, 8 * 1024), BF16)
    ctx.sbuf("d_wb", (128, 8), BF16)
    ctx.sbuf("d_S", (128, 8 * 1024), BF16)
    ctx.sbuf("d_kt", (128, 1024), BF16)
    ctx.sbuf("d_vt", (128, 1024), BF16)
    ctx.sbuf("d_qt", (128, 1024), BF16)
    ctx.sbuf("d_ktT", (128, 8 * 128), BF16)
    ctx.sbuf("d_qtT", (128, 8 * 128), BF16)
    ctx.sbuf("d_R", (128, 1024), BF16)
    ctx.sbuf("d_P", (128, 2 * 128), BF16)
    ctx.sbuf("d_M", (128, 128), BF16)
    ctx.sbuf("d_osb", (128, 1024), F32)
    ctx.sbuf("d_cols", (128, 8), F32)
    ctx.sbuf("d_lngb", (128, 2048), F32)
    ctx.sbuf("d_lngrow", (128, 2048), F32)
    ctx.sbuf("d_brow", (128, 256), F32)
    for i in range(7):
        ctx.psum(f"ps{i}")
    ctx.psum("ps7", (128, 1024), BF16)
    register_common_consts(ctx)
    packs = ctx.pack_consts()
    emit_ln_stream(ctx, XT, "xn", "ln_xb")
    p.op("vector", lambda e: e.memset(ctx.sb["contrib"][:, :], 0.0),
         writes=["contrib"])
    emit_deltanet(ctx, "xn", WQ, WK, WV, WB, LNG, "contrib", lngb_done=[False])
    emit_resid_out(ctx, XT, XO, ["contrib"])
    p.finalize()
    es.close()
    return nc, packs


def build_l1f():
    nc, p, es, ctx = new_ctx()
    XT = ctx.dram_t("XT", (1024, T), F32, kind="ExternalInput")
    XP = ctx.dram_t("XP", (1024, T), F32, kind="ExternalInput")
    WR = ctx.dram_t("WR", (1024, 1024), F32, kind="ExternalInput")
    WK = ctx.dram_t("WK", (1024, 1024), F32, kind="ExternalInput")
    WV = ctx.dram_t("WV", (1024, 1024), F32, kind="ExternalInput")
    MU = ctx.dram_t("MU", (128, 16), F32, kind="ExternalInput")
    XO = ctx.dram_t("XO", (1024, T), F32, kind="ExternalOutput")
    ctx.sbuf("xn", (128, KD * T), BF16)
    ctx.sbuf("ln_xb", (128, KD * 512), F32)
    ctx.sbuf("ln_sq", (128, 512), F32)
    ctx.sbuf("l1_mu", (128, 16), F32)
    ctx.sbuf("l1_xr", (128, KD * T), BF16)
    ctx.sbuf("l1_xk", (128, KD * T), BF16)
    ctx.sbuf("l1_xko", (128, KD * T), BF16)
    ctx.sbuf("l1_rbig", (128, KD * T), BF16)
    ctx.sbuf("l1_d", (128, T), F32)
    ctx.sbuf("l1_r", (128, T), F32)
    ctx.sbuf("wbuf", (128, 2 * 1024), BF16)
    for i in range(4):
        ctx.psum(f"ps{i}")
    register_common_consts(ctx, skip=("mask_sl_neg", "mask_triu_incl",
                                      "mask_triu_strict", "mask_tril_strict",
                                      "iota_row"))
    packs = ctx.pack_consts()
    p.dma("sync", ctx.sb["l1_mu"][:, :], MU[:, :], writes=["l1_mu"], tag="ldmu")
    emit_ln_stream(ctx, XP, "xn", "ln_xb", row_name="l1_d")
    emit_l1ffn(ctx, "xn", "l1_xr", WR, WK, WV)
    emit_resid_out(ctx, XT, XO, ["l1_xr"])
    p.finalize()
    es.close()
    return nc, packs


def build_attn(with_bias):
    nc, p, es, ctx = new_ctx()
    XT = ctx.dram_t("XT", (1024, T), F32, kind="ExternalInput")
    WQ = ctx.dram_t("WQ", (1024, 1024), F32, kind="ExternalInput")
    WK = ctx.dram_t("WK", (1024, 1024), F32, kind="ExternalInput")
    WV = ctx.dram_t("WV", (1024, 1024), F32, kind="ExternalInput")
    WO = ctx.dram_t("WO", (1024, 1024), F32, kind="ExternalInput")
    BIAS = ctx.dram_t("BIAS", (128, 32), F32, kind="ExternalInput")
    XO = ctx.dram_t("XO", (1024, T), F32, kind="ExternalOutput")
    ctx.sbuf("xn", (128, KD * T), BF16)
    ctx.sbuf("ln_xb", (128, KD * 512), F32)
    ctx.sbuf("ln_sq", (128, 512), F32)
    ctx.sbuf("ln_rows", (128, 2048), F32)
    ctx.sbuf("attn", (128, KD * T), BF16)
    ctx.sbuf("res", (128, KD * T), BF16)
    ctx.sbuf("mha_q", (128, 4 * T), BF16)
    ctx.sbuf("mha_k", (128, 4 * T), BF16)
    ctx.sbuf("mha_v", (128, 4 * T), F32)
    ctx.sbuf("mha_vt", (128, 16 * 64), BF16)
    ctx.sbuf("mha_pA", (128, 512), BF16)
    ctx.sbuf("mha_pB", (128, 512), BF16)
    ctx.sbuf("mha_den", (128, 1024), F32)
    ctx.sbuf("bias_sb", (128, 32), F32)
    ctx.sbuf("wbuf", (128, 2 * 1024), BF16)
    for i in range(8):
        ctx.psum(f"ps{i}")
    register_common_consts(ctx, skip=("mask_sl_neg", "mask_triu_incl",
                                      "mask_triu_strict", "mask_tril_strict",
                                      "iota_row", "ident_bf"))
    register_attn_consts(ctx)
    packs = ctx.pack_consts()
    p.dma("sync", ctx.sb["bias_sb"][:, :], BIAS[:, :], writes=["bias_sb"],
          tag="ldb")
    emit_ln_stream(ctx, XT, "xn", "ln_xb")
    p.op("vector", lambda e: e.memset(ctx.sb["res"][:, :], 0.0),
         writes=["res"], tag="zero")
    scale = 0.125 if with_bias else 1.0 / 32.0
    biases = ("bias_sb", [0, 8, 16]) if with_bias else None
    if with_bias:
        p.op("vector", lambda e: e.tensor_scalar(
            ctx.sb["bias_sb"][:, 0:8], ctx.sb["bias_sb"][:, 0:8], scale, None,
            ALU.mult), reads=["bias_sb"], writes=["bias_sb"], tag="bscale")
    emit_mha(ctx, "xn", "attn", [(WQ, 0), (WK, 0), (WV, 0)], scale,
             biases=biases)
    if with_bias:
        bias = ("bias_sb", lambda m: ctx.sb["bias_sb"][:, 24 + m:24 + m + 1])
    else:
        bias = None
    emit_proj(ctx, "res", "attn", WO, 8, 8, wslot="wbuf",
              banks=("ps0", "ps1", "ps2", "ps3"), bias=bias, dst_dt_bf=False)
    emit_resid_out(ctx, XT, XO, ["res"])
    p.finalize()
    es.close()
    return nc, packs


def build_l3f():
    nc, p, es, ctx = new_ctx()
    XT = ctx.dram_t("XT", (1024, T), F32, kind="ExternalInput")
    XP = ctx.dram_t("XP", (1024, T), F32, kind="ExternalInput")
    WG = ctx.dram_t("WG", (2816, 1024), F32, kind="ExternalInput")
    WU = ctx.dram_t("WU", (2816, 1024), F32, kind="ExternalInput")
    WD = ctx.dram_t("WD", (1024, 2816), F32, kind="ExternalInput")
    WGQ = ctx.dram_t("WGQ", (1024, 2816), BF16)
    WUQ = ctx.dram_t("WUQ", (1024, 2816), BF16)
    WDQ = ctx.dram_t("WDQ", (2816, 1024), BF16)
    XOI = ctx.dram_t("XOI", (1024, T), F32)
    XO = ctx.dram_t("XO", (1024, T), F32, kind="ExternalOutput")
    ctx.sbuf("xn", (128, KD * T), BF16)
    ctx.sbuf("ln_xb", (128, KD * 512), F32)
    ctx.sbuf("xq", (128, KD * T), BF16)
    ctx.sbuf("l3w", (128, 1408), F32)
    ctx.sbuf("l3wq", (128, 2816), BF16)
    ctx.sbuf("l3st", (128, 2816), BF16)
    ctx.sbuf("l3xt", (128, 1024), BF16)
    ctx.sbuf("l3xq", (128, 1024), BF16)
    ctx.sbuf("l3hq", (128, 22 * 512), BF16)
    ctx.sbuf("l3hsc", (128, 4 * 512), F32)
    ctx.sbuf("l3hbuf", (128, 512), F32)
    ctx.sbuf("l3hql", (128, 512), BF16)
    ctx.sbuf("qa", (128, 1408), F32)
    ctx.sbuf("qs", (128, 1408), F32)
    ctx.sbuf("qf", (128, 6 * 1408), F16)
    ctx.sbuf("qbm", (128, 3 * 88), F32)
    ctx.sbuf("wbuf", (128, 2 * 4096), BF16)
    ctx.sbuf("wbufd", (128, 2 * 22 * 128), BF16)
    for i in range(6):
        ctx.psum(f"ps{i}")
    ctx.psum("psbf", (128, 1024), BF16)
    register_common_consts(ctx)
    register_quant_consts(ctx)
    packs = ctx.pack_consts()
    emit_ln_stream(ctx, XP, "xn", "ln_xb", sq_name="l3hbuf",
                   row_name="l3hsc")
    emit_resid_out(ctx, XT, XOI, [])

    def epi(dt, tc, pb, bank):
        p.op("scalar", lambda e, pb=pb: e.activation(
            ctx.sb["l3hbuf"][:, :], pb[:, :], AF.Identity),
            reads=[bank], writes=["l3hbuf"], tag="oev")
        p.dma("gpsimd", XOI[dt * 128:(dt + 1) * 128, tc * 512:(tc + 1) * 512],
              ctx.sb["l3hbuf"][:, :], accum_op=ALU.add,
              reads=["l3hbuf"], writes=["XOUT"], tag="acc")
    emit_l3ffn(ctx, "xn", WG, WU, WD, WGQ, WUQ, WDQ, epi)
    for k in range(KD):
        for tc2 in range(4):
            p.dma("sync", ctx.sb["l3hbuf"][:, 0:512],
                  XOI[k * 128:(k + 1) * 128, tc2 * 512:(tc2 + 1) * 512],
                  reads=["XOUT"], writes=["l3hbuf"], tag="cpl")
            p.dma("sync", XO[k * 128:(k + 1) * 128, tc2 * 512:(tc2 + 1) * 512],
                  ctx.sb["l3hbuf"][:, 0:512], reads=["l3hbuf"],
                  writes=["XOF"], tag="st", final=True)
    p.finalize()
    es.close()
    return nc, packs


def build_l0f():
    E, K, CAP = 8, 2, 4096
    Eloc = 1
    NAT = (T * K) // 128
    nc, p, es, ctx = new_ctx()
    XT = ctx.dram_t("XT", (1024, T), F32, kind="ExternalInput")
    XP = ctx.dram_t("XP", (1024, T), F32, kind="ExternalInput")
    RWD = ctx.dram_t("RWD", (1024, E), F32, kind="ExternalInput")
    WFC1R = ctx.dram_t("WFC1R", (1024, 4096), F32, kind="ExternalInput")
    WFC2R = ctx.dram_t("WFC2R", (4096, 1024), F32, kind="ExternalInput")
    B1 = ctx.dram_t("B1", (128, 32), F32, kind="ExternalInput")
    B2 = ctx.dram_t("B2", (128, 8), F32, kind="ExternalInput")
    RANKIN = ctx.dram_t("RANKIN", (128, 40), F32, kind="ExternalInput")
    XO = ctx.dram_t("XO", (1024, T), F32, kind="ExternalOutput")
    spec = dict(E=E, K=K, CAP=CAP, kind="l0",
                RW=RWD,
                CNTI=ctx.dram_t("CNTI", (1, 16), F32),
                CNTO=ctx.dram_t("CNTO", (8, 16), F32),
                MAPI=ctx.dram_t("MAPI", (E * CAP + 128, 1), F32),
                MAPD=ctx.dram_t("MAPD", (Eloc * CAP, 1), F32),
                BARI=ctx.dram_t("BARI", (1, 128), F32),
                BARO=ctx.dram_t("BARO", (1, 128), F32),
                XAGI=ctx.dram_t("XAGI", (T, 1024), BF16),
                XALL=ctx.dram_t("XALL", (8 * T, 1024), BF16),
                EOI=ctx.dram_t("EOI", (Eloc * CAP, 1024), BF16),
                EOALL=ctx.dram_t("EOALL", (8 * Eloc * CAP, 1024), BF16),
                COMB=ctx.dram_t("COMB", (T, 1024), BF16),
                WFC1=ctx.dram_t("WFC1", (1024, 4096), BF16),
                WFC2=ctx.dram_t("WFC2", (4096, 1024), BF16),
                EADR=None, XP=XP)
    ctx.sbuf("x", (128, KD * T), F32)
    ctx.sbuf("xn", (128, KD * T), BF16)
    ctx.sbuf("e_mr", (128, 2 * T + 2048), F32)
    ctx.sbuf("e_xrt", (128, 256), F32)
    ctx.sbuf("e_sc", (128, 16 * 8 + 128), F32)
    ctx.sbuf("e_lg", (128, 16 * 8), F32)
    ctx.sbuf("e_easg", (128, 16 * 2), F32)
    ctx.sbuf("e_wasg", (128, 16 * 2), F32)
    ctx.sbuf("e_rw", (128, 8 * 8), F32)
    ctx.sbuf("e_eq", (128, 16 * 8), F32)
    ctx.sbuf("e_eq2", (128, 16 * 8), F32)
    ctx.sbuf("e_bf", (128, 2 * 16 * 2), BF16)
    ctx.sbuf("e_easgA", (128, 2 * NAT), F32)
    ctx.sbuf("e_oh4", (128, 4 * 8), F32)
    ctx.sbuf("e_slots", (128, NAT), I32)
    ctx.sbuf("e_slotg", (128, NAT), I32)
    ctx.sbuf("e_wfin", (128, NAT), F32)
    ctx.sbuf("e_rows", (128, 16 * 8), F32)
    ctx.sbuf("e_pc", (128, 32), F32)
    ctx.sbuf("e_pp", (128, 4 * 8), F32)
    ctx.sbuf("e_ioffi", (128, 8), I32)
    ctx.sbuf("e_tokpat", (128, 12), F32)
    ctx.sbuf("e_cnts", (128, 16), F32)
    ctx.sbuf("e_rmask", (128, 1), F32)
    ctx.sbuf("e_xtk", (128, 1024), BF16)
    ctx.sbuf("e_mapf", (128, 128), F32)
    ctx.sbuf("e_mapc", (128, CAP // 128), I32)
    ctx.sbuf("e_dtok", (128, 1024), BF16)
    ctx.sbuf("e_dspf", (128, 8 * 512), BF16)
    ctx.sbuf("e_hsb", (128, 32 * 512), BF16)
    ctx.sbuf("e_eosb", (128, 8 * 512), BF16)
    ctx.sbuf("e_b1", (128, 32), F32)
    ctx.sbuf("e_b2", (128, 8), F32)
    ctx.sbuf("e_combT", (128, T), BF16)
    ctx.sbuf("wbuf", (128, 4096), BF16)
    ctx.sbuf("outf", (128, 512), F32)
    for i in range(7):
        ctx.psum(f"ps{i}")
    ctx.psum("ps7", (128, 1024), BF16)
    register_common_consts(ctx, skip=("mask_sl_neg", "mask_triu_incl",
                                      "mask_tril_strict"))
    register_moe_consts(ctx)
    packs = ctx.pack_consts()
    emit_ln_stream(ctx, XP, "xn", "x", sq_name="outf",
                   row_name="e_mr", row_off=2 * T, stats_name="e_mr")
    for k in range(KD):
        p.dma("sync", ctx.sb["x"][:, k * T:(k + 1) * T],
              XT[k * 128:(k + 1) * 128, :], writes=["x"], tag="ldx")
    p.dma("sync", ctx.sb["e_b1"][:, :], B1[:, :], writes=["e_b1"], tag="ldb1")
    p.dma("sync", ctx.sb["e_b2"][:, :], B2[:, :], writes=["e_b2"], tag="ldb2")
    p.dma("sync", ctx.sb["e_rmask"][:, 0:1], RANKIN[:, 0:1],
          writes=["e_rmask"], allow_slow_non_contiguous=True)
    p.dma("sync", ctx.sb["e_tokpat"][:, 0:12], RANKIN[:, 2:14],
          writes=["e_tokpat"], allow_slow_non_contiguous=True)
    p.dma("gpsimd", spec["WFC1"][:, :], WFC1R[:, :], writes=["WFC1b"], tag="wc1")
    p.dma("gpsimd", spec["WFC2"][:, :], WFC2R[:, :], writes=["WFC2b"], tag="wc2")
    spec["ffn"] = make_moe_dispatch_ffn(ctx, spec)
    emit_moe(ctx, "xn", "x", ctx.sb["e_mr"][0:1, 0:T],
             ctx.sb["e_mr"][0:1, T:2 * T], spec)
    emit_moe_exec(ctx, "xn", "x", spec)
    for k in range(KD):
        for tc in range(4):
            p.dma("sync", XO[k * 128:(k + 1) * 128, tc * 512:(tc + 1) * 512],
                  ctx.sb["x"][:, k * T + tc * 512:k * T + (tc + 1) * 512],
                  reads=["x"], writes=["XOUT"], tag="st", final=True)
    p.finalize()
    es.close()
    return nc, packs


def build_l2f():
    E, K, CAP = 32, 8, 2048
    Eloc = 4
    NAT = (T * K) // 128
    NT = 16
    nc, p, es, ctx = new_ctx()
    XT = ctx.dram_t("XT", (1024, T), F32, kind="ExternalInput")
    XP = ctx.dram_t("XP", (1024, T), F32, kind="ExternalInput")
    RWD = ctx.dram_t("RWD", (1024, E), F32, kind="ExternalInput")
    BIASR = ctx.dram_t("BIASR", (1, E), F32, kind="ExternalInput")
    WGUR = ctx.dram_t("WGUR", (1024, Eloc * 1024), F32, kind="ExternalInput")
    WDNR = ctx.dram_t("WDNR", (512, Eloc * 1024), F32, kind="ExternalInput")
    SGD = ctx.dram_t("SGD", (1024, 2048), F32, kind="ExternalInput")
    SUD = ctx.dram_t("SUD", (1024, 2048), F32, kind="ExternalInput")
    SDTD = ctx.dram_t("SDTD", (2048, 1024), F32, kind="ExternalInput")
    RANKIN = ctx.dram_t("RANKIN", (128, 40), F32, kind="ExternalInput")
    XO = ctx.dram_t("XO", (1024, T), F32, kind="ExternalOutput")
    spec = dict(E=E, K=K, CAP=CAP, kind="l2",
                RW=RWD, BIASR=BIASR,
                CNTI=ctx.dram_t("CNTI", (1, E), F32),
                CNTO=ctx.dram_t("CNTO", (8, E), F32),
                MAPI=ctx.dram_t("MAPI", (E * CAP + 128, 1), F32),
                MAPD=ctx.dram_t("MAPD", (Eloc * CAP, 1), F32),
                BARI=ctx.dram_t("BARI", (1, 128), F32),
                BARO=ctx.dram_t("BARO", (1, 128), F32),
                EADR=ctx.dram_t("EADR", (2 * NT, 128, K), BF16),
                XAGI=ctx.dram_t("XAGI", (T, 1024), BF16),
                XALL=ctx.dram_t("XALL", (8 * T, 1024), BF16),
                EOI=ctx.dram_t("EOI", (Eloc * CAP, 1024), BF16),
                EOALL=ctx.dram_t("EOALL", (8 * Eloc * CAP, 1024), BF16),
                COMB=ctx.dram_t("COMB", (T, 1024), BF16),
                WGU=ctx.dram_t("WGU", (1024, Eloc * 1024), BF16),
                WDN=ctx.dram_t("WDN", (512, Eloc * 1024), BF16), XP=XP)
    SGB = ctx.dram_t("SGB", (1024, 2048), BF16)
    SUB = ctx.dram_t("SUB", (1024, 2048), BF16)
    SDB = ctx.dram_t("SDB", (2048, 1024), BF16)
    ctx.sbuf("x", (128, KD * T), F32)
    ctx.sbuf("xn", (128, KD * T), BF16)
    ctx.sbuf("e_mr", (128, 2 * T), F32)
    ctx.sbuf("ln_sq", (128, 512), F32)
    ctx.sbuf("e_xrt", (128, 256), F32)
    ctx.sbuf("e_sc", (128, 16 * E + 128), F32)
    ctx.sbuf("e_lg", (128, NT * E), F32)
    ctx.sbuf("e_sel", (128, NT * E + E), F32)
    ctx.sbuf("e_easg", (128, NT * K), F32)
    ctx.sbuf("e_wasg", (128, NT * K), F32)
    ctx.sbuf("e_rw", (128, 8 * E), F32)
    ctx.sbuf("e_eq", (128, NT * E), F32)
    ctx.sbuf("e_eq2", (128, NT * E), F32)
    ctx.sbuf("e_bf", (128, 2 * NT * K), BF16)
    ctx.sbuf("e_easgA", (128, 2 * NAT), BF16)
    ctx.sbuf("e_oh4", (128, 4 * E), F32)
    ctx.sbuf("e_slots", (128, NAT), I32)
    ctx.sbuf("e_slotg", (128, NAT), I32)
    ctx.sbuf("e_wfin", (128, NAT), F32)
    ctx.sbuf("e_rows", (128, 16 * E), F32)
    ctx.sbuf("e_pc", (128, 32), F32)
    ctx.sbuf("e_pp", (128, 4 * E), F32)
    ctx.sbuf("e_ioffi", (128, 8), I32)
    ctx.sbuf("e_tokpat", (128, 12), F32)
    ctx.sbuf("e_cnts", (128, E), F32)
    ctx.sbuf("e_rmask", (128, 1), F32)
    ctx.sbuf("e_xtk", (128, 1024), BF16)
    ctx.sbuf("e_mapf", (128, 128), F32)
    ctx.sbuf("e_mapc", (128, CAP // 128), I32)
    ctx.sbuf("e_dtok", (128, 1024), BF16)
    ctx.sbuf("e_dspf", (128, 8 * 512), BF16)
    ctx.sbuf("e_hsb", (128, 16 * 512), BF16)
    ctx.sbuf("e_eosb", (128, 8 * 512), BF16)
    ctx.sbuf("e_gbuf", (128, 4 * 512), F32)
    ctx.sbuf("e_gw", (128, 16), BF16)
    ctx.sbuf("e_stg", (128, 1024), BF16)
    ctx.sbuf("e_gath2", (128, 1024), BF16)
    ctx.sbuf("e_combT", (128, T), BF16)
    ctx.sbuf("wbuf", (128, 8192), BF16)
    for i in range(7):
        ctx.psum(f"ps{i}")
    ctx.psum("ps7", (128, 1024), BF16)
    register_common_consts(ctx, skip=("mask_sl_neg", "mask_triu_incl",
                                      "mask_tril_strict"))
    register_moe_consts(ctx)
    packs = ctx.pack_consts()
    emit_ln_stream(ctx, XP, "xn", "x", sq_name="ln_sq",
                   row_name="e_gbuf", stats_name="e_mr")
    for k in range(KD):
        p.dma("sync", ctx.sb["x"][:, k * T:(k + 1) * T],
              XT[k * 128:(k + 1) * 128, :], writes=["x"], tag="ldx")
    p.dma("sync", ctx.sb["e_rmask"][:, 0:1], RANKIN[:, 0:1],
          writes=["e_rmask"], allow_slow_non_contiguous=True)
    p.dma("sync", ctx.sb["e_tokpat"][:, 0:12], RANKIN[:, 2:14],
          writes=["e_tokpat"], allow_slow_non_contiguous=True)
    p.dma("gpsimd", spec["WGU"][:, :], WGUR[:, :], writes=["WGUb"], tag="wcg")
    p.dma("gpsimd", spec["WDN"][:, :], WDNR[:, :], writes=["WDNb"], tag="wcd")
    spec["ffn"] = make_moe_dispatch_ffn(ctx, spec)
    emit_moe(ctx, "xn", "x", ctx.sb["e_mr"][0:1, 0:T],
             ctx.sb["e_mr"][0:1, T:2 * T], spec)
    emit_moe_exec(ctx, "xn", "x", spec)
    emit_l2shared(ctx, "xn", "x", SGD, SUD, SDTD, SGB=SGB, SUB=SUB, SDB=SDB)
    for k in range(KD):
        for tc in range(4):
            p.dma("sync", XO[k * 128:(k + 1) * 128, tc * 512:(tc + 1) * 512],
                  ctx.sb["x"][:, k * T + tc * 512:k * T + (tc + 1) * 512],
                  reads=["x"], writes=["XOUT"], tag="st", final=True)
    p.finalize()
    es.close()
    return nc, packs


_BUILDS = {}


def _get_build(name):
    if name not in _BUILDS:
        print("KBUILD", name, flush=True)
        _BUILDS[name] = {
            "l0a": build_l0a, "l0f": build_l0f, "l1a": build_l1a,
            "l1f": build_l1f, "l2a": lambda: build_attn(False),
            "l3a": lambda: build_attn(True), "l2f": build_l2f,
            "l3f": build_l3f}[name]()
    return _BUILDS[name]


def _prep_mamba_cols(ins):
    C = np.zeros((128, 163), np.float32)
    cwr = np.asarray(ins["L0_attn_conv_w"], np.float32)
    for ct in range(24):
        for kk in range(4):
            C[:, ct * 4 + kk] = cwr[ct * 128:(ct + 1) * 128, kk]
    cb = ins["L0_attn_conv_b"]
    for ct in range(24):
        C[:, 96 + ct] = cb[ct * 128:(ct + 1) * 128]
    C[0:32, 120] = ins["L0_attn_dt_bias"]
    C[0:32, 121] = ins["L0_attn_A_log"]
    D = ins["L0_attn_D"]
    for mt in range(16):
        C[:, 122 + mt] = D[(mt * 128 + np.arange(128)) // 64]
    gnw = ins["L0_attn_gate_norm_w"]
    for mt in range(16):
        C[:, 138 + mt] = gnw[mt * 128:(mt + 1) * 128]
    nw = ins["L0_attn_norm_w"]
    for k in range(8):
        C[:, 154 + k] = nw[k * 128:(k + 1) * 128]
    return C


def _rankin(r):
    p_arr = np.arange(128)
    R = np.zeros((128, 40), np.float32)
    R[0:8, 0] = (np.arange(8) < r).astype(np.float32)
    for b in range(4):
        R[:, 2 + b] = r * 2048 + b * 16 + p_arr // 8
        R[:, 6 + b] = r * 2048 + p_arr + b * 128
    return R


def kernel(**ins):
    x = np.asarray(ins["L0_attn_hidden_states"], np.float32)
    xT = [np.ascontiguousarray(x[c].T) for c in range(8)]

    def run(name, extra_fn):
        nonlocal xT
        nc, packs = _get_build(name)
        in_maps = []
        for c in range(8):
            d = {"XT": xT[c]}
            d.update(extra_fn(c))
            d.update(packs)
            in_maps.append(d)
        res = run_bass_kernel_spmd(nc, in_maps, core_ids=list(range(8)))
        xT = [np.ascontiguousarray(res.results[c]["XO"]) for c in range(8)]

    mcols = _prep_mamba_cols(ins)
    xprev = xT
    run("l0a", lambda c: {"INPJ": ins["L0_attn_in_proj_w"],
                          "OUTPJ": ins["L0_attn_out_proj_w"],
                          "COLS": mcols})
    run("l0f", lambda c: {"XP": xprev[c],
        "RWD": np.ascontiguousarray(ins["L0_ffn_gate_w"].T.astype(np.float32)),
        "WFC1R": np.ascontiguousarray(ins["L0_ffn_fc1_w"][c].T.astype(np.float32)),
        "WFC2R": np.ascontiguousarray(ins["L0_ffn_fc2_w"][c].astype(np.float32)),
        "B1": np.ascontiguousarray(
            ins["L0_ffn_fc1_b"][c, 0].astype(np.float32).reshape(32, 128).T),
        "B2": np.ascontiguousarray(
            ins["L0_ffn_fc2_b"][c, 0].astype(np.float32).reshape(8, 128).T),
        "RANKIN": _rankin(c)})
    xprev = xT
    lng = np.concatenate([np.asarray(ins["L1_attn_ln_g"], np.float32),
                          np.asarray(ins["L1_attn_ln_b"], np.float32)])[None, :]
    run("l1a", lambda c: {
        "WQ": np.ascontiguousarray(ins["L1_attn_Wq"].astype(np.float32)),
        "WK": np.ascontiguousarray(ins["L1_attn_Wk"].astype(np.float32)),
        "WV": np.ascontiguousarray(ins["L1_attn_Wv"].astype(np.float32)),
        "WB": np.ascontiguousarray(
            ins["L1_attn_Wb"].astype(np.float32).reshape(8, 128).T),
        "LNG": np.ascontiguousarray(lng)})
    MU = np.zeros((128, 16), np.float32)
    MU[:, 0:8] = np.asarray(ins["L1_ffn_mu_r"], np.float32).reshape(8, 128).T
    MU[:, 8:16] = np.asarray(ins["L1_ffn_mu_k"], np.float32).reshape(8, 128).T
    run("l1f", lambda c: {"XP": xprev[c],
        "WR": np.ascontiguousarray(ins["L1_ffn_W_r"].astype(np.float32).T),
        "WK": np.ascontiguousarray(ins["L1_ffn_W_k"].astype(np.float32).T),
        "WV": np.ascontiguousarray(ins["L1_ffn_W_v"].astype(np.float32).T),
        "MU": MU})
    xprev = xT
    wqkv = np.asarray(ins["L2_attn_w_qkv"], np.float32)
    idx_q = np.concatenate([np.arange(h * 192, h * 192 + 64) for h in range(16)])
    run("l2a", lambda c: {
        "WQ": np.ascontiguousarray(wqkv[:, idx_q]),
        "WK": np.ascontiguousarray(wqkv[:, idx_q + 64]),
        "WV": np.ascontiguousarray(wqkv[:, idx_q + 128]),
        "WO": np.ascontiguousarray(np.asarray(ins["L2_attn_w_o"], np.float32)),
        "BIAS": np.zeros((128, 32), np.float32)})
    gup = np.asarray(ins["L2_ffn_gate_up_proj"], np.float32)
    dnp = np.asarray(ins["L2_ffn_down_proj"], np.float32)
    run("l2f", lambda c: {"XP": xprev[c],
        "RWD": np.ascontiguousarray(
            np.asarray(ins["L2_ffn_router_weight"], np.float32).T),
        "BIASR": np.asarray(
            ins["L2_ffn_e_score_correction_bias"], np.float32)[None, :],
        "WGUR": np.ascontiguousarray(np.concatenate(
            [gup[c * 4 + le].T for le in range(4)], axis=1)),
        "WDNR": np.ascontiguousarray(np.concatenate(
            [dnp[c * 4 + le].T for le in range(4)], axis=1)),
        "SGD": np.ascontiguousarray(
            np.asarray(ins["L2_ffn_shared_gate"], np.float32).T),
        "SUD": np.ascontiguousarray(
            np.asarray(ins["L2_ffn_shared_up"], np.float32).T),
        "SDTD": np.ascontiguousarray(
            np.asarray(ins["L2_ffn_shared_down"], np.float32).T),
        "RANKIN": _rankin(c)})
    xprev = xT
    BIAS = np.zeros((128, 32), np.float32)
    for i, nm in enumerate(["bq", "bk", "bv", "bp"]):
        BIAS[:, i * 8:(i + 1) * 8] = np.asarray(
            ins[f"L3_attn_{nm}"], np.float32).reshape(8, 128).T
    run("l3a", lambda c: {
        "WQ": np.ascontiguousarray(np.asarray(ins["L3_attn_Wq"], np.float32).T),
        "WK": np.ascontiguousarray(np.asarray(ins["L3_attn_Wk"], np.float32).T),
        "WV": np.ascontiguousarray(np.asarray(ins["L3_attn_Wv"], np.float32).T),
        "WO": np.ascontiguousarray(np.asarray(ins["L3_attn_Wp"], np.float32).T),
        "BIAS": BIAS})
    run("l3f", lambda c: {"XP": xprev[c],
        "WG": np.ascontiguousarray(np.asarray(ins["L3_ffn_w_gate"], np.float32)),
        "WU": np.ascontiguousarray(np.asarray(ins["L3_ffn_w_up"], np.float32)),
        "WD": np.ascontiguousarray(np.asarray(ins["L3_ffn_w_down"], np.float32))})
    out = np.stack([xT[c].T for c in range(8)]).astype(np.float32)
    return out
